# version 22
# baseline (speedup 1.0000x reference)
import numpy as np
import ml_dtypes
from contextlib import ExitStack

import concourse.bass as bass
import concourse.mybir as mybir
from concourse.bass_utils import run_bass_kernel_spmd

F32 = mybir.dt.float32
BF16 = mybir.dt.bfloat16
I8 = mybir.dt.int8
AF = mybir.ActivationFunctionType
ALU = mybir.AluOpType
AX = mybir.AxisListType

NCORES = 8
S_TILES = 16
E_BLKS = 16
O_TILES = 64
CHUNK = 512
N_CHUNKS = 4

RMS_EPS_L0 = 1e-6
RMS_EPS_L1 = float(np.finfo(np.float32).eps)
LN_EPS = 1e-5
FWHT_SCALE = float(1.0 / np.sqrt(8192.0))
SM_SCALE_L1 = float(128.0 ** -0.5)

ENGINES = ("sync", "gpsimd", "scalar", "vector", "tensor")


class Sem:
    def __init__(self, nc, ctx, name):
        self.h = ctx.enter_context(nc.semaphore(name))
        self.n = 0


class Prog:

    def __init__(self, nc, ctx):
        self.nc = nc
        self.ctx = ctx
        self.alloc_ctx = ctx
        self.q = {e: [] for e in ENGINES}
        self._sems = {}

    def barrier(self):
        waits = [(s, None) for s in self._sems.values() if s.n > 0]
        for eng in ENGINES:
            self.emit(eng, lambda e: None, waits=waits)

    def sem(self, name):
        if name not in self._sems:
            self._sems[name] = Sem(self.nc, self.ctx, name)
        return self._sems[name]

    def emit(self, eng, fn, waits=(), incs=()):
        wlist = [(s.h, (s.n if v is None else v)) for (s, v) in waits]
        ilist = []
        for s, k in incs:
            s.n += k
            ilist.append((s.h, k))

        def run(e):
            for h, v in wlist:
                if v > 0:
                    e.wait_ge(h, v)
            inst = fn(e)
            for h, k in ilist:
                assert inst is not None
                inst = inst.then_inc(h, k)
        self.q[eng].append(run)

    def dma(self, eng, out_ap, in_ap, waits=(), incs=()):
        self.emit(eng, lambda e: e.dma_start(out_ap, in_ap), waits, incs)

    def finalize(self):
        with self.nc.Block() as b:
            for name in ENGINES:
                fns = self.q[name]
                if not fns:
                    continue
                getattr(b, name)(lambda e, fns=fns: [f(e) for f in fns])


def _c(a):
    return np.ascontiguousarray(a)


def prep_weights(ins):
    w = {}
    for nm in ("wq", "wk", "wv"):
        a = ins[f"L0_attn_{nm}"]
        w[f"l0_{nm}"] = _c(a.reshape(16, 128, 16, 128).transpose(2, 1, 0, 3))
    w["l0_wo"] = _c(ins["L0_attn_wo"].reshape(16, 128, 2048))
    w["l0_qscale"] = _c(ins["L0_attn_q_scale"].reshape(128, 1))
    w["l0_kscale"] = _c(ins["L0_attn_k_scale"].reshape(128, 1))
    for nm, key in (("wg", "L0_ffn_w_gate"), ("wu", "L0_ffn_w_up")):
        a = ins[key]
        w[f"l0_{nm}"] = _c(a.reshape(16, 128, 64, 128).transpose(2, 1, 0, 3))
    w["l0_wd"] = _c(ins["L0_ffn_w_down"].reshape(64, 128, 2048))
    w["l0_sg"] = _c(ins["L0_ffn_s_gate"].reshape(64, 128).T)
    w["l0_su"] = _c(ins["L0_ffn_s_up"].reshape(64, 128).T)
    w["l0_sdown"] = _c(ins["L0_ffn_s_down"].reshape(1, 2048))
    a = ins["L1_attn_Wq"]
    w["l1_wq"] = _c(a.reshape(16, 128, 16, 128).transpose(0, 3, 2, 1))
    for nm in ("Wk", "Wv"):
        a = ins[f"L1_attn_{nm}"]
        w[f"l1_{nm.lower()}"] = _c(a.reshape(4, 128, 16, 128).transpose(0, 3, 2, 1))
    w["l1_wv_n"] = _c(ins["L1_attn_Wv"].T.reshape(16, 128, 512))
    w["l1_wo"] = _c(ins["L1_attn_Wo"].T.reshape(16, 128, 2048))
    for nm, key in (("w1", "L1_ffn_W1"), ("w3", "L1_ffn_W3")):
        a = ins[key]
        w[f"l1_{nm}"] = _c(a.reshape(64, 128, 16, 128).transpose(0, 3, 2, 1))
    w["l1_w2t"] = _c(ins["L1_ffn_W2"].T.reshape(64, 128, 2048))
    return w


def host_consts():
    c = {}
    c["idn_bf"] = np.eye(128, dtype=ml_dtypes.bfloat16)
    c["idn_f32"] = np.eye(128, dtype=np.float32)

    def had(n):
        h = np.array([[1.0]])
        while h.shape[0] < n:
            h = np.block([[h, h], [h, -h]])
        return h
    c["h128"] = had(128).astype(ml_dtypes.bfloat16)
    h64 = had(64)
    z = np.zeros((64, 64))
    c["h2x64"] = np.block([[h64, z], [z, h64]]).astype(ml_dtypes.bfloat16)
    return c


class Consts:
    def __init__(self, nc, ctx, p, T):
        sb = lambda name, shape, dt: ctx.enter_context(nc.sbuf_tensor(name, shape, dt))
        self.idn_bf = sb("c_idn_bf", [128, 128], BF16)
        self.idn_f32 = sb("c_idn_f32", [128, 128], F32)
        self.bcast1_f32 = sb("c_bcast1", [1, 128], F32)
        self.ones_row_bf = sb("c_ones_row_bf", [1, 128], BF16)
        self.red1_f32 = sb("c_red1_f", [128, 1], F32)
        self.red1_bf = sb("c_red1_b", [128, 1], BF16)
        self.ones128_f32 = sb("c_ones128f", [128, 128], F32)
        self.ones128_bf = sb("c_ones128b", [128, 128], BF16)
        self.vals = sb("c_vals", [128, 12], F32)
        ld = p.sem("c_ld")
        p.dma("sync", self.idn_bf[:, :], T["idn_bf"].ap()[:, :], incs=[(ld, 16)])
        p.dma("sync", self.idn_f32[:, :], T["idn_f32"].ap()[:, :], incs=[(ld, 16)])
        g = p.sem("c_set")
        for col, v in ((0, -1.0 / 2048.0), (1, 1.0 / 2048.0), (2, LN_EPS),
                       (3, RMS_EPS_L0), (4, RMS_EPS_L1), (5, FWHT_SCALE),
                       (6, 1.0 / 128.0), (7, 1e-24), (8, SM_SCALE_L1)):
            p.emit("vector", lambda e, col=col, v=v: e.memset(self.vals[:, col:col + 1], v),
                   incs=[(g, 1)])
        for t in (self.bcast1_f32, self.ones_row_bf, self.red1_f32, self.red1_bf,
                  self.ones128_f32, self.ones128_bf):
            p.emit("vector", lambda e, t=t: e.memset(t[:, :], 1.0), incs=[(g, 1)])
        self.ld = ld
        self.ready = g

    def v(self, col):
        return self.vals[:, col:col + 1]

    def v1(self, col):
        return self.vals[0:1, col:col + 1]



def emit_ffn(p, C, layer, T, tag, xg=None, aux=None, aux_done=None):
    nc = p.nc
    ctx = p.alloc_ctx
    sb = lambda name, shape, dt: ctx.enter_context(nc.sbuf_tensor(tag + name, shape, dt))

    x_in = T["x_in"].ap()
    xnT_d = T["xnT"].ap()
    xacc_d = T["x_acc"].ap()
    if layer == 0:
        wg_d, wu_d, wd_d = T["l0_wg"].ap(), T["l0_wu"].ap(), T["l0_wd"].ap()
    else:
        wg_d, wu_d, wd_d = T["l1_w1"].ap(), T["l1_w3"].ap(), T["l1_w2pp"].ap()

    xbuf = sb("xbuf", [128, 2048], F32)
    xsq = sb("xsq", [128, 1024], F32)
    st = sb("stats", [128, 12], F32)
    xn_bf = sb("xn_bf", [128, 2048], BF16)
    xnT_sb = xg if xg is not None else sb("xnT_sb", [128, E_BLKS * CHUNK], BF16)
    XS = 2048 if xg is not None else CHUNK
    hidT = sb("hidT", [128, O_TILES * CHUNK], BF16)
    wgl = [sb(f"wgl{b}", [128, 2048], BF16) for b in range(2)]
    wul = [sb(f"wul{b}", [128, 2048], BF16) for b in range(2)]
    wdp = [sb(f"wdp{b}", [128, 2048], BF16) for b in range(2)]
    silu_s = [sb(f"silu{b}", [128, CHUNK], F32) for b in range(2)]
    up_s = [sb(f"up{b}", [128, CHUNK], F32) for b in range(2)]
    dsc = sb("dsc", [128, CHUNK], F32)
    xacc = [sb(f"xacc{i}", [128, 1024], F32) for i in range(4)]
    if layer == 0:
        sg_sb = sb("sg", [128, 64], F32)
        su_sb = sb("su", [128, 64], F32)
        sdb = sb("sdb", [128, 2048], F32)

    xld = p.sem(tag + "xld")
    ald = p.sem(tag + "ald")
    wld = p.sem(tag + "wld")
    ggu = [p.sem(tag + f"ggu{b}") for b in range(2)]
    gwd = [p.sem(tag + f"gwd{b}") for b in range(2)]
    lnv = p.sem(tag + "lnv")
    trp = p.sem(tag + "trp")
    xnv = p.sem(tag + "xnv")
    mmg = p.sem(tag + "mmg")
    ev = p.sem(tag + "ev")
    mmd = p.sem(tag + "mmd")
    dev = p.sem(tag + "dev")
    sto = p.sem(tag + "sto")

    with nc.psum_tensor(tag + "ptr", [128, 512], BF16) as ptr, \
         nc.psum_tensor(tag + "pgA", [128, 512], F32) as pg, \
         nc.psum_tensor(tag + "puA", [128, 512], F32) as pu, \
         ExitStack() as ps:
        pd = [ps.enter_context(nc.psum_tensor(f"{tag}pd{i}", [128, 512], F32))
              for i in range(4)]

        if layer == 0:
            p.dma("sync", sg_sb[:, :], T["l0_sg"].ap()[:, :], incs=[(wld, 16)])
            p.dma("sync", su_sb[:, :], T["l0_su"].ap()[:, :], incs=[(wld, 16)])
            p.dma("sync", sdb[0:1, :], T["l0_sdown"].ap()[:, :], incs=[(wld, 16)])
            for i in range(4):
                p.emit("tensor",
                       lambda e, i=i: e.matmul(pd[0][:, :], C.bcast1_f32[:, :],
                                               sdb[0:1, i * 512:(i + 1) * 512],
                                               start=True, stop=True),
                       waits=[(wld, None), (C.ready, None), (dev, None)],
                       incs=[(mmd, 1)])
                mmd_ = mmd.n
                p.emit("vector",
                       lambda e, i=i: e.tensor_copy(sdb[:, i * 512:(i + 1) * 512],
                                                    pd[0][:, :]),
                       waits=[(mmd, mmd_)], incs=[(dev, 1)])
            sdb_dev = dev.n
        else:
            sdb_dev = 0

        gu_mm_done = {}
        silu_mark = {}
        upev_mark = {}
        fuse_mark = {}
        d_last = {}
        dq_mm = {}
        pass_dev = {"last": sdb_dev, "gqt": 0}
        aux_i = 0
        aux = aux or []

        for q in range(N_CHUNKS):
            tiles = [q * 4 + i for i in range(4)]
            for i, t in enumerate(tiles):
                p.dma("sync", xbuf[:, :], x_in[t], waits=[(lnv, None)], incs=[(xld, 16)])
                p.emit("vector", lambda e: e.reduce_sum(st[:, 0:1], xbuf[:, :], axis=AX.X),
                       waits=[(xld, None)], incs=[(lnv, 1)])
                p.emit("scalar", lambda e: e.activation(xsq[:, :], xbuf[:, 0:1024],
                                                        AF.Square,
                                                        accum_out=st[:, 8:9]),
                       waits=[(xld, None), (lnv, None)], incs=[(lnv, 1)])
                p.emit("scalar", lambda e: e.activation(xsq[:, :], xbuf[:, 1024:2048],
                                                        AF.Square,
                                                        accum_out=st[:, 9:10]),
                       waits=[(lnv, None)], incs=[(lnv, 1)])
                p.emit("vector", lambda e: e.tensor_tensor(st[:, 1:2], st[:, 8:9],
                                                           st[:, 9:10], ALU.add),
                       waits=[(lnv, None)], incs=[(lnv, 1)])
                p.emit("vector", lambda e: e.tensor_scalar(st[:, 2:3], st[:, 0:1],
                                                           C.v(0), None, ALU.mult),
                       waits=[(lnv, None)], incs=[(lnv, 1)])
                p.emit("vector", lambda e: e.tensor_tensor(st[:, 3:4], st[:, 2:3], st[:, 2:3],
                                                           ALU.mult),
                       waits=[(lnv, None)], incs=[(lnv, 1)])
                p.emit("vector", lambda e: e.scalar_tensor_tensor(st[:, 4:5], st[:, 1:2],
                                                                  C.v(1), st[:, 3:4],
                                                                  ALU.mult, ALU.subtract),
                       waits=[(lnv, None)], incs=[(lnv, 1)])
                p.emit("scalar", lambda e: e.activation(st[:, 5:6], st[:, 4:5], AF.Ln,
                                                        bias=C.v(2)),
                       waits=[(lnv, None)], incs=[(lnv, 1)])
                p.emit("scalar", lambda e: e.activation(st[:, 6:7], st[:, 5:6], AF.Exp,
                                                        scale=-0.5),
                       waits=[(lnv, None)], incs=[(lnv, 1)])
                p.emit("vector", lambda e: e.tensor_scalar(xn_bf[:, :], xbuf[:, :],
                                                           st[:, 2:3], st[:, 6:7],
                                                           ALU.add, ALU.mult),
                       waits=[(lnv, None), (trp, None)], incs=[(lnv, 1)])
                for gblk in range(4):
                    for j in range(4):
                        eb = gblk * 4 + j
                        w = [(lnv, None), (C.ld, None), (xnv, None)] if j == 0 else []
                        p.emit("tensor",
                               lambda e, eb=eb, j=j: e.transpose(
                                   ptr[:, j * 128:(j + 1) * 128],
                                   xn_bf[:, eb * 128:(eb + 1) * 128],
                                   C.idn_bf[:, :]),
                               waits=w, incs=([(trp, 1)] if j == 3 else []))
                    dst = xnT_sb[:, :].rearrange("p (eb s) -> p eb s", eb=E_BLKS)
                    co = (q * CHUNK if xg is not None else 0) + i * 128
                    cw = [(trp, None)] if xg is not None else [(trp, None), (sto, None)]
                    p.emit("vector",
                           lambda e, gblk=gblk, co=co, dst=dst: e.tensor_copy(
                               dst[:, gblk * 4:gblk * 4 + 4, co:co + 128],
                               ptr[:, :].rearrange("p (j s) -> p j s", j=4)),
                           waits=cw, incs=[(xnv, 1)])
            if xg is None:
                for eb in range(E_BLKS):
                    p.dma("sync", xnT_d[eb, :, q * CHUNK:(q + 1) * CHUNK],
                          xnT_sb[:, eb * CHUNK:(eb + 1) * CHUNK],
                          waits=[(xnv, None)], incs=[(sto, 16)])

            for ot in range(O_TILES):
                b = ot & 1
                gq = q * O_TILES + ot
                wg_w = [(mmg, gu_mm_done.get(gq - 2, 0))]
                p.dma("gpsimd", wgl[b][:, :], wg_d[ot].rearrange("p c j -> p (c j)"),
                      waits=wg_w, incs=[(ggu[b], 16)])
                p.dma("gpsimd", wul[b][:, :], wu_d[ot].rearrange("p c j -> p (c j)"),
                      incs=[(ggu[b], 16)])

                def mm_gate(e, b=b, q=q):
                    qo = q * CHUNK if xg is not None else 0
                    for c in range(E_BLKS):
                        inst = e.matmul(pg[:, :],
                                        wgl[b][:, c * 128:(c + 1) * 128],
                                        xnT_sb[:, c * XS + qo: c * XS + qo + CHUNK],
                                        start=(c == 0), stop=(c == E_BLKS - 1))
                    return inst
                g_w = [(ggu[b], None), (xnv, None)]
                if gq:
                    g_w.append((ev, silu_mark[gq - 1]))
                p.emit("tensor", mm_gate, waits=g_w, incs=[(mmg, 1)])
                mg_ = mmg.n

                def mm_up(e, b=b, q=q):
                    qo = q * CHUNK if xg is not None else 0
                    for c in range(E_BLKS):
                        inst = e.matmul(pu[:, :], wul[b][:, c * 128:(c + 1) * 128],
                                        xnT_sb[:, c * XS + qo: c * XS + qo + CHUNK],
                                        start=(c == 0), stop=(c == E_BLKS - 1))
                    return inst
                u_w = []
                if gq:
                    u_w.append((ev, upev_mark[gq - 1]))
                p.emit("tensor", mm_up, waits=u_w, incs=[(mmg, 1)])
                mu_ = mmg.n
                gu_mm_done[gq] = mmg.n

                sb2 = gq & 1
                sw_ = [(mmg, mg_)]
                if gq >= 2:
                    sw_.append((ev, fuse_mark[gq - 2]))
                if layer == 0:
                    p.emit("scalar",
                           lambda e, ot=ot, sb2=sb2: e.activation(
                               silu_s[sb2][:, :], pg[:, :], AF.Silu,
                               scale=sg_sb[:, ot:ot + 1]),
                           waits=sw_ + [(wld, None)], incs=[(ev, 1)])
                else:
                    p.emit("scalar",
                           lambda e, sb2=sb2: e.activation(silu_s[sb2][:, :], pg[:, :],
                                                           AF.Silu),
                           waits=sw_, incs=[(ev, 1)])
                silu_mark[gq] = ev.n
                uw_ = [(mmg, mu_)]
                if gq >= 2:
                    uw_.append((ev, fuse_mark[gq - 2]))
                if layer == 0:
                    p.emit("vector",
                           lambda e, ot=ot, sb2=sb2: e.tensor_scalar(
                               up_s[sb2][:, :], pu[:, :],
                               su_sb[:, ot:ot + 1], None, ALU.mult),
                           waits=uw_ + [(wld, None)], incs=[(ev, 1)])
                else:
                    p.emit("vector",
                           lambda e, sb2=sb2: e.tensor_copy(up_s[sb2][:, :], pu[:, :]),
                           waits=uw_, incs=[(ev, 1)])
                upev_mark[gq] = ev.n
                fw_ = [(ev, upev_mark[gq]), (ev, silu_mark[gq])]
                if q:
                    fw_.append((mmd, d_last[(q - 1, ot)]))
                p.emit("vector",
                       lambda e, ot=ot, sb2=sb2: e.tensor_tensor(
                           hidT[:, ot * CHUNK:(ot + 1) * CHUNK],
                           silu_s[sb2][:, :], up_s[sb2][:, :], ALU.mult),
                       waits=fw_, incs=[(ev, 1)])
                fuse_mark[gq] = ev.n
                if aux_i < len(aux) and gq % 2 == 1:
                    aux[aux_i]()
                    aux_i += 1

            for dh in range(2):
                for tp in range(2):
                    for i2 in (2 * tp, 2 * tp + 1):
                        aw_ = [(dev, None)]
                        if q or dh:
                            aw_.append((sto, None))
                        p.dma("sync", xacc[i2][:, :],
                              x_in[tiles[i2]][:, dh * 1024:(dh + 1) * 1024],
                              waits=aw_, incs=[(ald, 16)])
                    for qt in range(O_TILES // 2):
                        b = qt & 1
                        gqt = pass_dev["gqt"]
                        d_w = [(mmd, dq_mm.get(gqt - 2, 0))] if dq_mm.get(gqt - 2, 0) else []
                        p.dma("gpsimd", wdp[b][:, :].rearrange("p (t d) -> p t d", t=2),
                              wd_d[2 * qt:2 * qt + 2, :, dh * 1024:(dh + 1) * 1024]
                              .rearrange("t p d -> p t d"),
                              waits=d_w, incs=[(gwd[b], 16)])
                        for j in range(2):
                            tt = 2 * qt + j

                            def mm_down(e, tt=tt, j=j, b=b, tp=tp):
                                for k in range(2):
                                    i2 = 2 * tp + k
                                    for dc in range(2):
                                        inst = e.matmul(
                                            pd[k * 2 + dc][:, :],
                                            hidT[:, tt * CHUNK + i2 * 128: tt * CHUNK + (i2 + 1) * 128],
                                            wdp[b][:, j * 1024 + dc * 512: j * 1024 + (dc + 1) * 512],
                                            start=(tt == 0), stop=(tt == O_TILES - 1))
                                return inst
                            mw = [(gwd[b], None)]
                            if tt == 0:
                                mw.append((ev, fuse_mark[q * O_TILES + O_TILES - 1]))
                                mw.append((dev, pass_dev["last"]))
                                if q == 0 and dh == 0 and tp == 0 and aux_done is not None:
                                    mw.append(aux_done)
                            p.emit("tensor", mm_down, waits=mw, incs=[(mmd, 1)])
                            if dh == 1 and tp == 1:
                                d_last[(q, tt)] = mmd.n
                        dq_mm[gqt] = mmd.n
                        pass_dev["gqt"] = gqt + 1
                    mmd_p = mmd.n
                    for k in range(2):
                        i2 = 2 * tp + k
                        for dc in range(2):
                            col = dh * 1024 + dc * 512
                            if layer == 0:
                                p.emit("vector",
                                       lambda e, k=k, dc=dc, col=col: e.tensor_tensor(
                                           dsc[:, :], pd[k * 2 + dc][:, :],
                                           sdb[:, col:col + 512], ALU.mult),
                                       waits=[(mmd, mmd_p), (ald, None), (dev, None)],
                                       incs=[(dev, 1)])
                                p.emit("vector",
                                       lambda e, i2=i2, dc=dc: e.tensor_tensor(
                                           xacc[i2][:, dc * 512:dc * 512 + 512],
                                           xacc[i2][:, dc * 512:dc * 512 + 512],
                                           dsc[:, :], ALU.add),
                                       waits=[(dev, None)], incs=[(dev, 1)])
                            else:
                                p.emit("vector",
                                       lambda e, i2=i2, k=k, dc=dc: e.tensor_tensor(
                                           xacc[i2][:, dc * 512:dc * 512 + 512],
                                           xacc[i2][:, dc * 512:dc * 512 + 512],
                                           pd[k * 2 + dc][:, :], ALU.add),
                                       waits=[(mmd, mmd_p), (ald, None)], incs=[(dev, 1)])
                    pass_dev["last"] = dev.n
                    for k in range(2):
                        i2 = 2 * tp + k
                        p.dma("sync", xacc_d[tiles[i2]][:, dh * 1024:(dh + 1) * 1024],
                              xacc[i2][:, :],
                              waits=[(dev, None)], incs=[(sto, 16)])
        while aux_i < len(aux):
            aux[aux_i]()
            aux_i += 1

    return sto





def emit_w2pp(p, C, T, tag="w2_"):
    nc = p.nc
    ctx = p.alloc_ctx
    sb = lambda name, shape, dt: ctx.enter_context(nc.sbuf_tensor(tag + name, shape, dt))
    w2t = T["l1_w2t"].ap()
    w2a = T["l1_w2a"].ap()
    w2pp_flat = T["l1_w2pp"].ap().rearrange("t p d -> (t p) d")

    NB = 4
    h128 = sb("h128", [128, 128], BF16)
    h2x64 = sb("h2x64", [128, 128], BF16)
    win = [sb(f"win{b}", [128, 2048], BF16) for b in range(NB)]
    wout = [sb(f"wout{b}", [128, 2048], BF16) for b in range(NB)]
    ld = p.sem(tag + "ld")
    gld = [p.sem(tag + f"gld{b}") for b in range(NB)]
    mm = p.sem(tag + "mm")
    evs = p.sem(tag + "ev")
    sto = p.sem(tag + "sto")
    p.dma("sync", h128[:, :], T["h128"].ap()[:, :], incs=[(ld, 16)])
    p.dma("sync", h2x64[:, :], T["h2x64"].ap()[:, :], incs=[(ld, 16)])

    w2a_v = w2a.rearrange("(a b tt) d -> a b tt d", a=64, b=2)
    w2a_g = w2a.rearrange("(g q) d -> g q d", g=64)
    w2pp_v = w2pp_flat.rearrange("(tau gb) d -> tau gb d", gb=128)

    mm_done = {}
    sto_done = {}
    with nc.psum_tensor(tag + "psA", [128, 512], F32) as ps0, \
         nc.psum_tensor(tag + "psB", [128, 512], F32) as ps1:
        pss = [ps0, ps1]
        evm = {}
        gchunk = 0
        for t in range(128):
            b = t & (NB - 1)
            is2 = t >= 64
            src = w2a_g[t - 64] if is2 else w2t[t]
            dw = [(mm, mm_done.get(t - NB, 0))]
            if is2:
                dw.append((sto, sto_done[63]))
            p.dma("gpsimd", win[b][:, :], src, waits=dw, incs=[(gld[b], 16)])
            hmat = h2x64 if is2 else h128
            for dchunk in range(4):
                ps = pss[gchunk & 1]

                def mmw2(e, dchunk=dchunk, b=b, hmat=hmat, ps=ps):
                    return e.matmul(ps[:, :], hmat[:, :],
                                    win[b][:, dchunk * 512:(dchunk + 1) * 512],
                                    start=True, stop=True)
                mw = [(ld, None), (gld[b], None)]
                if gchunk >= 2:
                    mw.append((evs, evm[gchunk - 2]))
                p.emit("tensor", mmw2, waits=mw, incs=[(mm, 1)])
                mmv_ = mm.n
                if is2:
                    p.emit("vector",
                           lambda e, dchunk=dchunk, b=b, ps=ps: e.tensor_scalar(
                               wout[b][:, dchunk * 512:(dchunk + 1) * 512], ps[:, :],
                               C.v(5), None, ALU.mult),
                           waits=[(mm, mmv_), (C.ready, None), (sto, sto_done.get(t - NB, 0))],
                           incs=[(evs, 1)])
                else:
                    p.emit("vector",
                           lambda e, dchunk=dchunk, b=b, ps=ps: e.tensor_copy(
                               wout[b][:, dchunk * 512:(dchunk + 1) * 512], ps[:, :]),
                           waits=[(mm, mmv_), (sto, sto_done.get(t - NB, 0))],
                           incs=[(evs, 1)])
                evm[gchunk] = evs.n
                gchunk += 1
            mm_done[t] = mm.n
            if is2:
                g = t - 64
                for beta in range(2):
                    p.dma("sync", w2pp_v[:, g * 2 + beta, :],
                          wout[b][beta * 64:(beta + 1) * 64, :],
                          waits=[(evs, evm[gchunk - 1])], incs=[(sto, 16)])
            else:
                p.dma("sync", w2a_v[:, :, t, :].rearrange("a b d -> (a b) d"), wout[b][:, :],
                      waits=[(evs, evm[gchunk - 1])], incs=[(sto, 16)])
            sto_done[t] = sto.n
    return sto


def emit_attn(p, C, layer, T, tag, xg=None):
    nc = p.nc
    ctx = p.alloc_ctx
    sb = lambda name, shape, dt: ctx.enter_context(nc.sbuf_tensor(tag + name, shape, dt))

    xnT_d = T["xnT"].ap()
    xbase_d = T["x_base"].ap()
    xout_d = T["x_out"].ap()
    aoT_d = T["aoT"].ap()
    NH = 16

    xnT_sb = xg if xg is not None else sb("xnT", [128, E_BLKS * 2048], BF16)
    wql = [sb(f"wql{b}", [128, 2048], BF16) for b in range(2)]
    nkv = 2 if layer == 0 else 1
    wkl = [sb(f"wkl{b}", [128, 2048], BF16) for b in range(nkv)]
    wvl = [sb(f"wvl{b}", [128, 2048], BF16) for b in range(nkv)]
    fT = sb("fT", [128, 2048], F32)
    fsq = sb("fsq", [128, 2048], BF16)
    rstd2 = sb("rstd2", [128, 2048], F32)
    xbf = sb("xbf", [128, 2048], F32)
    lnb = sb("lnb", [128, 512], F32)
    eT = sb("eT", [128, 16 * 512], BF16)
    zb = sb("zb", [128, 512], F32)
    ao_p = sb("ao_p", [128, 512], BF16)
    aop_sb = [wql[b][:, 1024:1536] for b in range(2)]
    if layer == 0:
        qhatT = [sb(f"qhatT{b}", [128, 2048], BF16) for b in range(2)]
        khatT = [sb(f"khatT{b}", [128, 2048], BF16) for b in range(2)]
        v_nat = [sb(f"v{b}", [128, 2048], BF16) for b in range(2)]
        e_nat = sb("e_nat", [128, 2048], BF16)
        cols = sb("cols", [128, 34], F32)
        colsZ = sb("colsZ", [128, 32], F32)
        mrow = sb("mrow", [1, 2048], BF16)
        zrow = sb("zrow", [1, 2048], F32)
        biasb = sb("biasb", [128, 512], F32)
        smtmp = [sb(f"smtmp{b}", [128, 1024], F32) for b in range(2)]
        qs_c = sb("qs_c", [128, 1], F32)
        ks_c = sb("ks_c", [128, 1], F32)
        s2c = sb("s2c", [128, 1], F32)
    else:
        qhatT = [sb("qhatT0", [128, 2048], BF16), wkl[0]]
        khatT = [sb(f"khatT{g}", [128, 2048], BF16) for g in range(4)]
        vT_bf = [sb(f"vT{g}", [128, 2048], BF16) for g in range(4)]
        v_nat = sb("v", [128, 16 * 512], BF16)
        uv_bf = sb("uv_bf", [128, 512], BF16)
        uv_f = sb("uv_f", [128, 512], F32)
        df_f = sb("df_f", [128, 512], F32)
        zln = sb("zln", [128, 512], F32)

    ld = p.sem(tag + "ld")
    gld = [p.sem(tag + f"gld{b}") for b in range(2)]
    gvn = p.sem(tag + "gvn")
    gkv = p.sem(tag + "gkv")
    glw = p.sem(tag + "glw")
    aold2 = [p.sem(tag + f"aold{b}") for b in range(2)]
    xpl = p.sem(tag + "xpl")
    mm = p.sem(tag + "mm")
    se = p.sem(tag + "se")
    ve = p.sem(tag + "ve")
    sto = p.sem(tag + "sto")

    big = [ctx.enter_context(nc.psum_tensor(tag + f"big{i}", [128, 2048], F32))
           for i in range(2)]

    if xg is None:
        for eb in range(E_BLKS):
            p.dma("sync", xnT_sb[:, eb * 2048:(eb + 1) * 2048], xnT_d[eb],
                  incs=[(ld, 16)])
    if layer == 0:
        p.dma("sync", qs_c[:, :], T["l0_qscale"].ap()[:, :], incs=[(ld, 16)])
        p.dma("sync", ks_c[:, :], T["l0_kscale"].ap()[:, :], incs=[(ld, 16)])
        p.emit("vector", lambda e: e.tensor_tensor(s2c[:, :], qs_c[:, :], ks_c[:, :],
                                                   ALU.mult),
               waits=[(ld, None)], incs=[(ve, 1)])

    misc = {"rms_mm": 0}

    def proj_chunk(wsb, pq, sc, wsem, mm_w, fT_free):
        def mmf(e, sc=sc, pq=pq, wsb=wsb):
            for c in range(E_BLKS):
                inst = e.matmul(pq[:, :], wsb[:, c * 128:(c + 1) * 128],
                                xnT_sb[:, c * 2048 + sc * 512: c * 2048 + (sc + 1) * 512],
                                start=(c == 0), stop=(c == E_BLKS - 1))
            return inst
        p.emit("tensor", mmf, waits=[(ld, None), (wsem, None)] + list(mm_w),
               incs=[(mm, 1)])
        mmv_ = mm.n
        fw = [(mm, mmv_)]
        if fT_free:
            fw.append((ve, fT_free))
        p.emit("vector",
               lambda e, sc=sc, pq=pq: e.tensor_copy(fT[:, sc * 512:(sc + 1) * 512],
                                                     pq[:, :]),
               waits=fw, incs=[(ve, 1)])
        return ve.n

    def rms_chunks(pb, eps_col, fT_ve, pb_free, rstd_free_ve, l2=False):
        p.emit("vector", lambda e: e.tensor_tensor(fsq[:, :], fT[:, :], fT[:, :],
                                                   ALU.mult),
               waits=[(ve, fT_ve), (mm, misc["rms_mm"])], incs=[(ve, 1)])
        fsqv = ve.n
        pbr = {}
        ex = {}
        for sc in range(4):
            w = [(ve, fsqv), (C.ready, None)]
            if sc == 0:
                if pb_free:
                    w.append(pb_free)
            else:
                w.append(pbr[sc - 1])
            p.emit("tensor",
                   lambda e, sc=sc, pb=pb: e.matmul(pb, C.ones128_bf[:, :],
                                                    fsq[:, sc * 512:(sc + 1) * 512],
                                                    start=True, stop=True),
                   waits=w, incs=[(mm, 1)])
            misc["rms_mm"] = mm.n
            mmv_ = mm.n
            if l2:
                mw = [(mm, mmv_)]
                if sc:
                    mw.append((se, ex[sc - 1]))
                p.emit("vector",
                       lambda e, pb=pb: e.tensor_scalar(lnb[:, :], pb, C.v(7), None,
                                                        ALU.max),
                       waits=mw, incs=[(ve, 1)])
                pbr[sc] = (ve, ve.n)
                p.emit("scalar", lambda e: e.activation(lnb[:, :], lnb[:, :], AF.Ln),
                       waits=[(ve, ve.n)] + ([(se, ex[sc - 1])] if sc else []),
                       incs=[(se, 1)])
            else:
                mw = [(mm, mmv_)]
                if sc:
                    mw.append((se, ex[sc - 1]))
                p.emit("scalar",
                       lambda e, pb=pb: e.activation(lnb[:, :], pb, AF.Ln,
                                                     bias=eps_col, scale=1.0 / 128.0),
                       waits=mw, incs=[(se, 1)])
                pbr[sc] = (se, se.n)
            lnm = se.n
            rw = [(se, lnm)]
            if sc == 0 and rstd_free_ve:
                rw.append((ve, rstd_free_ve))
            p.emit("scalar",
                   lambda e, sc=sc: e.activation(rstd2[:, sc * 512:(sc + 1) * 512],
                                                 lnb[:, :], AF.Exp, scale=-0.5),
                   waits=rw, incs=[(se, 1)])
            ex[sc] = se.n
        return ex[3]

    kv_marks = {}
    if layer == 1:
        wvn = eT
        for c in range(E_BLKS):
            p.dma("gpsimd", wvn[:, c * 512:(c + 1) * 512], T["l1_wv_n"].ap()[c],
                  incs=[(gvn, 16)])
        pvs = [big[1][:, 0:512], big[1][:, 512:1024]]
        vcp = {}
        for t in range(S_TILES):
            pv = pvs[t & 1]

            def mmv(e, t=t, pv=pv):
                for c in range(E_BLKS):
                    inst = e.matmul(pv[:, :],
                                    xnT_sb[:, c * 2048 + t * 128: c * 2048 + (t + 1) * 128],
                                    wvn[:, c * 512:(c + 1) * 512],
                                    start=(c == 0), stop=(c == E_BLKS - 1))
                return inst
            w = [(ld, None), (gvn, None)]
            if t >= 2:
                w.append((ve, vcp[t - 2]))
            p.emit("tensor", mmv, waits=w, incs=[(mm, 1)])
            mmv_ = mm.n
            p.emit("vector",
                   lambda e, t=t, pv=pv: e.tensor_copy(v_nat[:, t * 512:(t + 1) * 512],
                                                       pv[:, :]),
                   waits=[(mm, mmv_)], incs=[(ve, 1)])
            vcp[t] = ve.n
        kvw_mm = 0
        prev_ve = vcp[S_TILES - 1]
        prev_pk_ve = 0
        pk0 = [big[0][:, 0:512], big[0][:, 512:1024]]
        pb0 = big[0][:, 1536:2048]
        for g in range(4):
            p.dma("gpsimd", wkl[0][:, :],
                  T["l1_wk"].ap()[g].rearrange("p c j -> p (c j)"),
                  waits=[(mm, kvw_mm)], incs=[(gkv, 16)])
            kse = {}
            for sc in range(4):
                mw = ([(ve, kse[sc - 2])] if sc >= 2
                      else ([(ve, prev_pk_ve)] if prev_pk_ve else []))
                kse[sc] = proj_chunk(wkl[0], pk0[sc & 1], sc, gkv, mw, prev_ve)
            krms = rms_chunks(pb0, C.v(4), kse[3], None, prev_ve)
            p.emit("vector",
                   lambda e, g=g: e.scalar_tensor_tensor(khatT[g][:, :], fT[:, :],
                                                         C.v(8), rstd2[:, :],
                                                         ALU.mult, ALU.mult),
                   waits=[(se, krms), (ve, kse[3])], incs=[(ve, 1)])
            khm = ve.n
            kvw_mm = mm.n
            p.dma("gpsimd", wvl[0][:, :],
                  T["l1_wv"].ap()[g].rearrange("p c j -> p (c j)"),
                  waits=[(mm, kvw_mm)], incs=[(gkv, 16)])
            vse = {}
            for sc in range(4):
                mw = [(ve, vse[sc - 2])] if sc >= 2 else [(ve, kse[3])]
                vse[sc] = proj_chunk(wvl[0], pk0[sc & 1], sc, gkv, mw, khm)
            vrms = rms_chunks(pb0, None, vse[3], None, khm, l2=True)
            p.emit("vector",
                   lambda e, g=g: e.tensor_tensor(vT_bf[g][:, :], fT[:, :],
                                                  rstd2[:, :], ALU.mult),
                   waits=[(se, vrms), (ve, vse[3])], incs=[(ve, 1)])
            prev_ve = ve.n
            prev_pk_ve = vse[3]
            kvw_mm = mm.n
        kv_marks["ve"] = prev_ve
        kv_marks["mm"] = kvw_mm

    qhat_ready = {}
    p_se_end = {}
    head_mm_done = {}
    c_end = {}
    rows_marks = {}
    a_marks = {}

    def P_steps(h):
        hb = h & 1
        arena = big[hb]
        pk = [arena[:, 0:512], arena[:, 512:1024]]
        pb = arena[:, 1536:2048]
        ce2 = c_end.get(h - 2, {"ve": 0, "se": 0, "mm": 0})
        fT_free = qhat_ready.get(h - 1, kv_marks.get("ve", 0))
        steps = []
        loc = {}
        if layer == 0:
            def s_dma():
                p.dma("gpsimd", wql[hb][:, :],
                      T["l0_wq"].ap()[h].rearrange("p c j -> p (c j)"),
                      waits=[(mm, head_mm_done.get(h - 2, 0))], incs=[(gld[hb], 16)])
                p.dma("gpsimd", wkl[hb][:, :],
                      T["l0_wk"].ap()[h].rearrange("p c j -> p (c j)"),
                      incs=[(gld[hb], 16)])
                p.dma("gpsimd", wvl[hb][:, :],
                      T["l0_wv"].ap()[h].rearrange("p c j -> p (c j)"),
                      incs=[(gld[hb], 16)])
            steps.append(s_dma)
            kse = {}

            def s_kproj(sc):
                mw = [(ve, kse[sc - 2])] if sc >= 2 else [(ve, ce2["ve"]), (se, ce2["se"])]
                kse[sc] = proj_chunk(wkl[hb], pk[sc & 1], sc, gld[hb], mw, fT_free)
            steps += [lambda sc=sc: s_kproj(sc) for sc in range(4)]

            def s_krms():
                loc["krms"] = rms_chunks(pb, C.v(3), kse[3], (ve, ce2["ve"]), fT_free)
            steps.append(s_krms)

            def s_khat():
                p.emit("vector",
                       lambda e: e.scalar_tensor_tensor(khatT[hb][:, :], fT[:, :],
                                                        s2c[:, :], rstd2[:, :],
                                                        ALU.mult, ALU.mult),
                       waits=[(se, loc["krms"]), (ve, kse[3]), (mm, ce2["mm"])],
                       incs=[(ve, 1)])
                loc["khat"] = ve.n
            steps.append(s_khat)

            def s_vt(t0):
                for t in (t0, t0 + 1, t0 + 2, t0 + 3):
                    pv = pk[t & 1][:, 0:128]

                    def mmv(e, t=t, pv=pv, hb=hb):
                        for c in range(E_BLKS):
                            inst = e.matmul(pv,
                                            xnT_sb[:, c * 2048 + t * 128: c * 2048 + (t + 1) * 128],
                                            wvl[hb][:, c * 128:(c + 1) * 128],
                                            start=(c == 0), stop=(c == E_BLKS - 1))
                        return inst
                    vw = [(gld[hb], None)]
                    if t < 2:
                        vw.append((ve, kse[3]))
                    else:
                        vw.append((ve, loc[("vcp", t - 2)]))
                    p.emit("tensor", mmv, waits=vw, incs=[(mm, 1)])
                    mmv_ = mm.n
                    p.emit("vector",
                           lambda e, t=t, pv=pv, hb=hb: e.tensor_copy(
                               v_nat[hb][:, t * 128:(t + 1) * 128], pv),
                           waits=[(mm, mmv_)], incs=[(ve, 1)])
                    loc[("vcp", t)] = ve.n
            steps += [lambda t0=t0: s_vt(t0) for t0 in range(0, 16, 4)]
            qse = {}

            def s_qproj(sc):
                mw = [(ve, qse[sc - 2])] if sc >= 2 else [(ve, loc[("vcp", 14 + sc)])]
                qse[sc] = proj_chunk(wql[hb], pk[sc & 1], sc, gld[hb], mw, loc["khat"])
            steps += [lambda sc=sc: s_qproj(sc) for sc in range(4)]

            def s_qrms():
                loc["qrms"] = rms_chunks(pb, C.v(3), qse[3], None, loc["khat"])
            steps.append(s_qrms)

            def s_qhat():
                p.emit("vector",
                       lambda e: e.tensor_tensor(qhatT[hb][:, :], fT[:, :], rstd2[:, :],
                                                 ALU.mult),
                       waits=[(se, loc["qrms"]), (ve, qse[3]), (mm, ce2["mm"])],
                       incs=[(ve, 1)])
                qhat_ready[h] = ve.n
                head_mm_done[h] = mm.n
                p_se_end[h] = se.n
            steps.append(s_qhat)
        else:
            def s_dma():
                p.dma("gpsimd", wql[hb][:, :],
                      T["l1_wq"].ap()[h].rearrange("p c j -> p (c j)"),
                      waits=[(mm, head_mm_done.get(h - 2, kv_marks["mm"]))],
                      incs=[(gld[hb], 16)])
            steps.append(s_dma)
            qse = {}

            def s_qproj(sc):
                mw = [(ve, qse[sc - 2])] if sc >= 2 else [(ve, ce2["ve"]), (se, ce2["se"])]
                qse[sc] = proj_chunk(wql[hb], pk[sc & 1], sc, gld[hb], mw, fT_free)
            steps += [lambda sc=sc: s_qproj(sc) for sc in range(4)]

            def s_qrms():
                loc["qrms"] = rms_chunks(pb, C.v(4), qse[3], (ve, ce2["ve"]), fT_free)
            steps.append(s_qrms)

            def s_qhat():
                p.emit("vector",
                       lambda e: e.tensor_tensor(qhatT[hb][:, :], fT[:, :], rstd2[:, :],
                                                 ALU.mult),
                       waits=[(se, loc["qrms"]), (ve, qse[3]), (mm, ce2["mm"])],
                       incs=[(ve, 1)])
                qhat_ready[h] = ve.n
                head_mm_done[h] = mm.n
                p_se_end[h] = se.n
            steps.append(s_qhat)
        return steps

    def A_steps(h):
        hb = h & 1
        psn = big[hb][:, :]
        stt = {}

        def s_tile(t):
            def mmsn(e, t=t):
                for kc in range(4):
                    inst = e.matmul(psn[:, kc * 512:(kc + 1) * 512],
                                    qhatT[hb][:, t * 128:(t + 1) * 128],
                                    khatT[hb][:, kc * 512:(kc + 1) * 512],
                                    start=True, stop=True)
                return inst
            if t == 0:
                aw = [(ve, qhat_ready[h]), (se, p_se_end[h])]
            else:
                aw = [(ve, stt[("mx", t - 1)]), (se, stt[("ex", t - 1)])]
            p.emit("tensor", mmsn, waits=aw, incs=[(mm, 1)])
            mmv_ = mm.n
            p.emit("vector",
                   lambda e, t=t: e.reduce_max(cols[:, t:t + 1], psn[:, :],
                                               axis=AX.X, negate=True),
                   waits=[(mm, mmv_)], incs=[(ve, 1)])
            stt[("mx", t)] = ve.n
            p.emit("scalar",
                   lambda e, t=t: e.activation(e_nat[:, :], psn[:, :], AF.Exp,
                                               bias=cols[:, t:t + 1],
                                               accum_out=cols[:, 16 + t:17 + t]),
                   waits=[(ve, stt[("mx", t)])], incs=[(se, 1)])
            stt[("ex", t)] = se.n

        def done():
            a_marks[h] = dict(mx15=stt[("mx", 15)], ex15=stt[("ex", 15)],
                              zs15=stt[("ex", 15)])
        return [lambda t=t: s_tile(t) for t in range(S_TILES)] + [done]

    def rows_step(h):
        hb = h & 1
        pmr = big[hb][0:1, 0:128]
        am = a_marks[h]

        def s_rows():
            trcp = 0
            for t in range(S_TILES):
                w = ([(ve, am["mx15"]), (se, am["ex15"]), (C.ld, None)]
                     if t == 0 else [(ve, trcp)])
                p.emit("tensor",
                       lambda e, t=t: e.transpose(pmr[:, :], cols[:, t:t + 1],
                                                  C.idn_f32[:, :]),
                       waits=w, incs=[(mm, 1)])
                mmv_ = mm.n
                p.emit("vector",
                       lambda e, t=t: e.tensor_copy(mrow[0:1, t * 128:(t + 1) * 128],
                                                    pmr[0:1, :]),
                       waits=[(mm, mmv_)], incs=[(ve, 1)])
                trcp = ve.n
            p.emit("vector", lambda e: e.reciprocal(cols[:, 16:32], cols[:, 16:32]),
                   waits=[(se, am["ex15"]), (ve, trcp)], incs=[(ve, 1)])
            trcp = ve.n
            for t in range(S_TILES):
                p.emit("tensor",
                       lambda e, t=t: e.transpose(pmr[:, :], cols[:, 16 + t:17 + t],
                                                  C.idn_f32[:, :]),
                       waits=[(ve, trcp)], incs=[(mm, 1)])
                mmv_ = mm.n
                p.emit("vector",
                       lambda e, t=t: e.tensor_copy(zrow[0:1, t * 128:(t + 1) * 128],
                                                    pmr[0:1, :]),
                       waits=[(mm, mmv_)], incs=[(ve, 1)])
                trcp = ve.n
            rows_marks[h] = trcp
        return [s_rows]

    def C_steps(h):
        hb = h & 1
        arena = big[hb]
        pT = [arena[:, 0:512], arena[:, 512:1024]]
        pav = arena[:, 1024:1536]
        paux = arena[:, 1536:2048]
        st = {}
        if layer == 0:
            am = a_marks[h]
            ce1 = c_end.get(h - 1, {})

            def s_qc(qc):
                bw = ([(ve, rows_marks[h]), (se, am["ex15"])] if qc == 0
                      else [(ve, st[("zbev", qc - 1)])])
                p.emit("tensor",
                       lambda e, qc=qc: e.matmul(paux, C.ones_row_bf[:, :],
                                                 mrow[0:1, qc * 512:(qc + 1) * 512],
                                                 start=True, stop=True),
                       waits=bw, incs=[(mm, 1)])
                mmb = mm.n
                bev = [(mm, mmb)]
                if qc:
                    bev.append((ve, st[("add", qc - 1, 15)]))
                elif ce1:
                    bev.append((ve, ce1["add15"]))
                p.emit("vector", lambda e: e.tensor_copy(biasb[:, :], paux),
                       waits=bev, incs=[(ve, 1)])
                st[("bias", qc)] = ve.n
                for kt in range(S_TILES):
                    b = kt & 1
                    pr = kt // 2
                    if kt >= 2:
                        sw = [(ve, st[("add", qc, kt - 2)])]
                    elif qc == 0:
                        sw = [(ve, am["mx15"]), (se, am["ex15"])]
                    else:
                        sw = [(ve, st[("add", qc - 1, 14 + kt)])]
                    p.emit("tensor",
                           lambda e, kt=kt, qc=qc, b=b: e.matmul(
                               pT[b], khatT[hb][:, kt * 128:(kt + 1) * 128],
                               qhatT[hb][:, qc * 512:(qc + 1) * 512],
                               start=True, stop=True),
                           waits=sw, incs=[(mm, 1)])
                    mms = mm.n
                    aw2 = [(mm, mms), (ve, st[("bias", qc)])]
                    expp = st.get(("exp", qc, pr - 2), (ce1.get("exp15", 0)
                                                       if qc == 0 else
                                                       st.get(("exp", qc - 1, 7 + pr), 0)))
                    if expp:
                        aw2.append((se, expp))
                    p.emit("vector",
                           lambda e, kt=kt, b=b, pr=pr: e.tensor_tensor(
                               smtmp[pr & 1][:, (kt & 1) * 512:(kt & 1) * 512 + 512],
                               pT[b], biasb[:, :], ALU.add),
                           waits=aw2, incs=[(ve, 1)])
                    st[("add", qc, kt)] = ve.n
                    if kt & 1:
                        p.emit("scalar",
                               lambda e, kt=kt, pr=pr: e.activation(
                                   eT[:, (kt - 1) * 512:(kt + 1) * 512],
                                   smtmp[pr & 1][:, :], AF.Exp),
                               waits=[(ve, st[("add", qc, kt)])], incs=[(se, 1)])
                        st[("exp", qc, pr)] = se.n

                def mmav(e, hb=hb):
                    for kt in range(S_TILES):
                        inst = e.matmul(pav, v_nat[hb][:, kt * 128:(kt + 1) * 128],
                                        eT[:, kt * 512:(kt + 1) * 512],
                                        start=(kt == 0), stop=(kt == S_TILES - 1))
                    return inst
                avw = [(se, st[("exp", qc, 7)])]
                aop = st.get(("ao", qc - 1), ce1.get("ao", 0))
                if aop:
                    avw.append((ve, aop))
                p.emit("tensor", mmav, waits=avw, incs=[(mm, 1)])
                st[("avmm", qc)] = mm.n
                p.emit("tensor",
                       lambda e, qc=qc: e.matmul(paux, C.bcast1_f32[:, :],
                                                 zrow[0:1, qc * 512:(qc + 1) * 512],
                                                 start=True, stop=True),
                       waits=[(ve, st[("bias", qc)])], incs=[(mm, 1)])
                mmz = mm.n
                zw = [(mm, mmz)]
                if qc:
                    zw.append((ve, st[("ao", qc - 1)]))
                elif ce1:
                    zw.append((ve, ce1["ao"]))
                p.emit("vector", lambda e: e.tensor_copy(zb[:, :], paux),
                       waits=zw, incs=[(ve, 1)])
                st[("zbev", qc)] = ve.n
                aow = [(mm, st[("avmm", qc)]), (ve, st[("zbev", qc)])]
                dprev = st.get(("dma", qc - 1), ce1.get("dma", 0))
                if dprev:
                    aow.append((sto, dprev))
                p.emit("vector",
                       lambda e: e.tensor_tensor(ao_p[:, :], pav, zb[:, :], ALU.mult),
                       waits=aow, incs=[(ve, 1)])
                st[("ao", qc)] = ve.n
                p.dma("sync", aoT_d[h, :, qc * 512:(qc + 1) * 512], ao_p[:, :],
                      waits=[(ve, st[("ao", qc)])], incs=[(sto, 16)])
                st[("dma", qc)] = sto.n

            def fin():
                c_end[h] = dict(ve=ve.n, se=se.n, mm=mm.n, av_mm=st[("avmm", 3)],
                                ao=st[("ao", 3)], dma=st[("dma", 3)],
                                add15=st[("add", 3, 15)], exp15=st[("exp", 3, 7)])
            return [lambda qc=qc: s_qc(qc) for qc in range(4)] + [fin]
        else:
            g = h // 4
            pzb = paux
            ce1 = c_end.get(h - 1, {})

            def s_qc(qc):
                for kt in range(S_TILES):
                    b = kt & 1
                    if kt >= 2:
                        sw = [(se, st[("exp", qc, kt - 2)])]
                    elif qc == 0:
                        sw = [(ve, qhat_ready[h]), (se, p_se_end[h])]
                        if ce1:
                            sw.append((se, ce1["exp15"]))
                    else:
                        sw = [(se, st[("exp", qc - 1, 14 + kt)])]
                    p.emit("tensor",
                           lambda e, kt=kt, qc=qc, b=b: e.matmul(
                               pT[b], khatT[g][:, kt * 128:(kt + 1) * 128],
                               qhatT[hb][:, qc * 512:(qc + 1) * 512],
                               start=True, stop=True),
                           waits=sw, incs=[(mm, 1)])
                    mms = mm.n
                    p.emit("scalar",
                           lambda e, kt=kt, b=b: e.activation(
                               eT[:, kt * 512:(kt + 1) * 512], pT[b], AF.Exp),
                           waits=[(mm, mms)], incs=[(se, 1)])
                    st[("exp", qc, kt)] = se.n

                def mmav(e, g=g):
                    for kt in range(S_TILES):
                        inst = e.matmul(pav,
                                        v_nat[:, kt * 512 + g * 128: kt * 512 + g * 128 + 128],
                                        eT[:, kt * 512:(kt + 1) * 512],
                                        start=(kt == 0), stop=(kt == S_TILES - 1))
                    return inst
                avw = [(se, st[("exp", qc, 15)])]
                dfp = st.get(("df", qc - 1), ce1.get("df", 0))
                if dfp:
                    avw.append((ve, dfp))
                p.emit("tensor", mmav, waits=avw, incs=[(mm, 1)])
                st[("avmm", qc)] = mm.n

                def mmzs(e):
                    for kt in range(S_TILES):
                        inst = e.matmul(pzb, C.ones128_bf[:, :],
                                        eT[:, kt * 512:(kt + 1) * 512],
                                        start=(kt == 0), stop=(kt == S_TILES - 1))
                    return inst
                zw = [(se, st[("exp", qc, 15)])]
                t2p = st.get(("tmp2", qc - 1), ce1.get("tmp2", 0))
                if t2p:
                    zw.append((ve, t2p))
                p.emit("tensor", mmzs, waits=zw, incs=[(mm, 1)])
                st[("zmm", qc)] = mm.n
                zlw = [(mm, st[("zmm", qc)])]
                p.emit("scalar", lambda e: e.activation(zln[:, :], pzb, AF.Ln),
                       waits=zlw, incs=[(se, 1)])
                st[("zln", qc)] = se.n
                zbw = [(se, st[("zln", qc)])]
                aop = st.get(("ao", qc - 1), ce1.get("ao", 0))
                if aop:
                    zbw.append((ve, aop))
                p.emit("scalar", lambda e: e.activation(zb[:, :], zln[:, :], AF.Exp,
                                                        scale=-1.0),
                       waits=zbw, incs=[(se, 1)])
                st[("zb", qc)] = se.n
                uvw = [(mm, st[("zmm", qc)])]
                p.emit("vector",
                       lambda e, qc=qc, g=g: e.tensor_tensor(
                           uv_bf[:, :], pav, vT_bf[g][:, qc * 512:(qc + 1) * 512],
                           ALU.mult),
                       waits=uvw, incs=[(ve, 1)])
                st[("uv", qc)] = ve.n
                p.emit("tensor",
                       lambda e: e.matmul(pzb, C.ones128_bf[:, :], uv_bf[:, :],
                                          start=True, stop=True),
                       waits=[(ve, st[("uv", qc)]), (se, st[("zln", qc)])],
                       incs=[(mm, 1)])
                mpd = mm.n
                p.emit("vector",
                       lambda e, qc=qc, g=g: e.tensor_tensor(
                           uv_f[:, :], pzb, vT_bf[g][:, qc * 512:(qc + 1) * 512],
                           ALU.mult),
                       waits=[(mm, mpd)], incs=[(ve, 1)])
                st[("tmp2", qc)] = ve.n
                p.emit("vector",
                       lambda e: e.tensor_tensor(df_f[:, :], pav, uv_f[:, :],
                                                 ALU.subtract),
                       waits=[(ve, st[("tmp2", qc)])], incs=[(ve, 1)])
                st[("df", qc)] = ve.n
                aow = [(ve, st[("df", qc)]), (se, st[("zb", qc)])]
                dprev = st.get(("dma", qc - 1), ce1.get("dma", 0))
                if dprev:
                    aow.append((sto, dprev))
                p.emit("vector",
                       lambda e: e.tensor_tensor(ao_p[:, :], df_f[:, :], zb[:, :],
                                                 ALU.mult),
                       waits=aow, incs=[(ve, 1)])
                st[("ao", qc)] = ve.n
                p.dma("sync", aoT_d[h, :, qc * 512:(qc + 1) * 512], ao_p[:, :],
                      waits=[(ve, st[("ao", qc)])], incs=[(sto, 16)])
                st[("dma", qc)] = sto.n

            def fin():
                c_end[h] = dict(ve=ve.n, se=se.n, mm=mm.n, av_mm=st[("zmm", 3)],
                                df=st[("df", 3)], tmp2=st[("tmp2", 3)],
                                ao=st[("ao", 3)], dma=st[("dma", 3)],
                                exp15=st[("exp", 3, 15)])
            return [lambda qc=qc: s_qc(qc) for qc in range(4)] + [fin]

    for f in P_steps(0):
        f()

    for h in range(NH):
        pp = P_steps(h + 1) if h + 1 < NH else []
        ac = A_steps(h) if layer == 0 else []
        na, np_ = len(ac), len(pp)
        ppi = 0
        for i, f in enumerate(ac):
            f()
            want = (i + 1) * np_ // max(na, 1) if na else 0
            while ppi < min(want, np_ // 2):
                pp[ppi]()
                ppi += 1
        if layer == 0:
            for f in rows_step(h):
                f()
        cc = C_steps(h)
        nc_, np2 = len(cc), len(pp) - ppi
        for i, f in enumerate(cc):
            f()
            want = ppi + (i + 1) * np2 // max(nc_, 1)
            while ppi < min(want, len(pp)):
                pp[ppi]()
                ppi += 1
        while ppi < len(pp):
            pp[ppi]()
            ppi += 1

    woall = xnT_sb[:, 0:16 * 1024]
    xb4 = [fT[:, 0:1024], fT[:, 1024:2048], xbf[:, 0:1024], xbf[:, 1024:2048]]
    xo2 = [rstd2[:, 0:1024], rstd2[:, 1024:2048]]
    wo_d = T["l0_wo"].ap() if layer == 0 else T["l1_wo"].ap()
    ce = c_end[NH - 1]
    pw = [big[i // 4][:, (i % 4) * 512:(i % 4 + 1) * 512] for i in range(8)]
    xnT_free = head_mm_done[NH - 1]

    glw_marks = {}
    woldw = {}
    gstop = {}
    evd = {}
    xplm = {}
    outd = {}
    gc = 0
    for i in range(4):
        p.dma("scalar", xb4[i], xbase_d[i, :, 0:1024],
              waits=([(ve, qhat_ready[NH - 1])] if i == 0 else []), incs=[(xpl, 16)])
    xplm[(0, 0)] = xpl.n
    groups = [(eh, tg) for eh in range(2) for tg in range(4)]
    for eh in range(2):
        for c in range(16):
            w = []
            if c == 0:
                w = [(mm, xnT_free)] if eh == 0 else [(mm, gstop[(0, 3)])]
            p.dma("gpsimd", woall[:, c * 1024:(c + 1) * 1024],
                  wo_d[c, :, eh * 1024:(eh + 1) * 1024], waits=w, incs=[(glw, 16)])
            glw_marks[(eh, c)] = glw.n
        for tg in range(4):
            for c in range(16):
                wb = gc & 1
                ww = [(mm, woldw[gc - 2])] if gc >= 2 else [(mm, xnT_free)]
                p.dma("sync", aop_sb[wb], aoT_d[c, :, tg * 512:(tg + 1) * 512],
                      waits=ww, incs=[(aold2[wb], 16)])

                def mmw(e, c=c, wb=wb):
                    for i in range(4):
                        for ec in range(2):
                            inst = e.matmul(pw[i * 2 + ec],
                                            aop_sb[wb][:, i * 128:(i + 1) * 128],
                                            woall[:, c * 1024 + ec * 512: c * 1024 + (ec + 1) * 512],
                                            start=(c == 0), stop=(c == 15))
                    return inst
                mw = [(aold2[wb], None), (glw, glw_marks[(eh, c)])]
                if c == 0:
                    gi = groups.index((eh, tg))
                    if gi == 0:
                        mw += [(ve, ce["ve"]), (se, ce["se"])]
                    else:
                        mw.append((ve, evd[groups[gi - 1]]))
                p.emit("tensor", mmw, waits=mw, incs=[(mm, 1)])
                woldw[gc] = mm.n
                gc += 1
            gstop[(eh, tg)] = mm.n
            gi = groups.index((eh, tg))
            vem = 0
            for i in range(4):
                t = tg * 4 + i
                for ec in range(2):
                    aw = [(mm, gstop[(eh, tg)]), (xpl, xplm[(eh, tg)])]
                    od = outd.get((eh, tg, i - 2), 0)
                    if od:
                        aw.append((sto, od))
                    p.emit("vector",
                           lambda e, i=i, ec=ec: e.tensor_tensor(
                               xo2[i & 1][:, ec * 512:(ec + 1) * 512],
                               xb4[i][:, ec * 512:(ec + 1) * 512],
                               pw[i * 2 + ec], ALU.add),
                           waits=aw, incs=[(ve, 1)])
                    vem = ve.n
                p.dma("sync", xout_d[t, :, eh * 1024:(eh + 1) * 1024], xo2[i & 1],
                      waits=[(ve, vem)], incs=[(sto, 16)])
                outd[(eh, tg, i)] = sto.n
            evd[(eh, tg)] = vem
            if gi + 1 < len(groups):
                neh, ntg = groups[gi + 1]
                for i in range(4):
                    p.dma("scalar", xb4[i], xbase_d[ntg * 4 + i, :, neh * 1024:(neh + 1) * 1024],
                          waits=([(ve, vem)] if i == 0 else []), incs=[(xpl, 16)])
                xplm[(neh, ntg)] = xpl.n
    return sto






def _mk_dram(nc, specs):
    T = {}
    for name, shape, dt, kind in specs:
        T[name] = nc.dram_tensor(name, list(shape), dt, kind=kind)
    return T


_FFN_L0_W = [
    ("l0_wg", (64, 128, 16, 128), I8, "ExternalInput"),
    ("l0_wu", (64, 128, 16, 128), I8, "ExternalInput"),
    ("l0_wd", (64, 128, 2048), I8, "ExternalInput"),
    ("l0_sg", (128, 64), F32, "ExternalInput"),
    ("l0_su", (128, 64), F32, "ExternalInput"),
    ("l0_sdown", (1, 2048), F32, "ExternalInput"),
]
_FFN_L1_W = [
    ("l1_w1", (64, 128, 16, 128), F32, "ExternalInput"),
    ("l1_w3", (64, 128, 16, 128), F32, "ExternalInput"),
    ("l1_w2t", (64, 128, 2048), F32, "ExternalInput"),
    ("h128", (128, 128), BF16, "ExternalInput"),
    ("h2x64", (128, 128), BF16, "ExternalInput"),
    ("l1_w2a", (8192, 2048), BF16, "Internal"),
    ("l1_w2pp", (64, 128, 2048), BF16, "Internal"),
]


_ATTN_L0_W = [
    ("l0_wq", (16, 128, 16, 128), F32, "ExternalInput"),
    ("l0_wk", (16, 128, 16, 128), F32, "ExternalInput"),
    ("l0_wv", (16, 128, 16, 128), F32, "ExternalInput"),
    ("l0_wo", (16, 128, 2048), F32, "ExternalInput"),
    ("l0_qscale", (128, 1), F32, "ExternalInput"),
    ("l0_kscale", (128, 1), F32, "ExternalInput"),
]
_ATTN_L1_W = [
    ("l1_wq", (16, 128, 16, 128), F32, "ExternalInput"),
    ("l1_wk", (4, 128, 16, 128), F32, "ExternalInput"),
    ("l1_wv", (4, 128, 16, 128), F32, "ExternalInput"),
    ("l1_wv_n", (16, 128, 512), F32, "ExternalInput"),
    ("l1_wo", (16, 128, 2048), F32, "ExternalInput"),
]


def build_attn_nc(layer, dbg=False):
    nc = bass.Bass("TRN2", target_bir_lowering=False)
    specs = [
        ("xnT", (16, 128, 2048), BF16, "ExternalInput"),
        ("x_base", (16, 128, 2048), F32, "ExternalInput"),
        ("idn_bf", (128, 128), BF16, "ExternalInput"),
        ("idn_f32", (128, 128), F32, "ExternalInput"),
        ("x_out", (16, 128, 2048), F32, "ExternalOutput"),
        ("aoT", (16, 128, 2048), BF16, "ExternalOutput" if dbg else "Internal"),
    ]
    if dbg:
        specs += [
            ("d_qhatT", (128, 2048), BF16, "ExternalOutput"),
            ("d_fT", (128, 2048), F32, "ExternalOutput"),
            ("d_wql", (128, 2048), BF16, "ExternalOutput"),
            ("d_xnT0", (128, 2048), BF16, "ExternalOutput"),
            ("d_rstd2", (128, 2048), F32, "ExternalOutput"),
            ("d_khatT", (128, 2048), BF16, "ExternalOutput"),
            ("d_v", (128, 2048), BF16, "ExternalOutput"),
            ("d_cols", (128, 34), F32, "ExternalOutput"),
            ("d_mrow", (1, 2048), BF16, "ExternalOutput"),
            ("d_zrow", (1, 2048), F32, "ExternalOutput"),
            ("d_eT", (128, 8192), BF16, "ExternalOutput"),
        ]
    specs += _ATTN_L0_W if layer == 0 else _ATTN_L1_W
    with ExitStack() as ctx:
        T = _mk_dram(nc, specs)
        p = Prog(nc, ctx)
        C = Consts(nc, ctx, p, T)
        sto = emit_attn(p, C, layer, T, tag=f"a{layer}_", dbg=dbg)
        p.emit("sync", lambda e: None, waits=[(sto, None)])
        p.finalize()
    return nc


def build_ffn_nc(layer):
    nc = bass.Bass("TRN2", target_bir_lowering=False)
    specs = [
        ("x_in", (16, 128, 2048), F32, "ExternalInput"),
        ("idn_bf", (128, 128), BF16, "ExternalInput"),
        ("idn_f32", (128, 128), F32, "ExternalInput"),
        ("xnT", (16, 128, 2048), BF16, "ExternalOutput"),
        ("x_acc", (16, 128, 2048), F32, "ExternalOutput"),
    ]
    specs += _FFN_L0_W if layer == 0 else _FFN_L1_W
    with ExitStack() as ctx:
        T = _mk_dram(nc, specs)
        p = Prog(nc, ctx)
        C = Consts(nc, ctx, p, T)
        if layer == 1:
            w2done = emit_w2pp(p, C, T)
            for eng in ENGINES:
                p.emit(eng, lambda e: None, waits=[(w2done, None)])
        sto = emit_ffn(p, C, layer, T, tag=f"f{layer}_")
        p.emit("sync", lambda e: None, waits=[(sto, None)])
        p.finalize()
    return nc


def build_fused_nc():
    nc = bass.Bass("TRN2", target_bir_lowering=False)
    specs = [
        ("x_in", (16, 128, 2048), F32, "ExternalInput"),
        ("idn_bf", (128, 128), BF16, "ExternalInput"),
        ("idn_f32", (128, 128), F32, "ExternalInput"),
        ("h128", (128, 128), BF16, "ExternalInput"),
        ("h2x64", (128, 128), BF16, "ExternalInput"),
        ("out", (16, 128, 2048), F32, "ExternalOutput"),
        ("xnT_s", (16, 128, 2048), BF16, "Internal"),
        ("xacc_s", (16, 128, 2048), F32, "Internal"),
        ("x1_s", (16, 128, 2048), F32, "Internal"),
        ("aoT_s", (16, 128, 2048), BF16, "Internal"),
        ("l1_w2a", (8192, 2048), BF16, "Internal"),
        ("l1_w2pp", (64, 128, 2048), BF16, "Internal"),
    ]
    specs += _FFN_L0_W + _ATTN_L0_W
    specs += [t for t in _FFN_L1_W if t[0] in ("l1_w1", "l1_w3", "l1_w2t")]
    specs += _ATTN_L1_W
    with ExitStack() as ctx:
        T = _mk_dram(nc, specs)
        p = Prog(nc, ctx)
        C = Consts(nc, ctx, p, T)
        xg = ctx.enter_context(nc.sbuf_tensor("xnT_glob", [128, E_BLKS * 2048], BF16))

        def phase(emitter):
            with ExitStack() as pc:
                p.alloc_ctx = pc
                emitter()
            p.alloc_ctx = ctx
            p.barrier()

        Tf0 = dict(T, x_in=T["x_in"], xnT=T["xnT_s"], x_acc=T["xacc_s"])
        Ta0 = dict(T, xnT=T["xnT_s"], x_base=T["xacc_s"], x_out=T["x1_s"], aoT=T["aoT_s"])
        Tf1 = dict(T, x_in=T["x1_s"], xnT=T["xnT_s"], x_acc=T["xacc_s"])
        Ta1 = dict(T, xnT=T["xnT_s"], x_base=T["xacc_s"], x_out=T["out"], aoT=T["aoT_s"])

        phase(lambda: emit_w2pp(p, C, T))
        phase(lambda: emit_ffn(p, C, 0, Tf0, tag="f0_", xg=xg))
        phase(lambda: emit_attn(p, C, 0, Ta0, tag="a0_", xg=xg))
        phase(lambda: emit_ffn(p, C, 1, Tf1, tag="f1_", xg=xg))
        phase(lambda: emit_attn(p, C, 1, Ta1, tag="a1_", xg=xg))
        p.finalize()
    return nc


_BUILD_CACHE = {}


def _get_nc(key, builder):
    if key not in _BUILD_CACHE:
        _BUILD_CACHE[key] = builder()
    return _BUILD_CACHE[key]


def kernel_fused(**inputs):
    w = prep_weights(inputs)
    c = host_consts()
    x = inputs["L0_attn_x"]
    nc = _get_nc("fused", build_fused_nc)
    base = {"idn_bf": c["idn_bf"], "idn_f32": c["idn_f32"],
            "h128": c["h128"], "h2x64": c["h2x64"]}
    for k in ("l0_wg", "l0_wu", "l0_wd", "l0_sg", "l0_su", "l0_sdown",
              "l0_wq", "l0_wk", "l0_wv", "l0_wo", "l0_qscale", "l0_kscale",
              "l1_w1", "l1_w3", "l1_w2t",
              "l1_wq", "l1_wk", "l1_wv", "l1_wv_n", "l1_wo"):
        base[k] = w[k]
    maps = [dict(base, x_in=np.ascontiguousarray(x[i].reshape(16, 128, 2048)))
            for i in range(NCORES)]
    res = run_bass_kernel_spmd(nc, maps, core_ids=list(range(NCORES)))
    out = np.empty((NCORES, 2048, 2048), dtype=np.float32)
    for i in range(NCORES):
        out[i] = res.results[i]["out"].reshape(2048, 2048)
    return out


def kernel(**inputs):
    return kernel_fused(**inputs)


def kernel_unfused(**inputs):
    w = prep_weights(inputs)
    c = host_consts()
    x = inputs["L0_attn_x"]
    consts = {"idn_bf": c["idn_bf"], "idn_f32": c["idn_f32"]}

    def shard(arr):
        return [np.ascontiguousarray(arr[cid].reshape(16, 128, 2048))
                for cid in range(NCORES)]

    core_ids = list(range(NCORES))

    nc = _get_nc("ffn0", lambda: build_ffn_nc(0))
    base = dict(consts)
    for k in ("l0_wg", "l0_wu", "l0_wd", "l0_sg", "l0_su", "l0_sdown"):
        base[k] = w[k]
    xs = shard(x)
    maps = [dict(base, x_in=xs[i]) for i in range(NCORES)]
    res = run_bass_kernel_spmd(nc, maps, core_ids=core_ids)
    xnT = [r["xnT"] for r in res.results]
    xacc = [r["x_acc"] for r in res.results]

    nc = _get_nc("attn0", lambda: build_attn_nc(0))
    base = dict(consts)
    for k in ("l0_wq", "l0_wk", "l0_wv", "l0_wo", "l0_qscale", "l0_kscale"):
        base[k] = w[k]
    maps = [dict(base, xnT=xnT[i], x_base=xacc[i]) for i in range(NCORES)]
    res = run_bass_kernel_spmd(nc, maps, core_ids=core_ids)
    x1 = [r["x_out"] for r in res.results]

    nc = _get_nc("ffn1", lambda: build_ffn_nc(1))
    base = dict(consts)
    for k in ("l1_w1", "l1_w3", "l1_w2t"):
        base[k] = w[k]
    base["h128"] = c["h128"]
    base["h2x64"] = c["h2x64"]
    maps = [dict(base, x_in=x1[i]) for i in range(NCORES)]
    res = run_bass_kernel_spmd(nc, maps, core_ids=core_ids)
    xnT = [r["xnT"] for r in res.results]
    xacc = [r["x_acc"] for r in res.results]

    nc = _get_nc("attn1", lambda: build_attn_nc(1))
    base = dict(consts)
    for k in ("l1_wq", "l1_wk", "l1_wv", "l1_wv_n", "l1_wo"):
        base[k] = w[k]
    maps = [dict(base, xnT=xnT[i], x_base=xacc[i]) for i in range(NCORES)]
    res = run_bass_kernel_spmd(nc, maps, core_ids=core_ids)

    out = np.empty((NCORES, 2048, 2048), dtype=np.float32)
    for i in range(NCORES):
        out[i] = res.results[i]["x_out"].reshape(2048, 2048)
    return out


# version 23 (final)
# speedup vs baseline: 1.1315x; 1.1315x over previous
import numpy as np
import ml_dtypes
from contextlib import ExitStack

import concourse.bass as bass
import concourse.mybir as mybir
from concourse.bass_utils import run_bass_kernel_spmd

F32 = mybir.dt.float32
BF16 = mybir.dt.bfloat16
I8 = mybir.dt.int8
AF = mybir.ActivationFunctionType
ALU = mybir.AluOpType
AX = mybir.AxisListType

NCORES = 8
S_TILES = 16
E_BLKS = 16
O_TILES = 64
CHUNK = 512
N_CHUNKS = 4

RMS_EPS_L0 = 1e-6
RMS_EPS_L1 = float(np.finfo(np.float32).eps)
LN_EPS = 1e-5
FWHT_SCALE = float(1.0 / np.sqrt(8192.0))
SM_SCALE_L1 = float(128.0 ** -0.5)

ENGINES = ("sync", "gpsimd", "scalar", "vector", "tensor")


class Sem:
    def __init__(self, nc, ctx, name):
        self.h = ctx.enter_context(nc.semaphore(name))
        self.n = 0


class Prog:

    def __init__(self, nc, ctx):
        self.nc = nc
        self.ctx = ctx
        self.alloc_ctx = ctx
        self.q = {e: [] for e in ENGINES}
        self._sems = {}

    def barrier(self):
        waits = [(s, None) for s in self._sems.values() if s.n > 0]
        for eng in ENGINES:
            self.emit(eng, lambda e: None, waits=waits)

    def sem(self, name):
        if name not in self._sems:
            self._sems[name] = Sem(self.nc, self.ctx, name)
        return self._sems[name]

    def emit(self, eng, fn, waits=(), incs=()):
        wlist = [(s.h, (s.n if v is None else v)) for (s, v) in waits]
        ilist = []
        for s, k in incs:
            s.n += k
            ilist.append((s.h, k))

        def run(e):
            for h, v in wlist:
                if v > 0:
                    e.wait_ge(h, v)
            inst = fn(e)
            for h, k in ilist:
                assert inst is not None
                inst = inst.then_inc(h, k)
        self.q[eng].append(run)

    def dma(self, eng, out_ap, in_ap, waits=(), incs=()):
        self.emit(eng, lambda e: e.dma_start(out_ap, in_ap), waits, incs)

    def finalize(self):
        with self.nc.Block() as b:
            for name in ENGINES:
                fns = self.q[name]
                if not fns:
                    continue
                getattr(b, name)(lambda e, fns=fns: [f(e) for f in fns])


def _c(a):
    return np.ascontiguousarray(a)


def prep_weights(ins):
    w = {}
    for nm in ("wq", "wk", "wv"):
        a = ins[f"L0_attn_{nm}"]
        w[f"l0_{nm}"] = _c(a.reshape(16, 128, 16, 128).transpose(2, 1, 0, 3))
    w["l0_wo"] = _c(ins["L0_attn_wo"].reshape(16, 128, 2048))
    w["l0_qscale"] = _c(ins["L0_attn_q_scale"].reshape(128, 1))
    w["l0_kscale"] = _c(ins["L0_attn_k_scale"].reshape(128, 1))
    for nm, key in (("wg", "L0_ffn_w_gate"), ("wu", "L0_ffn_w_up")):
        a = ins[key]
        w[f"l0_{nm}"] = _c(a.reshape(16, 128, 64, 128).transpose(2, 1, 0, 3))
    w["l0_wd"] = _c(ins["L0_ffn_w_down"].reshape(64, 128, 2048))
    w["l0_sg"] = _c(ins["L0_ffn_s_gate"].reshape(64, 128).T)
    w["l0_su"] = _c(ins["L0_ffn_s_up"].reshape(64, 128).T)
    w["l0_sdown"] = _c(ins["L0_ffn_s_down"].reshape(1, 2048))
    a = ins["L1_attn_Wq"]
    w["l1_wq"] = _c(a.reshape(16, 128, 16, 128).transpose(0, 3, 2, 1))
    for nm in ("Wk", "Wv"):
        a = ins[f"L1_attn_{nm}"]
        w[f"l1_{nm.lower()}"] = _c(a.reshape(4, 128, 16, 128).transpose(0, 3, 2, 1))
    w["l1_wv_n"] = _c(ins["L1_attn_Wv"].T.reshape(16, 128, 512))
    w["l1_wo"] = _c(ins["L1_attn_Wo"].T.reshape(16, 128, 2048))
    for nm, key in (("w1", "L1_ffn_W1"), ("w3", "L1_ffn_W3")):
        a = ins[key]
        w[f"l1_{nm}"] = _c(a.reshape(64, 128, 16, 128).transpose(0, 3, 2, 1))
    w["l1_w2t"] = _c(ins["L1_ffn_W2"].T.reshape(64, 128, 2048))
    return w


def host_consts():
    c = {}
    c["idn_bf"] = np.eye(128, dtype=ml_dtypes.bfloat16)
    c["idn_f32"] = np.eye(128, dtype=np.float32)

    def had(n):
        h = np.array([[1.0]])
        while h.shape[0] < n:
            h = np.block([[h, h], [h, -h]])
        return h
    c["h128"] = had(128).astype(ml_dtypes.bfloat16)
    h64 = had(64)
    z = np.zeros((64, 64))
    c["h2x64"] = np.block([[h64, z], [z, h64]]).astype(ml_dtypes.bfloat16)
    return c


class Consts:
    def __init__(self, nc, ctx, p, T):
        sb = lambda name, shape, dt: ctx.enter_context(nc.sbuf_tensor(name, shape, dt))
        self.idn_bf = sb("c_idn_bf", [128, 128], BF16)
        self.idn_f32 = sb("c_idn_f32", [128, 128], F32)
        self.bcast1_f32 = sb("c_bcast1", [1, 128], F32)
        self.ones_row_bf = sb("c_ones_row_bf", [1, 128], BF16)
        self.red1_f32 = sb("c_red1_f", [128, 1], F32)
        self.red1_bf = sb("c_red1_b", [128, 1], BF16)
        self.ones128_f32 = sb("c_ones128f", [128, 128], F32)
        self.ones128_bf = sb("c_ones128b", [128, 128], BF16)
        self.vals = sb("c_vals", [128, 12], F32)
        ld = p.sem("c_ld")
        p.dma("sync", self.idn_bf[:, :], T["idn_bf"].ap()[:, :], incs=[(ld, 16)])
        p.dma("sync", self.idn_f32[:, :], T["idn_f32"].ap()[:, :], incs=[(ld, 16)])
        g = p.sem("c_set")
        for col, v in ((0, -1.0 / 2048.0), (1, 1.0 / 2048.0), (2, LN_EPS),
                       (3, RMS_EPS_L0), (4, RMS_EPS_L1), (5, FWHT_SCALE),
                       (6, 1.0 / 128.0), (7, 1e-24), (8, SM_SCALE_L1)):
            p.emit("vector", lambda e, col=col, v=v: e.memset(self.vals[:, col:col + 1], v),
                   incs=[(g, 1)])
        for t in (self.bcast1_f32, self.ones_row_bf, self.red1_f32, self.red1_bf,
                  self.ones128_f32, self.ones128_bf):
            p.emit("vector", lambda e, t=t: e.memset(t[:, :], 1.0), incs=[(g, 1)])
        self.ld = ld
        self.ready = g

    def v(self, col):
        return self.vals[:, col:col + 1]

    def v1(self, col):
        return self.vals[0:1, col:col + 1]



def emit_ffn(p, C, layer, T, tag, xg=None):
    nc = p.nc
    ctx = p.alloc_ctx
    sb = lambda name, shape, dt: ctx.enter_context(nc.sbuf_tensor(tag + name, shape, dt))

    x_in = T["x_in"].ap()
    xnT_d = T["xnT"].ap()
    xacc_d = T["x_acc"].ap()
    if layer == 0:
        wg_d, wu_d, wd_d = T["l0_wg"].ap(), T["l0_wu"].ap(), T["l0_wd"].ap()
    else:
        wg_d, wu_d, wd_d = T["l1_w1"].ap(), T["l1_w3"].ap(), T["l1_w2pp"].ap()

    xbuf = sb("xbuf", [128, 2048], F32)
    xsq = sb("xsq", [128, 1024], F32)
    st = sb("stats", [128, 12], F32)
    xn_bf = sb("xn_bf", [128, 2048], BF16)
    xnT_sb = xg if xg is not None else sb("xnT_sb", [128, E_BLKS * CHUNK], BF16)
    XS = 2048 if xg is not None else CHUNK
    hidT = sb("hidT", [128, O_TILES * CHUNK], BF16)
    wgl = [sb(f"wgl{b}", [128, 2048], BF16) for b in range(2)]
    wul = [sb(f"wul{b}", [128, 2048], BF16) for b in range(2)]
    wdp = [sb(f"wdp{b}", [128, 4096], BF16) for b in range(2)]
    silu_s = sb("silu_s", [128, CHUNK], F32)
    up_s = sb("up_s", [128, CHUNK], F32)
    xacc = [sb(f"xacc{i}", [128, 1024], F32) for i in range(4)]
    if layer == 0:
        sg_sb = sb("sg", [128, 64], F32)
        su_sb = sb("su", [128, 64], F32)
        sdb = sb("sdb", [128, 2048], F32)

    xld = p.sem(tag + "xld")
    ald = p.sem(tag + "ald")
    wld = p.sem(tag + "wld")
    ggu = [p.sem(tag + f"ggu{b}") for b in range(2)]
    gwd = [p.sem(tag + f"gwd{b}") for b in range(2)]
    lnv = p.sem(tag + "lnv")
    trp = p.sem(tag + "trp")
    xnv = p.sem(tag + "xnv")
    mmg = p.sem(tag + "mmg")
    ev = p.sem(tag + "ev")
    mmd = p.sem(tag + "mmd")
    dev = p.sem(tag + "dev")
    sto = p.sem(tag + "sto")

    if layer == 0:
        p.dma("sync", sg_sb[:, :], T["l0_sg"].ap()[:, :], incs=[(wld, 16)])
        p.dma("sync", su_sb[:, :], T["l0_su"].ap()[:, :], incs=[(wld, 16)])
        p.dma("sync", sdb[0:1, :], T["l0_sdown"].ap()[:, :], incs=[(wld, 16)])
        with nc.psum_tensor(tag + "pbc", [128, 512], F32) as pbc:
            for i in range(4):
                p.emit("tensor",
                       lambda e, i=i, pbc=pbc: e.matmul(pbc[:, :], C.bcast1_f32[:, :],
                                               sdb[0:1, i * 512:(i + 1) * 512],
                                               start=True, stop=True),
                       waits=[(wld, None), (C.ready, None), (ev, None)], incs=[(mmg, 1)])
                p.emit("vector",
                       lambda e, i=i, pbc=pbc: e.tensor_copy(sdb[:, i * 512:(i + 1) * 512], pbc[:, :]),
                       waits=[(mmg, None)], incs=[(ev, 1)])

    gu_ev_done = {}
    gu_mm_done = {}
    dq_mm_done = {}
    for q in range(N_CHUNKS):
        tiles = [q * 4 + i for i in range(4)]
        with nc.psum_tensor(f"{tag}ptr{q}", [128, 512], BF16) as ptr, \
             nc.psum_tensor(f"{tag}pg{q}a", [128, 512], F32) as pg0, \
             nc.psum_tensor(f"{tag}pg{q}b", [128, 512], F32) as pg1, \
             nc.psum_tensor(f"{tag}pu{q}a", [128, 512], F32) as pu0, \
             nc.psum_tensor(f"{tag}pu{q}b", [128, 512], F32) as pu1:
            pg = [pg0, pg1]
            pu = [pu0, pu1]
            for i, t in enumerate(tiles):
                p.dma("sync", xbuf[:, :], x_in[t], waits=[(lnv, None)], incs=[(xld, 16)])
                p.emit("vector", lambda e: e.reduce_sum(st[:, 0:1], xbuf[:, :], axis=AX.X),
                       waits=[(xld, None)], incs=[(lnv, 1)])
                p.emit("scalar", lambda e: e.activation(xsq[:, :], xbuf[:, 0:1024],
                                                        AF.Square,
                                                        accum_out=st[:, 8:9]),
                       waits=[(xld, None), (lnv, None)], incs=[(lnv, 1)])
                p.emit("scalar", lambda e: e.activation(xsq[:, :], xbuf[:, 1024:2048],
                                                        AF.Square,
                                                        accum_out=st[:, 9:10]),
                       waits=[(lnv, None)], incs=[(lnv, 1)])
                p.emit("vector", lambda e: e.tensor_tensor(st[:, 1:2], st[:, 8:9],
                                                           st[:, 9:10], ALU.add),
                       waits=[(lnv, None)], incs=[(lnv, 1)])
                p.emit("vector", lambda e: e.tensor_scalar(st[:, 2:3], st[:, 0:1],
                                                           C.v(0), None, ALU.mult),
                       waits=[(lnv, None)], incs=[(lnv, 1)])
                p.emit("vector", lambda e: e.tensor_tensor(st[:, 3:4], st[:, 2:3], st[:, 2:3],
                                                           ALU.mult),
                       waits=[(lnv, None)], incs=[(lnv, 1)])
                p.emit("vector", lambda e: e.scalar_tensor_tensor(st[:, 4:5], st[:, 1:2],
                                                                  C.v(1), st[:, 3:4],
                                                                  ALU.mult, ALU.subtract),
                       waits=[(lnv, None)], incs=[(lnv, 1)])
                p.emit("scalar", lambda e: e.activation(st[:, 5:6], st[:, 4:5], AF.Ln,
                                                        bias=C.v(2)),
                       waits=[(lnv, None)], incs=[(lnv, 1)])
                p.emit("scalar", lambda e: e.activation(st[:, 6:7], st[:, 5:6], AF.Exp,
                                                        scale=-0.5),
                       waits=[(lnv, None)], incs=[(lnv, 1)])
                p.emit("vector", lambda e: e.tensor_scalar(xn_bf[:, :], xbuf[:, :],
                                                           st[:, 2:3], st[:, 6:7],
                                                           ALU.add, ALU.mult),
                       waits=[(lnv, None), (trp, None)], incs=[(lnv, 1)])
                for gblk in range(4):
                    for j in range(4):
                        eb = gblk * 4 + j
                        w = [(lnv, None), (C.ld, None), (xnv, None), (dev, None),
                             (ev, None)] if j == 0 else []
                        p.emit("tensor",
                               lambda e, eb=eb, j=j, ptr=ptr: e.transpose(
                                   ptr[:, j * 128:(j + 1) * 128],
                                   xn_bf[:, eb * 128:(eb + 1) * 128],
                                   C.idn_bf[:, :]),
                               waits=w, incs=([(trp, 1)] if j == 3 else []))
                    dst = xnT_sb[:, :].rearrange("p (eb s) -> p eb s", eb=E_BLKS)
                    co = (q * CHUNK if xg is not None else 0) + i * 128
                    cw = [(trp, None)] if xg is not None else [(trp, None), (sto, None)]
                    p.emit("vector",
                           lambda e, gblk=gblk, co=co, dst=dst, ptr=ptr: e.tensor_copy(
                               dst[:, gblk * 4:gblk * 4 + 4, co:co + 128],
                               ptr[:, :].rearrange("p (j s) -> p j s", j=4)),
                           waits=cw, incs=[(xnv, 1)])
            if xg is None:
                for eb in range(E_BLKS):
                    p.dma("sync", xnT_d[eb, :, q * CHUNK:(q + 1) * CHUNK],
                          xnT_sb[:, eb * CHUNK:(eb + 1) * CHUNK],
                          waits=[(xnv, None)], incs=[(sto, 16)])

            for ot in range(O_TILES):
                b = ot & 1
                gq = q * O_TILES + ot
                wg_w = [(mmg, gu_mm_done.get(gq - 2, 0))]
                p.dma("gpsimd", wgl[b][:, :], wg_d[ot].rearrange("p c j -> p (c j)"),
                      waits=wg_w, incs=[(ggu[b], 16)])
                p.dma("gpsimd", wul[b][:, :], wu_d[ot].rearrange("p c j -> p (c j)"),
                      incs=[(ggu[b], 16)])

                def mm_gate(e, pg=pg, b=b, q=q):
                    qo = q * CHUNK if xg is not None else 0
                    for c in range(E_BLKS):
                        inst = e.matmul(pg[b][:, :],
                                        wgl[b][:, c * 128:(c + 1) * 128],
                                        xnT_sb[:, c * XS + qo: c * XS + qo + CHUNK],
                                        start=(c == 0), stop=(c == E_BLKS - 1))
                    return inst
                g_w = [(ggu[b], None), (xnv, None), (dev, None)]
                if gq >= 2:
                    g_w.append((ev, gu_ev_done[gq - 2]))
                else:
                    g_w.append((ev, None))
                p.emit("tensor", mm_gate, waits=g_w, incs=[(mmg, 1)])

                def mm_up(e, pu=pu, b=b, q=q):
                    qo = q * CHUNK if xg is not None else 0
                    for c in range(E_BLKS):
                        inst = e.matmul(pu[b][:, :], wul[b][:, c * 128:(c + 1) * 128],
                                        xnT_sb[:, c * XS + qo: c * XS + qo + CHUNK],
                                        start=(c == 0), stop=(c == E_BLKS - 1))
                    return inst
                p.emit("tensor", mm_up, incs=[(mmg, 1)])

                if layer == 0:
                    p.emit("scalar",
                           lambda e, ot=ot, pg=pg, b=b: e.activation(
                               silu_s[:, :], pg[b][:, :], AF.Silu,
                               scale=sg_sb[:, ot:ot + 1]),
                           waits=[(mmg, None), (ev, None), (wld, None)], incs=[(ev, 1)])
                    p.emit("vector",
                           lambda e, ot=ot, pu=pu, b=b: e.tensor_scalar(
                               up_s[:, :], pu[b][:, :],
                               su_sb[:, ot:ot + 1], None, ALU.mult),
                           waits=[(mmg, None), (wld, None)], incs=[(ev, 1)])
                    p.emit("vector",
                           lambda e, ot=ot: e.tensor_tensor(
                               hidT[:, ot * CHUNK:(ot + 1) * CHUNK],
                               silu_s[:, :], up_s[:, :], ALU.mult),
                           waits=[(ev, None)], incs=[(ev, 1)])
                else:
                    p.emit("scalar",
                           lambda e, pg=pg, b=b: e.activation(silu_s[:, :], pg[b][:, :],
                                                              AF.Silu),
                           waits=[(mmg, None), (ev, None)], incs=[(ev, 1)])
                    p.emit("vector",
                           lambda e, ot=ot, pu=pu, b=b: e.tensor_tensor(
                               hidT[:, ot * CHUNK:(ot + 1) * CHUNK],
                               silu_s[:, :], pu[b][:, :], ALU.mult),
                           waits=[(mmg, None), (ev, None)], incs=[(ev, 1)])
                gu_ev_done[gq] = ev.n
                gu_mm_done[gq] = mmg.n

        with ExitStack() as ps:
            pd = [ps.enter_context(nc.psum_tensor(f"{tag}pd{q}_{i}", [128, 512], F32))
                  for i in range(8)]
            for dh in range(2):
                for i, t in enumerate(tiles):
                    aw_ = [(dev, None)] if dh == 0 else [(sto, None)]
                    p.dma("sync", xacc[i][:, :], x_in[t][:, dh * 1024:(dh + 1) * 1024],
                          waits=aw_, incs=[(ald, 16)])
                for qt in range(O_TILES // 4):
                    b = qt & 1
                    gqt = (q * 2 + dh) * (O_TILES // 4) + qt
                    d_w = [(mmd, dq_mm_done.get(gqt - 2, 0))]
                    p.dma("gpsimd", wdp[b][:, :].rearrange("p (t d) -> p t d", t=4),
                          wd_d[4 * qt:4 * qt + 4, :, dh * 1024:(dh + 1) * 1024]
                          .rearrange("t p d -> p t d"),
                          waits=d_w, incs=[(gwd[b], 16)])
                    for j in range(4):
                        tt = 4 * qt + j

                        def mm_down(e, tt=tt, j=j, b=b, pd=pd):
                            for i in range(4):
                                for dc in range(2):
                                    inst = e.matmul(
                                        pd[i * 2 + dc][:, :],
                                        hidT[:, tt * CHUNK + i * 128: tt * CHUNK + (i + 1) * 128],
                                        wdp[b][:, j * 1024 + dc * 512: j * 1024 + (dc + 1) * 512],
                                        start=(tt == 0), stop=(tt == O_TILES - 1))
                            return inst
                        mw = [(gwd[b], None)]
                        if tt == 0:
                            mw += [(ev, None), (dev, None)]
                        p.emit("tensor", mm_down, waits=mw, incs=[(mmd, 1)])
                    dq_mm_done[gqt] = mmd.n
                for i in range(4):
                    for dc in range(2):
                        col = dh * 1024 + dc * 512
                        lcol = dc * 512
                        if layer == 0:
                            p.emit("vector",
                                   lambda e, i=i, dc=dc, col=col, pd=pd: e.tensor_tensor(
                                       up_s[:, :], pd[i * 2 + dc][:, :],
                                       sdb[:, col:col + 512], ALU.mult),
                                   waits=[(mmd, None), (ald, None), (dev, None)],
                                   incs=[(dev, 1)])
                            p.emit("vector",
                                   lambda e, i=i, lcol=lcol: e.tensor_tensor(
                                       xacc[i][:, lcol:lcol + 512], xacc[i][:, lcol:lcol + 512],
                                       up_s[:, :], ALU.add),
                                   waits=[(dev, None)], incs=[(dev, 1)])
                        else:
                            p.emit("vector",
                                   lambda e, i=i, dc=dc, lcol=lcol, pd=pd: e.tensor_tensor(
                                       xacc[i][:, lcol:lcol + 512], xacc[i][:, lcol:lcol + 512],
                                       pd[i * 2 + dc][:, :], ALU.add),
                                   waits=[(mmd, None), (ald, None)], incs=[(dev, 1)])
                for i, t in enumerate(tiles):
                    p.dma("sync", xacc_d[t][:, dh * 1024:(dh + 1) * 1024], xacc[i][:, :],
                          waits=[(dev, None)], incs=[(sto, 16)])

    return sto



def emit_w2pp(p, C, T, tag="w2_"):
    nc = p.nc
    ctx = p.alloc_ctx
    sb = lambda name, shape, dt: ctx.enter_context(nc.sbuf_tensor(tag + name, shape, dt))
    w2t = T["l1_w2t"].ap()
    w2a = T["l1_w2a"].ap()
    w2pp_flat = T["l1_w2pp"].ap().rearrange("t p d -> (t p) d")

    NB = 6
    h128 = sb("h128", [128, 128], BF16)
    h2x64 = sb("h2x64", [128, 128], BF16)
    win = [sb(f"win{b}", [128, 2048], BF16) for b in range(NB)]
    wout = [sb(f"wout{b}", [128, 2048], BF16) for b in range(NB)]
    ld = p.sem(tag + "ld")
    gld = [p.sem(tag + f"gld{b}") for b in range(NB)]
    mm = p.sem(tag + "mm")
    evs = p.sem(tag + "ev")
    sto = p.sem(tag + "sto")
    p.dma("sync", h128[:, :], T["h128"].ap()[:, :], incs=[(ld, 16)])
    p.dma("sync", h2x64[:, :], T["h2x64"].ap()[:, :], incs=[(ld, 16)])

    w2a_v = w2a.rearrange("(a b tt) d -> a b tt d", a=64, b=2)
    w2a_g = w2a.rearrange("(g q) d -> g q d", g=64)
    w2pp_v = w2pp_flat.rearrange("(tau gb) d -> tau gb d", gb=128)

    mm_done = {}
    sto_done = {}
    with nc.psum_tensor(tag + "psA", [128, 512], F32) as ps0, \
         nc.psum_tensor(tag + "psB", [128, 512], F32) as ps1, \
         nc.psum_tensor(tag + "psC", [128, 512], F32) as ps2, \
         nc.psum_tensor(tag + "psD", [128, 512], F32) as ps3:
        pss = [ps0, ps1, ps2, ps3]
        evm = {}
        gchunk = 0
        for t in range(128):
            b = t % NB
            is2 = t >= 64
            src = w2a_g[t - 64] if is2 else w2t[t]
            dw = [(mm, mm_done.get(t - NB, 0))]
            if is2:
                dw.append((sto, sto_done[63]))
            p.dma("gpsimd", win[b][:, :], src, waits=dw, incs=[(gld[b], 16)])
            hmat = h2x64 if is2 else h128
            for dchunk in range(4):
                ps = pss[gchunk & 3]

                def mmw2(e, dchunk=dchunk, b=b, hmat=hmat, ps=ps):
                    return e.matmul(ps[:, :], hmat[:, :],
                                    win[b][:, dchunk * 512:(dchunk + 1) * 512],
                                    start=True, stop=True)
                mw = [(ld, None), (gld[b], None)]
                if gchunk >= 4:
                    mw.append((evs, evm[gchunk - 4]))
                p.emit("tensor", mmw2, waits=mw, incs=[(mm, 1)])
                mmv_ = mm.n
                if is2:
                    p.emit("vector",
                           lambda e, dchunk=dchunk, b=b, ps=ps: e.tensor_scalar(
                               wout[b][:, dchunk * 512:(dchunk + 1) * 512], ps[:, :],
                               C.v(5), None, ALU.mult),
                           waits=[(mm, mmv_), (C.ready, None), (sto, sto_done.get(t - NB, 0))],
                           incs=[(evs, 1)])
                else:
                    p.emit("vector",
                           lambda e, dchunk=dchunk, b=b, ps=ps: e.tensor_copy(
                               wout[b][:, dchunk * 512:(dchunk + 1) * 512], ps[:, :]),
                           waits=[(mm, mmv_), (sto, sto_done.get(t - NB, 0))],
                           incs=[(evs, 1)])
                evm[gchunk] = evs.n
                gchunk += 1
            mm_done[t] = mm.n
            if is2:
                g = t - 64
                for beta in range(2):
                    p.dma("sync", w2pp_v[:, g * 2 + beta, :],
                          wout[b][beta * 64:(beta + 1) * 64, :],
                          waits=[(evs, evm[gchunk - 1])], incs=[(sto, 16)])
            else:
                p.dma("sync", w2a_v[:, :, t, :].rearrange("a b d -> (a b) d"), wout[b][:, :],
                      waits=[(evs, evm[gchunk - 1])], incs=[(sto, 16)])
            sto_done[t] = sto.n
    return sto


def emit_attn(p, C, layer, T, tag, xg=None):
    nc = p.nc
    ctx = p.alloc_ctx
    sb = lambda name, shape, dt: ctx.enter_context(nc.sbuf_tensor(tag + name, shape, dt))

    xnT_d = T["xnT"].ap()
    xbase_d = T["x_base"].ap()
    xout_d = T["x_out"].ap()
    aoT_d = T["aoT"].ap()
    NH = 16

    xnT_sb = xg if xg is not None else sb("xnT", [128, E_BLKS * 2048], BF16)
    wql = [sb(f"wql{b}", [128, 2048], BF16) for b in range(2)]
    nkv = 2 if layer == 0 else 1
    wkl = [sb(f"wkl{b}", [128, 2048], BF16) for b in range(nkv)]
    wvl = [sb(f"wvl{b}", [128, 2048], BF16) for b in range(nkv)]
    fT = sb("fT", [128, 2048], F32)
    fsq = sb("fsq", [128, 2048], BF16)
    rstd2 = sb("rstd2", [128, 2048], F32)
    xbf = sb("xbf", [128, 2048], F32)
    lnb = sb("lnb", [128, 512], F32)
    eT = sb("eT", [128, 16 * 512], BF16)
    zb = sb("zb", [128, 512], F32)
    ao_p = sb("ao_p", [128, 512], BF16)
    aop_sb = [wql[b][:, 1024:1536] for b in range(2)]
    if layer == 0:
        qhatT = [sb(f"qhatT{b}", [128, 2048], BF16) for b in range(2)]
        khatT = [sb(f"khatT{b}", [128, 2048], BF16) for b in range(2)]
        v_nat = [sb(f"v{b}", [128, 2048], BF16) for b in range(2)]
        e_nat = sb("e_nat", [128, 2048], BF16)
        cols = sb("cols", [128, 34], F32)
        colsZ = sb("colsZ", [128, 32], F32)
        mrow = sb("mrow", [1, 2048], BF16)
        zrow = sb("zrow", [1, 2048], F32)
        biasb = sb("biasb", [128, 512], F32)
        smtmp = [sb(f"smtmp{b}", [128, 1024], F32) for b in range(2)]
        qs_c = sb("qs_c", [128, 1], F32)
        ks_c = sb("ks_c", [128, 1], F32)
        s2c = sb("s2c", [128, 1], F32)
    else:
        qhatT = [sb("qhatT0", [128, 2048], BF16), wkl[0]]
        khatT = [sb(f"khatT{g}", [128, 2048], BF16) for g in range(4)]
        vT_bf = [sb(f"vT{g}", [128, 2048], BF16) for g in range(4)]
        v_nat = sb("v", [128, 16 * 512], BF16)
        uv_bf = sb("uv_bf", [128, 512], BF16)
        uv_f = sb("uv_f", [128, 512], F32)
        df_f = sb("df_f", [128, 512], F32)
        zln = sb("zln", [128, 512], F32)

    ld = p.sem(tag + "ld")
    gld = [p.sem(tag + f"gld{b}") for b in range(2)]
    gvn = p.sem(tag + "gvn")
    gkv = p.sem(tag + "gkv")
    glw = p.sem(tag + "glw")
    aold2 = [p.sem(tag + f"aold{b}") for b in range(2)]
    xpl = p.sem(tag + "xpl")
    mm = p.sem(tag + "mm")
    se = p.sem(tag + "se")
    ve = p.sem(tag + "ve")
    sto = p.sem(tag + "sto")

    big = [ctx.enter_context(nc.psum_tensor(tag + f"big{i}", [128, 2048], F32))
           for i in range(2)]

    if xg is None:
        for eb in range(E_BLKS):
            p.dma("sync", xnT_sb[:, eb * 2048:(eb + 1) * 2048], xnT_d[eb],
                  incs=[(ld, 16)])
    if layer == 0:
        p.dma("sync", qs_c[:, :], T["l0_qscale"].ap()[:, :], incs=[(ld, 16)])
        p.dma("sync", ks_c[:, :], T["l0_kscale"].ap()[:, :], incs=[(ld, 16)])
        p.emit("vector", lambda e: e.tensor_tensor(s2c[:, :], qs_c[:, :], ks_c[:, :],
                                                   ALU.mult),
               waits=[(ld, None)], incs=[(ve, 1)])

    misc = {"rms_mm": 0}

    def proj_chunk(wsb, pq, sc, wsem, mm_w, fT_free):
        def mmf(e, sc=sc, pq=pq, wsb=wsb):
            for c in range(E_BLKS):
                inst = e.matmul(pq[:, :], wsb[:, c * 128:(c + 1) * 128],
                                xnT_sb[:, c * 2048 + sc * 512: c * 2048 + (sc + 1) * 512],
                                start=(c == 0), stop=(c == E_BLKS - 1))
            return inst
        p.emit("tensor", mmf, waits=[(ld, None), (wsem, None)] + list(mm_w),
               incs=[(mm, 1)])
        mmv_ = mm.n
        fw = [(mm, mmv_)]
        if fT_free:
            fw.append((ve, fT_free))
        p.emit("vector",
               lambda e, sc=sc, pq=pq: e.tensor_copy(fT[:, sc * 512:(sc + 1) * 512],
                                                     pq[:, :]),
               waits=fw, incs=[(ve, 1)])
        return ve.n

    def rms_chunks(pb, eps_col, fT_ve, pb_free, rstd_free_ve, l2=False):
        p.emit("vector", lambda e: e.tensor_tensor(fsq[:, :], fT[:, :], fT[:, :],
                                                   ALU.mult),
               waits=[(ve, fT_ve), (mm, misc["rms_mm"])], incs=[(ve, 1)])
        fsqv = ve.n
        pbr = {}
        ex = {}
        for sc in range(4):
            w = [(ve, fsqv), (C.ready, None)]
            if sc == 0:
                if pb_free:
                    w.append(pb_free)
            else:
                w.append(pbr[sc - 1])
            p.emit("tensor",
                   lambda e, sc=sc, pb=pb: e.matmul(pb, C.ones128_bf[:, :],
                                                    fsq[:, sc * 512:(sc + 1) * 512],
                                                    start=True, stop=True),
                   waits=w, incs=[(mm, 1)])
            misc["rms_mm"] = mm.n
            mmv_ = mm.n
            if l2:
                mw = [(mm, mmv_)]
                if sc:
                    mw.append((se, ex[sc - 1]))
                p.emit("vector",
                       lambda e, pb=pb: e.tensor_scalar(lnb[:, :], pb, C.v(7), None,
                                                        ALU.max),
                       waits=mw, incs=[(ve, 1)])
                pbr[sc] = (ve, ve.n)
                p.emit("scalar", lambda e: e.activation(lnb[:, :], lnb[:, :], AF.Ln),
                       waits=[(ve, ve.n)] + ([(se, ex[sc - 1])] if sc else []),
                       incs=[(se, 1)])
            else:
                mw = [(mm, mmv_)]
                if sc:
                    mw.append((se, ex[sc - 1]))
                p.emit("scalar",
                       lambda e, pb=pb: e.activation(lnb[:, :], pb, AF.Ln,
                                                     bias=eps_col, scale=1.0 / 128.0),
                       waits=mw, incs=[(se, 1)])
                pbr[sc] = (se, se.n)
            lnm = se.n
            rw = [(se, lnm)]
            if sc == 0 and rstd_free_ve:
                rw.append((ve, rstd_free_ve))
            p.emit("scalar",
                   lambda e, sc=sc: e.activation(rstd2[:, sc * 512:(sc + 1) * 512],
                                                 lnb[:, :], AF.Exp, scale=-0.5),
                   waits=rw, incs=[(se, 1)])
            ex[sc] = se.n
        return ex[3]

    kv_marks = {}
    if layer == 1:
        wvn = eT
        for c in range(E_BLKS):
            p.dma("gpsimd", wvn[:, c * 512:(c + 1) * 512], T["l1_wv_n"].ap()[c],
                  incs=[(gvn, 16)])
        pvs = [big[1][:, 0:512], big[1][:, 512:1024]]
        vcp = {}
        for t in range(S_TILES):
            pv = pvs[t & 1]

            def mmv(e, t=t, pv=pv):
                for c in range(E_BLKS):
                    inst = e.matmul(pv[:, :],
                                    xnT_sb[:, c * 2048 + t * 128: c * 2048 + (t + 1) * 128],
                                    wvn[:, c * 512:(c + 1) * 512],
                                    start=(c == 0), stop=(c == E_BLKS - 1))
                return inst
            w = [(ld, None), (gvn, None)]
            if t >= 2:
                w.append((ve, vcp[t - 2]))
            p.emit("tensor", mmv, waits=w, incs=[(mm, 1)])
            mmv_ = mm.n
            p.emit("vector",
                   lambda e, t=t, pv=pv: e.tensor_copy(v_nat[:, t * 512:(t + 1) * 512],
                                                       pv[:, :]),
                   waits=[(mm, mmv_)], incs=[(ve, 1)])
            vcp[t] = ve.n
        kvw_mm = 0
        prev_ve = vcp[S_TILES - 1]
        prev_pk_ve = 0
        pk0 = [big[0][:, 0:512], big[0][:, 512:1024]]
        pb0 = big[0][:, 1536:2048]
        for g in range(4):
            p.dma("gpsimd", wkl[0][:, :],
                  T["l1_wk"].ap()[g].rearrange("p c j -> p (c j)"),
                  waits=[(mm, kvw_mm)], incs=[(gkv, 16)])
            kse = {}
            for sc in range(4):
                mw = ([(ve, kse[sc - 2])] if sc >= 2
                      else ([(ve, prev_pk_ve)] if prev_pk_ve else []))
                kse[sc] = proj_chunk(wkl[0], pk0[sc & 1], sc, gkv, mw, prev_ve)
            krms = rms_chunks(pb0, C.v(4), kse[3], None, prev_ve)
            p.emit("vector",
                   lambda e, g=g: e.scalar_tensor_tensor(khatT[g][:, :], fT[:, :],
                                                         C.v(8), rstd2[:, :],
                                                         ALU.mult, ALU.mult),
                   waits=[(se, krms), (ve, kse[3])], incs=[(ve, 1)])
            khm = ve.n
            kvw_mm = mm.n
            p.dma("gpsimd", wvl[0][:, :],
                  T["l1_wv"].ap()[g].rearrange("p c j -> p (c j)"),
                  waits=[(mm, kvw_mm)], incs=[(gkv, 16)])
            vse = {}
            for sc in range(4):
                mw = [(ve, vse[sc - 2])] if sc >= 2 else [(ve, kse[3])]
                vse[sc] = proj_chunk(wvl[0], pk0[sc & 1], sc, gkv, mw, khm)
            vrms = rms_chunks(pb0, None, vse[3], None, khm, l2=True)
            p.emit("vector",
                   lambda e, g=g: e.tensor_tensor(vT_bf[g][:, :], fT[:, :],
                                                  rstd2[:, :], ALU.mult),
                   waits=[(se, vrms), (ve, vse[3])], incs=[(ve, 1)])
            prev_ve = ve.n
            prev_pk_ve = vse[3]
            kvw_mm = mm.n
        kv_marks["ve"] = prev_ve
        kv_marks["mm"] = kvw_mm

    qhat_ready = {}
    p_se_end = {}
    head_mm_done = {}
    c_end = {}
    rows_marks = {}
    a_marks = {}

    def P_steps(h):
        hb = h & 1
        arena = big[hb]
        pk = [arena[:, 0:512], arena[:, 512:1024]]
        pb = arena[:, 1536:2048]
        ce2 = c_end.get(h - 2, {"ve": 0, "se": 0, "mm": 0})
        fT_free = qhat_ready.get(h - 1, kv_marks.get("ve", 0))
        steps = []
        loc = {}
        if layer == 0:
            def s_dma():
                p.dma("gpsimd", wql[hb][:, :],
                      T["l0_wq"].ap()[h].rearrange("p c j -> p (c j)"),
                      waits=[(mm, head_mm_done.get(h - 2, 0))], incs=[(gld[hb], 16)])
                p.dma("gpsimd", wkl[hb][:, :],
                      T["l0_wk"].ap()[h].rearrange("p c j -> p (c j)"),
                      incs=[(gld[hb], 16)])
                p.dma("gpsimd", wvl[hb][:, :],
                      T["l0_wv"].ap()[h].rearrange("p c j -> p (c j)"),
                      incs=[(gld[hb], 16)])
            steps.append(s_dma)
            kse = {}

            def s_kproj(sc):
                mw = [(ve, kse[sc - 2])] if sc >= 2 else [(ve, ce2["ve"]), (se, ce2["se"])]
                kse[sc] = proj_chunk(wkl[hb], pk[sc & 1], sc, gld[hb], mw, fT_free)
            steps += [lambda sc=sc: s_kproj(sc) for sc in range(4)]

            def s_krms():
                loc["krms"] = rms_chunks(pb, C.v(3), kse[3], (ve, ce2["ve"]), fT_free)
            steps.append(s_krms)

            def s_khat():
                p.emit("vector",
                       lambda e: e.scalar_tensor_tensor(khatT[hb][:, :], fT[:, :],
                                                        s2c[:, :], rstd2[:, :],
                                                        ALU.mult, ALU.mult),
                       waits=[(se, loc["krms"]), (ve, kse[3]), (mm, ce2["mm"])],
                       incs=[(ve, 1)])
                loc["khat"] = ve.n
            steps.append(s_khat)

            def s_vt(t0):
                for t in (t0, t0 + 1, t0 + 2, t0 + 3):
                    pv = pk[t & 1][:, 0:128]

                    def mmv(e, t=t, pv=pv, hb=hb):
                        for c in range(E_BLKS):
                            inst = e.matmul(pv,
                                            xnT_sb[:, c * 2048 + t * 128: c * 2048 + (t + 1) * 128],
                                            wvl[hb][:, c * 128:(c + 1) * 128],
                                            start=(c == 0), stop=(c == E_BLKS - 1))
                        return inst
                    vw = [(gld[hb], None)]
                    if t < 2:
                        vw.append((ve, kse[3]))
                    else:
                        vw.append((ve, loc[("vcp", t - 2)]))
                    p.emit("tensor", mmv, waits=vw, incs=[(mm, 1)])
                    mmv_ = mm.n
                    p.emit("vector",
                           lambda e, t=t, pv=pv, hb=hb: e.tensor_copy(
                               v_nat[hb][:, t * 128:(t + 1) * 128], pv),
                           waits=[(mm, mmv_)], incs=[(ve, 1)])
                    loc[("vcp", t)] = ve.n
            steps += [lambda t0=t0: s_vt(t0) for t0 in range(0, 16, 4)]
            qse = {}

            def s_qproj(sc):
                mw = [(ve, qse[sc - 2])] if sc >= 2 else [(ve, loc[("vcp", 14 + sc)])]
                qse[sc] = proj_chunk(wql[hb], pk[sc & 1], sc, gld[hb], mw, loc["khat"])
            steps += [lambda sc=sc: s_qproj(sc) for sc in range(4)]

            def s_qrms():
                loc["qrms"] = rms_chunks(pb, C.v(3), qse[3], None, loc["khat"])
            steps.append(s_qrms)

            def s_qhat():
                p.emit("vector",
                       lambda e: e.tensor_tensor(qhatT[hb][:, :], fT[:, :], rstd2[:, :],
                                                 ALU.mult),
                       waits=[(se, loc["qrms"]), (ve, qse[3]), (mm, ce2["mm"])],
                       incs=[(ve, 1)])
                qhat_ready[h] = ve.n
                head_mm_done[h] = mm.n
                p_se_end[h] = se.n
            steps.append(s_qhat)
        else:
            def s_dma():
                p.dma("gpsimd", wql[hb][:, :],
                      T["l1_wq"].ap()[h].rearrange("p c j -> p (c j)"),
                      waits=[(mm, head_mm_done.get(h - 2, 0))],
                      incs=[(gld[hb], 16)])
            steps.append(s_dma)
            qse = {}

            def s_qproj(sc):
                mw = [(ve, qse[sc - 2])] if sc >= 2 else [(ve, ce2["ve"]), (se, ce2["se"])]
                qse[sc] = proj_chunk(wql[hb], pk[sc & 1], sc, gld[hb], mw, fT_free)
            steps += [lambda sc=sc: s_qproj(sc) for sc in range(4)]

            def s_qrms():
                loc["qrms"] = rms_chunks(pb, C.v(4), qse[3], (ve, ce2["ve"]), fT_free)
            steps.append(s_qrms)

            def s_qhat():
                p.emit("vector",
                       lambda e: e.tensor_tensor(qhatT[hb][:, :], fT[:, :], rstd2[:, :],
                                                 ALU.mult),
                       waits=[(se, loc["qrms"]), (ve, qse[3]), (mm, ce2["mm"])],
                       incs=[(ve, 1)])
                qhat_ready[h] = ve.n
                head_mm_done[h] = mm.n
                p_se_end[h] = se.n
            steps.append(s_qhat)
        return steps

    def A_steps(h):
        hb = h & 1
        psn = big[hb][:, :]
        stt = {}

        def s_tile(t):
            def mmsn(e, t=t):
                for kc in range(4):
                    inst = e.matmul(psn[:, kc * 512:(kc + 1) * 512],
                                    qhatT[hb][:, t * 128:(t + 1) * 128],
                                    khatT[hb][:, kc * 512:(kc + 1) * 512],
                                    start=True, stop=True)
                return inst
            if t == 0:
                aw = [(ve, qhat_ready[h]), (se, p_se_end[h])]
            else:
                aw = [(ve, stt[("mx", t - 1)]), (se, stt[("ex", t - 1)])]
            p.emit("tensor", mmsn, waits=aw, incs=[(mm, 1)])
            mmv_ = mm.n
            p.emit("vector",
                   lambda e, t=t: e.reduce_max(cols[:, t:t + 1], psn[:, :],
                                               axis=AX.X, negate=True),
                   waits=[(mm, mmv_)], incs=[(ve, 1)])
            stt[("mx", t)] = ve.n
            p.emit("scalar",
                   lambda e, t=t: e.activation(e_nat[:, :], psn[:, :], AF.Exp,
                                               bias=cols[:, t:t + 1],
                                               accum_out=cols[:, 16 + t:17 + t]),
                   waits=[(ve, stt[("mx", t)])], incs=[(se, 1)])
            stt[("ex", t)] = se.n

        def done():
            a_marks[h] = dict(mx15=stt[("mx", 15)], ex15=stt[("ex", 15)],
                              zs15=stt[("ex", 15)])
        return [lambda t=t: s_tile(t) for t in range(S_TILES)] + [done]

    def rows_step(h):
        hb = h & 1
        pmr = big[hb][0:1, 0:128]
        am = a_marks[h]

        def s_rows():
            trcp = 0
            for t in range(S_TILES):
                w = ([(ve, am["mx15"]), (se, am["ex15"]), (C.ld, None)]
                     if t == 0 else [(ve, trcp)])
                p.emit("tensor",
                       lambda e, t=t: e.transpose(pmr[:, :], cols[:, t:t + 1],
                                                  C.idn_f32[:, :]),
                       waits=w, incs=[(mm, 1)])
                mmv_ = mm.n
                p.emit("vector",
                       lambda e, t=t: e.tensor_copy(mrow[0:1, t * 128:(t + 1) * 128],
                                                    pmr[0:1, :]),
                       waits=[(mm, mmv_)], incs=[(ve, 1)])
                trcp = ve.n
            p.emit("vector", lambda e: e.reciprocal(cols[:, 16:32], cols[:, 16:32]),
                   waits=[(se, am["ex15"]), (ve, trcp)], incs=[(ve, 1)])
            trcp = ve.n
            for t in range(S_TILES):
                p.emit("tensor",
                       lambda e, t=t: e.transpose(pmr[:, :], cols[:, 16 + t:17 + t],
                                                  C.idn_f32[:, :]),
                       waits=[(ve, trcp)], incs=[(mm, 1)])
                mmv_ = mm.n
                p.emit("vector",
                       lambda e, t=t: e.tensor_copy(zrow[0:1, t * 128:(t + 1) * 128],
                                                    pmr[0:1, :]),
                       waits=[(mm, mmv_)], incs=[(ve, 1)])
                trcp = ve.n
            rows_marks[h] = trcp
        return [s_rows]

    def C_steps(h):
        hb = h & 1
        arena = big[hb]
        pT = [arena[:, 0:512], arena[:, 512:1024]]
        pav = arena[:, 1024:1536]
        paux = arena[:, 1536:2048]
        st = {}
        if layer == 0:
            am = a_marks[h]
            ce1 = c_end.get(h - 1, {})

            def s_qc(qc):
                bw = ([(ve, rows_marks[h]), (se, am["ex15"])] if qc == 0
                      else [(ve, st[("zbev", qc - 1)])])
                p.emit("tensor",
                       lambda e, qc=qc: e.matmul(paux, C.ones_row_bf[:, :],
                                                 mrow[0:1, qc * 512:(qc + 1) * 512],
                                                 start=True, stop=True),
                       waits=bw, incs=[(mm, 1)])
                mmb = mm.n
                bev = [(mm, mmb)]
                if qc:
                    bev.append((ve, st[("add", qc - 1, 15)]))
                elif ce1:
                    bev.append((ve, ce1["add15"]))
                p.emit("vector", lambda e: e.tensor_copy(biasb[:, :], paux),
                       waits=bev, incs=[(ve, 1)])
                st[("bias", qc)] = ve.n
                for kt in range(S_TILES):
                    b = kt & 1
                    pr = kt // 2
                    if kt >= 2:
                        sw = [(ve, st[("add", qc, kt - 2)])]
                    elif qc == 0:
                        sw = [(ve, am["mx15"]), (se, am["ex15"])]
                    else:
                        sw = [(ve, st[("add", qc - 1, 14 + kt)])]
                    p.emit("tensor",
                           lambda e, kt=kt, qc=qc, b=b: e.matmul(
                               pT[b], khatT[hb][:, kt * 128:(kt + 1) * 128],
                               qhatT[hb][:, qc * 512:(qc + 1) * 512],
                               start=True, stop=True),
                           waits=sw, incs=[(mm, 1)])
                    mms = mm.n
                    aw2 = [(mm, mms), (ve, st[("bias", qc)])]
                    expp = st.get(("exp", qc, pr - 2), (ce1.get("exp15", 0)
                                                       if qc == 0 else
                                                       st.get(("exp", qc - 1, 7 + pr), 0)))
                    if expp:
                        aw2.append((se, expp))
                    p.emit("vector",
                           lambda e, kt=kt, b=b, pr=pr: e.tensor_tensor(
                               smtmp[pr & 1][:, (kt & 1) * 512:(kt & 1) * 512 + 512],
                               pT[b], biasb[:, :], ALU.add),
                           waits=aw2, incs=[(ve, 1)])
                    st[("add", qc, kt)] = ve.n
                    if kt & 1:
                        p.emit("scalar",
                               lambda e, kt=kt, pr=pr: e.activation(
                                   eT[:, (kt - 1) * 512:(kt + 1) * 512],
                                   smtmp[pr & 1][:, :], AF.Exp),
                               waits=[(ve, st[("add", qc, kt)])], incs=[(se, 1)])
                        st[("exp", qc, pr)] = se.n

                def mmav(e, hb=hb):
                    for kt in range(S_TILES):
                        inst = e.matmul(pav, v_nat[hb][:, kt * 128:(kt + 1) * 128],
                                        eT[:, kt * 512:(kt + 1) * 512],
                                        start=(kt == 0), stop=(kt == S_TILES - 1))
                    return inst
                avw = [(se, st[("exp", qc, 7)])]
                aop = st.get(("ao", qc - 1), ce1.get("ao", 0))
                if aop:
                    avw.append((ve, aop))
                p.emit("tensor", mmav, waits=avw, incs=[(mm, 1)])
                st[("avmm", qc)] = mm.n
                p.emit("tensor",
                       lambda e, qc=qc: e.matmul(paux, C.bcast1_f32[:, :],
                                                 zrow[0:1, qc * 512:(qc + 1) * 512],
                                                 start=True, stop=True),
                       waits=[(ve, st[("bias", qc)])], incs=[(mm, 1)])
                mmz = mm.n
                zw = [(mm, mmz)]
                if qc:
                    zw.append((ve, st[("ao", qc - 1)]))
                elif ce1:
                    zw.append((ve, ce1["ao"]))
                p.emit("vector", lambda e: e.tensor_copy(zb[:, :], paux),
                       waits=zw, incs=[(ve, 1)])
                st[("zbev", qc)] = ve.n
                aow = [(mm, st[("avmm", qc)]), (ve, st[("zbev", qc)])]
                dprev = st.get(("dma", qc - 1), ce1.get("dma", 0))
                if dprev:
                    aow.append((sto, dprev))
                p.emit("vector",
                       lambda e: e.tensor_tensor(ao_p[:, :], pav, zb[:, :], ALU.mult),
                       waits=aow, incs=[(ve, 1)])
                st[("ao", qc)] = ve.n
                p.dma("sync", aoT_d[h, :, qc * 512:(qc + 1) * 512], ao_p[:, :],
                      waits=[(ve, st[("ao", qc)])], incs=[(sto, 16)])
                st[("dma", qc)] = sto.n

            def fin():
                c_end[h] = dict(ve=ve.n, se=se.n, mm=mm.n, av_mm=st[("avmm", 3)],
                                ao=st[("ao", 3)], dma=st[("dma", 3)],
                                add15=st[("add", 3, 15)], exp15=st[("exp", 3, 7)])
            return [lambda qc=qc: s_qc(qc) for qc in range(4)] + [fin]
        else:
            g = h // 4
            pzb = paux
            ce1 = c_end.get(h - 1, {})

            def s_qc(qc):
                for kt in range(S_TILES):
                    b = kt & 1
                    if kt >= 2:
                        sw = [(se, st[("exp", qc, kt - 2)])]
                    elif qc == 0:
                        sw = [(ve, qhat_ready[h]), (se, p_se_end[h])]
                        if ce1:
                            sw.append((se, ce1["exp15"]))
                    else:
                        sw = [(se, st[("exp", qc - 1, 14 + kt)])]
                    p.emit("tensor",
                           lambda e, kt=kt, qc=qc, b=b: e.matmul(
                               pT[b], khatT[g][:, kt * 128:(kt + 1) * 128],
                               qhatT[hb][:, qc * 512:(qc + 1) * 512],
                               start=True, stop=True),
                           waits=sw, incs=[(mm, 1)])
                    mms = mm.n
                    p.emit("scalar",
                           lambda e, kt=kt, b=b: e.activation(
                               eT[:, kt * 512:(kt + 1) * 512], pT[b], AF.Exp),
                           waits=[(mm, mms)], incs=[(se, 1)])
                    st[("exp", qc, kt)] = se.n

                def mmav(e, g=g):
                    for kt in range(S_TILES):
                        inst = e.matmul(pav,
                                        v_nat[:, kt * 512 + g * 128: kt * 512 + g * 128 + 128],
                                        eT[:, kt * 512:(kt + 1) * 512],
                                        start=(kt == 0), stop=(kt == S_TILES - 1))
                    return inst
                avw = [(se, st[("exp", qc, 15)])]
                dfp = st.get(("df", qc - 1), ce1.get("df", 0))
                if dfp:
                    avw.append((ve, dfp))
                p.emit("tensor", mmav, waits=avw, incs=[(mm, 1)])
                st[("avmm", qc)] = mm.n

                def mmzs(e):
                    for kt in range(S_TILES):
                        inst = e.matmul(pzb, C.ones128_bf[:, :],
                                        eT[:, kt * 512:(kt + 1) * 512],
                                        start=(kt == 0), stop=(kt == S_TILES - 1))
                    return inst
                zw = [(se, st[("exp", qc, 15)])]
                t2p = st.get(("tmp2", qc - 1), ce1.get("tmp2", 0))
                if t2p:
                    zw.append((ve, t2p))
                p.emit("tensor", mmzs, waits=zw, incs=[(mm, 1)])
                st[("zmm", qc)] = mm.n
                zlw = [(mm, st[("zmm", qc)])]
                p.emit("scalar", lambda e: e.activation(zln[:, :], pzb, AF.Ln),
                       waits=zlw, incs=[(se, 1)])
                st[("zln", qc)] = se.n
                zbw = [(se, st[("zln", qc)])]
                aop = st.get(("ao", qc - 1), ce1.get("ao", 0))
                if aop:
                    zbw.append((ve, aop))
                p.emit("scalar", lambda e: e.activation(zb[:, :], zln[:, :], AF.Exp,
                                                        scale=-1.0),
                       waits=zbw, incs=[(se, 1)])
                st[("zb", qc)] = se.n
                uvw = [(mm, st[("zmm", qc)])]
                p.emit("vector",
                       lambda e, qc=qc, g=g: e.tensor_tensor(
                           uv_bf[:, :], pav, vT_bf[g][:, qc * 512:(qc + 1) * 512],
                           ALU.mult),
                       waits=uvw, incs=[(ve, 1)])
                st[("uv", qc)] = ve.n
                p.emit("tensor",
                       lambda e: e.matmul(pzb, C.ones128_bf[:, :], uv_bf[:, :],
                                          start=True, stop=True),
                       waits=[(ve, st[("uv", qc)]), (se, st[("zln", qc)])],
                       incs=[(mm, 1)])
                mpd = mm.n
                p.emit("vector",
                       lambda e, qc=qc, g=g: e.tensor_tensor(
                           uv_f[:, :], pzb, vT_bf[g][:, qc * 512:(qc + 1) * 512],
                           ALU.mult),
                       waits=[(mm, mpd)], incs=[(ve, 1)])
                st[("tmp2", qc)] = ve.n
                p.emit("vector",
                       lambda e: e.tensor_tensor(df_f[:, :], pav, uv_f[:, :],
                                                 ALU.subtract),
                       waits=[(ve, st[("tmp2", qc)])], incs=[(ve, 1)])
                st[("df", qc)] = ve.n
                aow = [(ve, st[("df", qc)]), (se, st[("zb", qc)])]
                dprev = st.get(("dma", qc - 1), ce1.get("dma", 0))
                if dprev:
                    aow.append((sto, dprev))
                p.emit("vector",
                       lambda e: e.tensor_tensor(ao_p[:, :], df_f[:, :], zb[:, :],
                                                 ALU.mult),
                       waits=aow, incs=[(ve, 1)])
                st[("ao", qc)] = ve.n
                p.dma("sync", aoT_d[h, :, qc * 512:(qc + 1) * 512], ao_p[:, :],
                      waits=[(ve, st[("ao", qc)])], incs=[(sto, 16)])
                st[("dma", qc)] = sto.n

            def fin():
                c_end[h] = dict(ve=ve.n, se=se.n, mm=mm.n, av_mm=st[("zmm", 3)],
                                df=st[("df", 3)], tmp2=st[("tmp2", 3)],
                                ao=st[("ao", 3)], dma=st[("dma", 3)],
                                exp15=st[("exp", 3, 15)])
            return [lambda qc=qc: s_qc(qc) for qc in range(4)] + [fin]

    for f in P_steps(0):
        f()

    for h in range(NH):
        pp = P_steps(h + 1) if h + 1 < NH else []
        ac = A_steps(h) if layer == 0 else []
        na, np_ = len(ac), len(pp)
        ppi = 0
        for i, f in enumerate(ac):
            f()
            want = (i + 1) * np_ // max(na, 1) if na else 0
            while ppi < min(want, np_ // 2):
                pp[ppi]()
                ppi += 1
        if layer == 0:
            for f in rows_step(h):
                f()
        cc = C_steps(h)
        nc_, np2 = len(cc), len(pp) - ppi
        for i, f in enumerate(cc):
            f()
            want = ppi + (i + 1) * np2 // max(nc_, 1)
            while ppi < min(want, len(pp)):
                pp[ppi]()
                ppi += 1
        while ppi < len(pp):
            pp[ppi]()
            ppi += 1

    woall = xnT_sb[:, 0:16 * 1024]
    xb4 = [fT[:, 0:1024], fT[:, 1024:2048], xbf[:, 0:1024], xbf[:, 1024:2048]]
    xo2 = [rstd2[:, 0:1024], rstd2[:, 1024:2048]]
    wo_d = T["l0_wo"].ap() if layer == 0 else T["l1_wo"].ap()
    ce = c_end[NH - 1]
    pw = [big[i // 4][:, (i % 4) * 512:(i % 4 + 1) * 512] for i in range(8)]
    xnT_free = head_mm_done[NH - 1]

    glw_marks = {}
    woldw = {}
    gstop = {}
    evd = {}
    xplm = {}
    outd = {}
    gc = 0
    for i in range(4):
        p.dma("scalar", xb4[i], xbase_d[i, :, 0:1024],
              waits=([(ve, qhat_ready[NH - 1])] if i == 0 else []), incs=[(xpl, 16)])
    xplm[(0, 0)] = xpl.n
    groups = [(eh, tg) for eh in range(2) for tg in range(4)]
    for eh in range(2):
        for c in range(16):
            w = []
            if c == 0:
                w = [(mm, xnT_free)] if eh == 0 else [(mm, gstop[(0, 3)])]
            p.dma("gpsimd", woall[:, c * 1024:(c + 1) * 1024],
                  wo_d[c, :, eh * 1024:(eh + 1) * 1024], waits=w, incs=[(glw, 16)])
            glw_marks[(eh, c)] = glw.n
        for tg in range(4):
            for c in range(16):
                wb = gc & 1
                ww = [(mm, woldw[gc - 2])] if gc >= 2 else [(mm, xnT_free)]
                p.dma("sync", aop_sb[wb], aoT_d[c, :, tg * 512:(tg + 1) * 512],
                      waits=ww, incs=[(aold2[wb], 16)])

                def mmw(e, c=c, wb=wb):
                    for i in range(4):
                        for ec in range(2):
                            inst = e.matmul(pw[i * 2 + ec],
                                            aop_sb[wb][:, i * 128:(i + 1) * 128],
                                            woall[:, c * 1024 + ec * 512: c * 1024 + (ec + 1) * 512],
                                            start=(c == 0), stop=(c == 15))
                    return inst
                mw = [(aold2[wb], None), (glw, glw_marks[(eh, c)])]
                if c == 0:
                    gi = groups.index((eh, tg))
                    if gi == 0:
                        mw += [(ve, ce["ve"]), (se, ce["se"])]
                    else:
                        mw.append((ve, evd[groups[gi - 1]]))
                p.emit("tensor", mmw, waits=mw, incs=[(mm, 1)])
                woldw[gc] = mm.n
                gc += 1
            gstop[(eh, tg)] = mm.n
            gi = groups.index((eh, tg))
            vem = 0
            for i in range(4):
                t = tg * 4 + i
                for ec in range(2):
                    aw = [(mm, gstop[(eh, tg)]), (xpl, xplm[(eh, tg)])]
                    od = outd.get((eh, tg, i - 2), 0)
                    if od:
                        aw.append((sto, od))
                    p.emit("vector",
                           lambda e, i=i, ec=ec: e.tensor_tensor(
                               xo2[i & 1][:, ec * 512:(ec + 1) * 512],
                               xb4[i][:, ec * 512:(ec + 1) * 512],
                               pw[i * 2 + ec], ALU.add),
                           waits=aw, incs=[(ve, 1)])
                    vem = ve.n
                p.dma("sync", xout_d[t, :, eh * 1024:(eh + 1) * 1024], xo2[i & 1],
                      waits=[(ve, vem)], incs=[(sto, 16)])
                outd[(eh, tg, i)] = sto.n
            evd[(eh, tg)] = vem
            if gi + 1 < len(groups):
                neh, ntg = groups[gi + 1]
                for i in range(4):
                    p.dma("scalar", xb4[i], xbase_d[ntg * 4 + i, :, neh * 1024:(neh + 1) * 1024],
                          waits=([(ve, vem)] if i == 0 else []), incs=[(xpl, 16)])
                xplm[(neh, ntg)] = xpl.n
    return sto






def _mk_dram(nc, specs):
    T = {}
    for name, shape, dt, kind in specs:
        T[name] = nc.dram_tensor(name, list(shape), dt, kind=kind)
    return T


_FFN_L0_W = [
    ("l0_wg", (64, 128, 16, 128), I8, "ExternalInput"),
    ("l0_wu", (64, 128, 16, 128), I8, "ExternalInput"),
    ("l0_wd", (64, 128, 2048), I8, "ExternalInput"),
    ("l0_sg", (128, 64), F32, "ExternalInput"),
    ("l0_su", (128, 64), F32, "ExternalInput"),
    ("l0_sdown", (1, 2048), F32, "ExternalInput"),
]
_FFN_L1_W = [
    ("l1_w1", (64, 128, 16, 128), F32, "ExternalInput"),
    ("l1_w3", (64, 128, 16, 128), F32, "ExternalInput"),
    ("l1_w2t", (64, 128, 2048), F32, "ExternalInput"),
    ("h128", (128, 128), BF16, "ExternalInput"),
    ("h2x64", (128, 128), BF16, "ExternalInput"),
    ("l1_w2a", (8192, 2048), BF16, "Internal"),
    ("l1_w2pp", (64, 128, 2048), BF16, "Internal"),
]


_ATTN_L0_W = [
    ("l0_wq", (16, 128, 16, 128), F32, "ExternalInput"),
    ("l0_wk", (16, 128, 16, 128), F32, "ExternalInput"),
    ("l0_wv", (16, 128, 16, 128), F32, "ExternalInput"),
    ("l0_wo", (16, 128, 2048), F32, "ExternalInput"),
    ("l0_qscale", (128, 1), F32, "ExternalInput"),
    ("l0_kscale", (128, 1), F32, "ExternalInput"),
]
_ATTN_L1_W = [
    ("l1_wq", (16, 128, 16, 128), F32, "ExternalInput"),
    ("l1_wk", (4, 128, 16, 128), F32, "ExternalInput"),
    ("l1_wv", (4, 128, 16, 128), F32, "ExternalInput"),
    ("l1_wv_n", (16, 128, 512), F32, "ExternalInput"),
    ("l1_wo", (16, 128, 2048), F32, "ExternalInput"),
]


def build_attn_nc(layer, dbg=False):
    nc = bass.Bass("TRN2", target_bir_lowering=False)
    specs = [
        ("xnT", (16, 128, 2048), BF16, "ExternalInput"),
        ("x_base", (16, 128, 2048), F32, "ExternalInput"),
        ("idn_bf", (128, 128), BF16, "ExternalInput"),
        ("idn_f32", (128, 128), F32, "ExternalInput"),
        ("x_out", (16, 128, 2048), F32, "ExternalOutput"),
        ("aoT", (16, 128, 2048), BF16, "ExternalOutput" if dbg else "Internal"),
    ]
    if dbg:
        specs += [
            ("d_qhatT", (128, 2048), BF16, "ExternalOutput"),
            ("d_fT", (128, 2048), F32, "ExternalOutput"),
            ("d_wql", (128, 2048), BF16, "ExternalOutput"),
            ("d_xnT0", (128, 2048), BF16, "ExternalOutput"),
            ("d_rstd2", (128, 2048), F32, "ExternalOutput"),
            ("d_khatT", (128, 2048), BF16, "ExternalOutput"),
            ("d_v", (128, 2048), BF16, "ExternalOutput"),
            ("d_cols", (128, 34), F32, "ExternalOutput"),
            ("d_mrow", (1, 2048), BF16, "ExternalOutput"),
            ("d_zrow", (1, 2048), F32, "ExternalOutput"),
            ("d_eT", (128, 8192), BF16, "ExternalOutput"),
        ]
    specs += _ATTN_L0_W if layer == 0 else _ATTN_L1_W
    with ExitStack() as ctx:
        T = _mk_dram(nc, specs)
        p = Prog(nc, ctx)
        C = Consts(nc, ctx, p, T)
        sto = emit_attn(p, C, layer, T, tag=f"a{layer}_", dbg=dbg)
        p.emit("sync", lambda e: None, waits=[(sto, None)])
        p.finalize()
    return nc


def build_ffn_nc(layer):
    nc = bass.Bass("TRN2", target_bir_lowering=False)
    specs = [
        ("x_in", (16, 128, 2048), F32, "ExternalInput"),
        ("idn_bf", (128, 128), BF16, "ExternalInput"),
        ("idn_f32", (128, 128), F32, "ExternalInput"),
        ("xnT", (16, 128, 2048), BF16, "ExternalOutput"),
        ("x_acc", (16, 128, 2048), F32, "ExternalOutput"),
    ]
    specs += _FFN_L0_W if layer == 0 else _FFN_L1_W
    with ExitStack() as ctx:
        T = _mk_dram(nc, specs)
        p = Prog(nc, ctx)
        C = Consts(nc, ctx, p, T)
        if layer == 1:
            w2done = emit_w2pp(p, C, T)
            for eng in ENGINES:
                p.emit(eng, lambda e: None, waits=[(w2done, None)])
        sto = emit_ffn(p, C, layer, T, tag=f"f{layer}_")
        p.emit("sync", lambda e: None, waits=[(sto, None)])
        p.finalize()
    return nc


def build_fused_nc():
    nc = bass.Bass("TRN2", target_bir_lowering=False)
    specs = [
        ("x_in", (16, 128, 2048), F32, "ExternalInput"),
        ("idn_bf", (128, 128), BF16, "ExternalInput"),
        ("idn_f32", (128, 128), F32, "ExternalInput"),
        ("h128", (128, 128), BF16, "ExternalInput"),
        ("h2x64", (128, 128), BF16, "ExternalInput"),
        ("out", (16, 128, 2048), F32, "ExternalOutput"),
        ("xnT_s", (16, 128, 2048), BF16, "Internal"),
        ("xacc_s", (16, 128, 2048), F32, "Internal"),
        ("x1_s", (16, 128, 2048), F32, "Internal"),
        ("aoT_s", (16, 128, 2048), BF16, "Internal"),
        ("l1_w2a", (8192, 2048), BF16, "Internal"),
        ("l1_w2pp", (64, 128, 2048), BF16, "Internal"),
    ]
    specs += _FFN_L0_W + _ATTN_L0_W
    specs += [t for t in _FFN_L1_W if t[0] in ("l1_w1", "l1_w3", "l1_w2t")]
    specs += _ATTN_L1_W
    with ExitStack() as ctx:
        T = _mk_dram(nc, specs)
        p = Prog(nc, ctx)
        C = Consts(nc, ctx, p, T)
        xg = ctx.enter_context(nc.sbuf_tensor("xnT_glob", [128, E_BLKS * 2048], BF16))

        def phase(emitter):
            with ExitStack() as pc:
                p.alloc_ctx = pc
                emitter()
            p.alloc_ctx = ctx
            p.barrier()

        Tf0 = dict(T, x_in=T["x_in"], xnT=T["xnT_s"], x_acc=T["xacc_s"])
        Ta0 = dict(T, xnT=T["xnT_s"], x_base=T["xacc_s"], x_out=T["x1_s"], aoT=T["aoT_s"])
        Tf1 = dict(T, x_in=T["x1_s"], xnT=T["xnT_s"], x_acc=T["xacc_s"])
        Ta1 = dict(T, xnT=T["xnT_s"], x_base=T["xacc_s"], x_out=T["out"], aoT=T["aoT_s"])

        phase(lambda: emit_w2pp(p, C, T))
        phase(lambda: emit_ffn(p, C, 0, Tf0, tag="f0_", xg=xg))
        phase(lambda: emit_attn(p, C, 0, Ta0, tag="a0_", xg=xg))
        phase(lambda: emit_ffn(p, C, 1, Tf1, tag="f1_", xg=xg))
        phase(lambda: emit_attn(p, C, 1, Ta1, tag="a1_", xg=xg))
        p.finalize()
    return nc


_BUILD_CACHE = {}


def _get_nc(key, builder):
    if key not in _BUILD_CACHE:
        _BUILD_CACHE[key] = builder()
    return _BUILD_CACHE[key]


def kernel_fused(**inputs):
    w = prep_weights(inputs)
    c = host_consts()
    x = inputs["L0_attn_x"]
    nc = _get_nc("fused", build_fused_nc)
    base = {"idn_bf": c["idn_bf"], "idn_f32": c["idn_f32"],
            "h128": c["h128"], "h2x64": c["h2x64"]}
    for k in ("l0_wg", "l0_wu", "l0_wd", "l0_sg", "l0_su", "l0_sdown",
              "l0_wq", "l0_wk", "l0_wv", "l0_wo", "l0_qscale", "l0_kscale",
              "l1_w1", "l1_w3", "l1_w2t",
              "l1_wq", "l1_wk", "l1_wv", "l1_wv_n", "l1_wo"):
        base[k] = w[k]
    maps = [dict(base, x_in=np.ascontiguousarray(x[i].reshape(16, 128, 2048)))
            for i in range(NCORES)]
    res = run_bass_kernel_spmd(nc, maps, core_ids=list(range(NCORES)))
    out = np.empty((NCORES, 2048, 2048), dtype=np.float32)
    for i in range(NCORES):
        out[i] = res.results[i]["out"].reshape(2048, 2048)
    return out


def kernel(**inputs):
    return kernel_fused(**inputs)


def kernel_unfused(**inputs):
    w = prep_weights(inputs)
    c = host_consts()
    x = inputs["L0_attn_x"]
    consts = {"idn_bf": c["idn_bf"], "idn_f32": c["idn_f32"]}

    def shard(arr):
        return [np.ascontiguousarray(arr[cid].reshape(16, 128, 2048))
                for cid in range(NCORES)]

    core_ids = list(range(NCORES))

    nc = _get_nc("ffn0", lambda: build_ffn_nc(0))
    base = dict(consts)
    for k in ("l0_wg", "l0_wu", "l0_wd", "l0_sg", "l0_su", "l0_sdown"):
        base[k] = w[k]
    xs = shard(x)
    maps = [dict(base, x_in=xs[i]) for i in range(NCORES)]
    res = run_bass_kernel_spmd(nc, maps, core_ids=core_ids)
    xnT = [r["xnT"] for r in res.results]
    xacc = [r["x_acc"] for r in res.results]

    nc = _get_nc("attn0", lambda: build_attn_nc(0))
    base = dict(consts)
    for k in ("l0_wq", "l0_wk", "l0_wv", "l0_wo", "l0_qscale", "l0_kscale"):
        base[k] = w[k]
    maps = [dict(base, xnT=xnT[i], x_base=xacc[i]) for i in range(NCORES)]
    res = run_bass_kernel_spmd(nc, maps, core_ids=core_ids)
    x1 = [r["x_out"] for r in res.results]

    nc = _get_nc("ffn1", lambda: build_ffn_nc(1))
    base = dict(consts)
    for k in ("l1_w1", "l1_w3", "l1_w2t"):
        base[k] = w[k]
    base["h128"] = c["h128"]
    base["h2x64"] = c["h2x64"]
    maps = [dict(base, x_in=x1[i]) for i in range(NCORES)]
    res = run_bass_kernel_spmd(nc, maps, core_ids=core_ids)
    xnT = [r["xnT"] for r in res.results]
    xacc = [r["x_acc"] for r in res.results]

    nc = _get_nc("attn1", lambda: build_attn_nc(1))
    base = dict(consts)
    for k in ("l1_wq", "l1_wk", "l1_wv", "l1_wv_n", "l1_wo"):
        base[k] = w[k]
    maps = [dict(base, xnT=xnT[i], x_base=xacc[i]) for i in range(NCORES)]
    res = run_bass_kernel_spmd(nc, maps, core_ids=core_ids)

    out = np.empty((NCORES, 2048, 2048), dtype=np.float32)
    for i in range(NCORES):
        out[i] = res.results[i]["x_out"].reshape(2048, 2048)
    return out


# version 24
# speedup vs baseline: 1.1843x; 1.0466x over previous
import numpy as np
import ml_dtypes
from contextlib import ExitStack

import concourse.bass as bass
import concourse.mybir as mybir
from concourse.bass_utils import run_bass_kernel_spmd

F32 = mybir.dt.float32
BF16 = mybir.dt.bfloat16
I8 = mybir.dt.int8
AF = mybir.ActivationFunctionType
ALU = mybir.AluOpType
AX = mybir.AxisListType

NCORES = 8
S_TILES = 16
E_BLKS = 16
O_TILES = 64
CHUNK = 512
N_CHUNKS = 4

RMS_EPS_L0 = 1e-6
RMS_EPS_L1 = float(np.finfo(np.float32).eps)
LN_EPS = 1e-5
FWHT_SCALE = float(1.0 / np.sqrt(8192.0))
SM_SCALE_L1 = float(128.0 ** -0.5)

ENGINES = ("sync", "gpsimd", "scalar", "vector", "tensor")


class Sem:
    def __init__(self, nc, ctx, name):
        self.h = ctx.enter_context(nc.semaphore(name))
        self.n = 0


class Prog:

    def __init__(self, nc, ctx):
        self.nc = nc
        self.ctx = ctx
        self.alloc_ctx = ctx
        self.q = {e: [] for e in ENGINES}
        self._sems = {}

    def barrier(self):
        waits = [(s, None) for s in self._sems.values() if s.n > 0]
        for eng in ENGINES:
            self.emit(eng, lambda e: None, waits=waits)

    def sem(self, name):
        if name not in self._sems:
            self._sems[name] = Sem(self.nc, self.ctx, name)
        return self._sems[name]

    def emit(self, eng, fn, waits=(), incs=()):
        wlist = [(s.h, (s.n if v is None else v)) for (s, v) in waits]
        ilist = []
        for s, k in incs:
            s.n += k
            ilist.append((s.h, k))

        def run(e):
            for h, v in wlist:
                if v > 0:
                    e.wait_ge(h, v)
            inst = fn(e)
            for h, k in ilist:
                assert inst is not None
                inst = inst.then_inc(h, k)
        self.q[eng].append(run)

    def dma(self, eng, out_ap, in_ap, waits=(), incs=()):
        self.emit(eng, lambda e: e.dma_start(out_ap, in_ap), waits, incs)

    def finalize(self):
        with self.nc.Block() as b:
            for name in ENGINES:
                fns = self.q[name]
                if not fns:
                    continue
                getattr(b, name)(lambda e, fns=fns: [f(e) for f in fns])


def _c(a):
    return np.ascontiguousarray(a)


def prep_weights(ins):
    w = {}
    for nm in ("wq", "wk", "wv"):
        a = ins[f"L0_attn_{nm}"]
        w[f"l0_{nm}"] = _c(a.reshape(16, 128, 16, 128).transpose(2, 1, 0, 3))
    w["l0_wo"] = _c(ins["L0_attn_wo"].reshape(16, 128, 2048))
    w["l0_qscale"] = _c(ins["L0_attn_q_scale"].reshape(128, 1))
    w["l0_kscale"] = _c(ins["L0_attn_k_scale"].reshape(128, 1))
    for nm, key in (("wg", "L0_ffn_w_gate"), ("wu", "L0_ffn_w_up")):
        a = ins[key]
        w[f"l0_{nm}"] = _c(a.reshape(16, 128, 64, 128).transpose(2, 1, 0, 3))
    w["l0_wd"] = _c(ins["L0_ffn_w_down"].reshape(64, 128, 2048))
    w["l0_sg"] = _c(ins["L0_ffn_s_gate"].reshape(64, 128).T)
    w["l0_su"] = _c(ins["L0_ffn_s_up"].reshape(64, 128).T)
    w["l0_sdown"] = _c(ins["L0_ffn_s_down"].reshape(1, 2048))
    a = ins["L1_attn_Wq"]
    w["l1_wq"] = _c(a.reshape(16, 128, 16, 128).transpose(0, 3, 2, 1))
    for nm in ("Wk", "Wv"):
        a = ins[f"L1_attn_{nm}"]
        w[f"l1_{nm.lower()}"] = _c(a.reshape(4, 128, 16, 128).transpose(0, 3, 2, 1))
    w["l1_wv_n"] = _c(ins["L1_attn_Wv"].T.reshape(16, 128, 512))
    w["l1_wo"] = _c(ins["L1_attn_Wo"].T.reshape(16, 128, 2048))
    for nm, key in (("w1", "L1_ffn_W1"), ("w3", "L1_ffn_W3")):
        a = ins[key]
        w[f"l1_{nm}"] = _c(a.reshape(64, 128, 16, 128).transpose(0, 3, 2, 1))
    w["l1_w2t"] = _c(ins["L1_ffn_W2"].T.reshape(64, 128, 2048))
    return w


def host_consts():
    c = {}
    c["idn_bf"] = np.eye(128, dtype=ml_dtypes.bfloat16)
    c["idn_f32"] = np.eye(128, dtype=np.float32)

    def had(n):
        h = np.array([[1.0]])
        while h.shape[0] < n:
            h = np.block([[h, h], [h, -h]])
        return h
    c["h128"] = had(128).astype(ml_dtypes.bfloat16)
    h64 = had(64)
    z = np.zeros((64, 64))
    c["h2x64"] = np.block([[h64, z], [z, h64]]).astype(ml_dtypes.bfloat16)
    return c


class Consts:
    def __init__(self, nc, ctx, p, T):
        sb = lambda name, shape, dt: ctx.enter_context(nc.sbuf_tensor(name, shape, dt))
        self.idn_bf = sb("c_idn_bf", [128, 128], BF16)
        self.idn_f32 = sb("c_idn_f32", [128, 128], F32)
        self.bcast1_f32 = sb("c_bcast1", [1, 128], F32)
        self.ones_row_bf = sb("c_ones_row_bf", [1, 128], BF16)
        self.red1_f32 = sb("c_red1_f", [128, 1], F32)
        self.red1_bf = sb("c_red1_b", [128, 1], BF16)
        self.ones128_f32 = sb("c_ones128f", [128, 128], F32)
        self.ones128_bf = sb("c_ones128b", [128, 128], BF16)
        self.vals = sb("c_vals", [128, 12], F32)
        ld = p.sem("c_ld")
        p.dma("sync", self.idn_bf[:, :], T["idn_bf"].ap()[:, :], incs=[(ld, 16)])
        p.dma("sync", self.idn_f32[:, :], T["idn_f32"].ap()[:, :], incs=[(ld, 16)])
        g = p.sem("c_set")
        for col, v in ((0, -1.0 / 2048.0), (1, 1.0 / 2048.0), (2, LN_EPS),
                       (3, RMS_EPS_L0), (4, RMS_EPS_L1), (5, FWHT_SCALE),
                       (6, 1.0 / 128.0), (7, 1e-24), (8, SM_SCALE_L1)):
            p.emit("vector", lambda e, col=col, v=v: e.memset(self.vals[:, col:col + 1], v),
                   incs=[(g, 1)])
        for t in (self.bcast1_f32, self.ones_row_bf, self.red1_f32, self.red1_bf,
                  self.ones128_f32, self.ones128_bf):
            p.emit("vector", lambda e, t=t: e.memset(t[:, :], 1.0), incs=[(g, 1)])
        self.ld = ld
        self.ready = g

    def v(self, col):
        return self.vals[:, col:col + 1]

    def v1(self, col):
        return self.vals[0:1, col:col + 1]



def emit_ffn(p, C, layer, T, tag, xg=None):
    nc = p.nc
    ctx = p.alloc_ctx
    sb = lambda name, shape, dt: ctx.enter_context(nc.sbuf_tensor(tag + name, shape, dt))

    x_in = T["x_in"].ap()
    xnT_d = T["xnT"].ap()
    xacc_d = T["x_acc"].ap()
    if layer == 0:
        wg_d, wu_d, wd_d = T["l0_wg"].ap(), T["l0_wu"].ap(), T["l0_wd"].ap()
    else:
        wg_d, wu_d, wd_d = T["l1_w1"].ap(), T["l1_w3"].ap(), T["l1_w2pp"].ap()

    xbuf = sb("xbuf", [128, 2048], F32)
    xsq = sb("xsq", [128, 1024], F32)
    st = sb("stats", [128, 12], F32)
    xn_bf = sb("xn_bf", [128, 2048], BF16)
    xnT_sb = xg if xg is not None else sb("xnT_sb", [128, E_BLKS * CHUNK], BF16)
    XS = 2048 if xg is not None else CHUNK
    hidT = sb("hidT", [128, O_TILES * CHUNK], BF16)
    wgl = [sb(f"wgl{b}", [128, 2048], BF16) for b in range(2)]
    wul = [sb(f"wul{b}", [128, 2048], BF16) for b in range(2)]
    wdp = [sb(f"wdp{b}", [128, 4096], BF16) for b in range(2)]
    silu_s = sb("silu_s", [128, CHUNK], F32)
    up_s = sb("up_s", [128, CHUNK], F32)
    xacc = [sb(f"xacc{i}", [128, 1024], F32) for i in range(4)]
    if layer == 0:
        sg_sb = sb("sg", [128, 64], F32)
        su_sb = sb("su", [128, 64], F32)
        sdb = sb("sdb", [128, 2048], F32)

    xld = p.sem(tag + "xld")
    ald = p.sem(tag + "ald")
    wld = p.sem(tag + "wld")
    ggu = [p.sem(tag + f"ggu{b}") for b in range(2)]
    gwd = [p.sem(tag + f"gwd{b}") for b in range(2)]
    lnv = p.sem(tag + "lnv")
    trp = p.sem(tag + "trp")
    xnv = p.sem(tag + "xnv")
    mmg = p.sem(tag + "mmg")
    ev = p.sem(tag + "ev")
    mmd = p.sem(tag + "mmd")
    dev = p.sem(tag + "dev")
    sto = p.sem(tag + "sto")

    if layer == 0:
        p.dma("sync", sg_sb[:, :], T["l0_sg"].ap()[:, :], incs=[(wld, 16)])
        p.dma("sync", su_sb[:, :], T["l0_su"].ap()[:, :], incs=[(wld, 16)])
        p.dma("sync", sdb[0:1, :], T["l0_sdown"].ap()[:, :], incs=[(wld, 16)])
        with nc.psum_tensor(tag + "pbc", [128, 512], F32) as pbc:
            for i in range(4):
                p.emit("tensor",
                       lambda e, i=i, pbc=pbc: e.matmul(pbc[:, :], C.bcast1_f32[:, :],
                                               sdb[0:1, i * 512:(i + 1) * 512],
                                               start=True, stop=True),
                       waits=[(wld, None), (C.ready, None), (ev, None)], incs=[(mmg, 1)])
                p.emit("vector",
                       lambda e, i=i, pbc=pbc: e.tensor_copy(sdb[:, i * 512:(i + 1) * 512], pbc[:, :]),
                       waits=[(mmg, None)], incs=[(ev, 1)])

    gu_ev_done = {}
    gu_mm_done = {}
    dq_mm_done = {}
    for q in range(N_CHUNKS):
        tiles = [q * 4 + i for i in range(4)]
        with nc.psum_tensor(f"{tag}ptr{q}", [128, 512], BF16) as ptr, \
             nc.psum_tensor(f"{tag}pg{q}a", [128, 512], F32) as pg0, \
             nc.psum_tensor(f"{tag}pg{q}b", [128, 512], F32) as pg1, \
             nc.psum_tensor(f"{tag}pu{q}a", [128, 512], F32) as pu0, \
             nc.psum_tensor(f"{tag}pu{q}b", [128, 512], F32) as pu1:
            pg = [pg0, pg1]
            pu = [pu0, pu1]
            for i, t in enumerate(tiles):
                p.dma("sync", xbuf[:, :], x_in[t], waits=[(lnv, None)], incs=[(xld, 16)])
                p.emit("vector", lambda e: e.reduce_sum(st[:, 0:1], xbuf[:, :], axis=AX.X),
                       waits=[(xld, None)], incs=[(lnv, 1)])
                p.emit("scalar", lambda e: e.activation(xsq[:, :], xbuf[:, 0:1024],
                                                        AF.Square,
                                                        accum_out=st[:, 8:9]),
                       waits=[(xld, None), (lnv, None)], incs=[(lnv, 1)])
                p.emit("scalar", lambda e: e.activation(xsq[:, :], xbuf[:, 1024:2048],
                                                        AF.Square,
                                                        accum_out=st[:, 9:10]),
                       waits=[(lnv, None)], incs=[(lnv, 1)])
                p.emit("vector", lambda e: e.tensor_tensor(st[:, 1:2], st[:, 8:9],
                                                           st[:, 9:10], ALU.add),
                       waits=[(lnv, None)], incs=[(lnv, 1)])
                p.emit("vector", lambda e: e.tensor_scalar(st[:, 2:3], st[:, 0:1],
                                                           C.v(0), None, ALU.mult),
                       waits=[(lnv, None)], incs=[(lnv, 1)])
                p.emit("vector", lambda e: e.tensor_tensor(st[:, 3:4], st[:, 2:3], st[:, 2:3],
                                                           ALU.mult),
                       waits=[(lnv, None)], incs=[(lnv, 1)])
                p.emit("vector", lambda e: e.scalar_tensor_tensor(st[:, 4:5], st[:, 1:2],
                                                                  C.v(1), st[:, 3:4],
                                                                  ALU.mult, ALU.subtract),
                       waits=[(lnv, None)], incs=[(lnv, 1)])
                p.emit("scalar", lambda e: e.activation(st[:, 5:6], st[:, 4:5], AF.Ln,
                                                        bias=C.v(2)),
                       waits=[(lnv, None)], incs=[(lnv, 1)])
                p.emit("scalar", lambda e: e.activation(st[:, 6:7], st[:, 5:6], AF.Exp,
                                                        scale=-0.5),
                       waits=[(lnv, None)], incs=[(lnv, 1)])
                p.emit("vector", lambda e: e.tensor_scalar(xn_bf[:, :], xbuf[:, :],
                                                           st[:, 2:3], st[:, 6:7],
                                                           ALU.add, ALU.mult),
                       waits=[(lnv, None), (trp, None)], incs=[(lnv, 1)])
                for gblk in range(4):
                    for j in range(4):
                        eb = gblk * 4 + j
                        w = [(lnv, None), (C.ld, None), (xnv, None), (dev, None),
                             (ev, None)] if j == 0 else []
                        p.emit("tensor",
                               lambda e, eb=eb, j=j, ptr=ptr: e.transpose(
                                   ptr[:, j * 128:(j + 1) * 128],
                                   xn_bf[:, eb * 128:(eb + 1) * 128],
                                   C.idn_bf[:, :]),
                               waits=w, incs=([(trp, 1)] if j == 3 else []))
                    dst = xnT_sb[:, :].rearrange("p (eb s) -> p eb s", eb=E_BLKS)
                    co = (q * CHUNK if xg is not None else 0) + i * 128
                    cw = [(trp, None)] if xg is not None else [(trp, None), (sto, None)]
                    p.emit("vector",
                           lambda e, gblk=gblk, co=co, dst=dst, ptr=ptr: e.tensor_copy(
                               dst[:, gblk * 4:gblk * 4 + 4, co:co + 128],
                               ptr[:, :].rearrange("p (j s) -> p j s", j=4)),
                           waits=cw, incs=[(xnv, 1)])
            if xg is None:
                for eb in range(E_BLKS):
                    p.dma("sync", xnT_d[eb, :, q * CHUNK:(q + 1) * CHUNK],
                          xnT_sb[:, eb * CHUNK:(eb + 1) * CHUNK],
                          waits=[(xnv, None)], incs=[(sto, 16)])

            for ot in range(O_TILES):
                b = ot & 1
                gq = q * O_TILES + ot
                wg_w = [(mmg, gu_mm_done.get(gq - 2, 0))]
                p.dma("gpsimd", wgl[b][:, :], wg_d[ot].rearrange("p c j -> p (c j)"),
                      waits=wg_w, incs=[(ggu[b], 16)])
                p.dma("gpsimd", wul[b][:, :], wu_d[ot].rearrange("p c j -> p (c j)"),
                      incs=[(ggu[b], 16)])

                def mm_gate(e, pg=pg, b=b, q=q):
                    qo = q * CHUNK if xg is not None else 0
                    for c in range(E_BLKS):
                        inst = e.matmul(pg[b][:, :],
                                        wgl[b][:, c * 128:(c + 1) * 128],
                                        xnT_sb[:, c * XS + qo: c * XS + qo + CHUNK],
                                        start=(c == 0), stop=(c == E_BLKS - 1))
                    return inst
                g_w = [(ggu[b], None), (xnv, None), (dev, None)]
                if gq >= 2:
                    g_w.append((ev, gu_ev_done[gq - 2]))
                else:
                    g_w.append((ev, None))
                p.emit("tensor", mm_gate, waits=g_w, incs=[(mmg, 1)])

                def mm_up(e, pu=pu, b=b, q=q):
                    qo = q * CHUNK if xg is not None else 0
                    for c in range(E_BLKS):
                        inst = e.matmul(pu[b][:, :], wul[b][:, c * 128:(c + 1) * 128],
                                        xnT_sb[:, c * XS + qo: c * XS + qo + CHUNK],
                                        start=(c == 0), stop=(c == E_BLKS - 1))
                    return inst
                p.emit("tensor", mm_up, incs=[(mmg, 1)])

                if layer == 0:
                    p.emit("scalar",
                           lambda e, ot=ot, pg=pg, b=b: e.activation(
                               silu_s[:, :], pg[b][:, :], AF.Silu,
                               scale=sg_sb[:, ot:ot + 1]),
                           waits=[(mmg, None), (ev, None), (wld, None)], incs=[(ev, 1)])
                    p.emit("vector",
                           lambda e, ot=ot, pu=pu, b=b: e.tensor_scalar(
                               up_s[:, :], pu[b][:, :],
                               su_sb[:, ot:ot + 1], None, ALU.mult),
                           waits=[(mmg, None), (wld, None)], incs=[(ev, 1)])
                    p.emit("vector",
                           lambda e, ot=ot: e.tensor_tensor(
                               hidT[:, ot * CHUNK:(ot + 1) * CHUNK],
                               silu_s[:, :], up_s[:, :], ALU.mult),
                           waits=[(ev, None)], incs=[(ev, 1)])
                else:
                    p.emit("scalar",
                           lambda e, pg=pg, b=b: e.activation(silu_s[:, :], pg[b][:, :],
                                                              AF.Silu),
                           waits=[(mmg, None), (ev, None)], incs=[(ev, 1)])
                    p.emit("vector",
                           lambda e, ot=ot, pu=pu, b=b: e.tensor_tensor(
                               hidT[:, ot * CHUNK:(ot + 1) * CHUNK],
                               silu_s[:, :], pu[b][:, :], ALU.mult),
                           waits=[(mmg, None), (ev, None)], incs=[(ev, 1)])
                gu_ev_done[gq] = ev.n
                gu_mm_done[gq] = mmg.n

        with ExitStack() as ps:
            pd = [ps.enter_context(nc.psum_tensor(f"{tag}pd{q}_{i}", [128, 512], F32))
                  for i in range(8)]
            for dh in range(2):
                for i, t in enumerate(tiles):
                    aw_ = [(dev, None)] if dh == 0 else [(sto, None)]
                    p.dma("sync", xacc[i][:, :], x_in[t][:, dh * 1024:(dh + 1) * 1024],
                          waits=aw_, incs=[(ald, 16)])
                for qt in range(O_TILES // 4):
                    b = qt & 1
                    gqt = (q * 2 + dh) * (O_TILES // 4) + qt
                    d_w = [(mmd, dq_mm_done.get(gqt - 2, 0))]
                    p.dma("gpsimd", wdp[b][:, :].rearrange("p (t d) -> p t d", t=4),
                          wd_d[4 * qt:4 * qt + 4, :, dh * 1024:(dh + 1) * 1024]
                          .rearrange("t p d -> p t d"),
                          waits=d_w, incs=[(gwd[b], 16)])
                    for j in range(4):
                        tt = 4 * qt + j

                        def mm_down(e, tt=tt, j=j, b=b, pd=pd):
                            for i in range(4):
                                for dc in range(2):
                                    inst = e.matmul(
                                        pd[i * 2 + dc][:, :],
                                        hidT[:, tt * CHUNK + i * 128: tt * CHUNK + (i + 1) * 128],
                                        wdp[b][:, j * 1024 + dc * 512: j * 1024 + (dc + 1) * 512],
                                        start=(tt == 0), stop=(tt == O_TILES - 1))
                            return inst
                        mw = [(gwd[b], None)]
                        if tt == 0:
                            mw += [(ev, None), (dev, None)]
                        p.emit("tensor", mm_down, waits=mw, incs=[(mmd, 1)])
                    dq_mm_done[gqt] = mmd.n
                for i in range(4):
                    for dc in range(2):
                        col = dh * 1024 + dc * 512
                        lcol = dc * 512
                        if layer == 0:
                            p.emit("vector",
                                   lambda e, i=i, dc=dc, col=col, pd=pd: e.tensor_tensor(
                                       up_s[:, :], pd[i * 2 + dc][:, :],
                                       sdb[:, col:col + 512], ALU.mult),
                                   waits=[(mmd, None), (ald, None), (dev, None)],
                                   incs=[(dev, 1)])
                            p.emit("vector",
                                   lambda e, i=i, lcol=lcol: e.tensor_tensor(
                                       xacc[i][:, lcol:lcol + 512], xacc[i][:, lcol:lcol + 512],
                                       up_s[:, :], ALU.add),
                                   waits=[(dev, None)], incs=[(dev, 1)])
                        else:
                            p.emit("vector",
                                   lambda e, i=i, dc=dc, lcol=lcol, pd=pd: e.tensor_tensor(
                                       xacc[i][:, lcol:lcol + 512], xacc[i][:, lcol:lcol + 512],
                                       pd[i * 2 + dc][:, :], ALU.add),
                                   waits=[(mmd, None), (ald, None)], incs=[(dev, 1)])
                for i, t in enumerate(tiles):
                    p.dma("sync", xacc_d[t][:, dh * 1024:(dh + 1) * 1024], xacc[i][:, :],
                          waits=[(dev, None)], incs=[(sto, 16)])

    return sto



def emit_w2pp(p, C, T, tag="w2_"):
    nc = p.nc
    ctx = p.alloc_ctx
    sb = lambda name, shape, dt: ctx.enter_context(nc.sbuf_tensor(tag + name, shape, dt))
    w2t = T["l1_w2t"].ap()
    w2a = T["l1_w2a"].ap()
    w2pp_flat = T["l1_w2pp"].ap().rearrange("t p d -> (t p) d")

    NB = 8
    h128 = sb("h128", [128, 128], BF16)
    h2x64 = sb("h2x64", [128, 128], BF16)
    win = [sb(f"win{b}", [128, 2048], BF16) for b in range(NB)]
    wout = [sb(f"wout{b}", [128, 2048], BF16) for b in range(NB)]
    ld = p.sem(tag + "ld")
    gld = [p.sem(tag + f"gld{b}") for b in range(NB)]
    mm = p.sem(tag + "mm")
    evs = p.sem(tag + "ev")
    sto = p.sem(tag + "sto")
    p.dma("sync", h128[:, :], T["h128"].ap()[:, :], incs=[(ld, 16)])
    p.dma("sync", h2x64[:, :], T["h2x64"].ap()[:, :], incs=[(ld, 16)])

    w2a_v = w2a.rearrange("(a b tt) d -> a b tt d", a=64, b=2)
    w2a_g = w2a.rearrange("(g q) d -> g q d", g=64)
    w2pp_v = w2pp_flat.rearrange("(tau gb) d -> tau gb d", gb=128)

    mm_done = {}
    sto_done = {}
    with nc.psum_tensor(tag + "psA", [128, 512], F32) as ps0, \
         nc.psum_tensor(tag + "psB", [128, 512], F32) as ps1, \
         nc.psum_tensor(tag + "psC", [128, 512], F32) as ps2, \
         nc.psum_tensor(tag + "psD", [128, 512], F32) as ps3, \
         nc.psum_tensor(tag + "psE", [128, 512], F32) as ps4, \
         nc.psum_tensor(tag + "psF", [128, 512], F32) as ps5:
        pss = [ps0, ps1, ps2, ps3, ps4, ps5]
        evm = {}
        gchunk = 0
        for t in range(128):
            b = t % NB
            is2 = t >= 64
            src = w2a_g[t - 64] if is2 else w2t[t]
            dw = [(mm, mm_done.get(t - NB, 0))]
            if is2:
                dw.append((sto, sto_done[63]))
            p.dma("gpsimd", win[b][:, :], src, waits=dw, incs=[(gld[b], 16)])
            hmat = h2x64 if is2 else h128
            for dchunk in range(4):
                ps = pss[gchunk % 6]

                def mmw2(e, dchunk=dchunk, b=b, hmat=hmat, ps=ps):
                    return e.matmul(ps[:, :], hmat[:, :],
                                    win[b][:, dchunk * 512:(dchunk + 1) * 512],
                                    start=True, stop=True)
                mw = [(ld, None), (gld[b], None)]
                if gchunk >= 6:
                    mw.append((evs, evm[gchunk - 6]))
                p.emit("tensor", mmw2, waits=mw, incs=[(mm, 1)])
                mmv_ = mm.n
                if is2:
                    p.emit("vector",
                           lambda e, dchunk=dchunk, b=b, ps=ps: e.tensor_scalar(
                               wout[b][:, dchunk * 512:(dchunk + 1) * 512], ps[:, :],
                               C.v(5), None, ALU.mult),
                           waits=[(mm, mmv_), (C.ready, None), (sto, sto_done.get(t - NB, 0))],
                           incs=[(evs, 1)])
                else:
                    p.emit("vector",
                           lambda e, dchunk=dchunk, b=b, ps=ps: e.tensor_copy(
                               wout[b][:, dchunk * 512:(dchunk + 1) * 512], ps[:, :]),
                           waits=[(mm, mmv_), (sto, sto_done.get(t - NB, 0))],
                           incs=[(evs, 1)])
                evm[gchunk] = evs.n
                gchunk += 1
            mm_done[t] = mm.n
            if is2:
                g = t - 64
                for beta in range(2):
                    p.dma("sync", w2pp_v[:, g * 2 + beta, :],
                          wout[b][beta * 64:(beta + 1) * 64, :],
                          waits=[(evs, evm[gchunk - 1])], incs=[(sto, 16)])
            else:
                p.dma("sync", w2a_v[:, :, t, :].rearrange("a b d -> (a b) d"), wout[b][:, :],
                      waits=[(evs, evm[gchunk - 1])], incs=[(sto, 16)])
            sto_done[t] = sto.n
    return sto


def emit_attn(p, C, layer, T, tag, xg=None):
    nc = p.nc
    ctx = p.alloc_ctx
    sb = lambda name, shape, dt: ctx.enter_context(nc.sbuf_tensor(tag + name, shape, dt))

    xnT_d = T["xnT"].ap()
    xbase_d = T["x_base"].ap()
    xout_d = T["x_out"].ap()
    aoT_d = T["aoT"].ap()
    NH = 16

    xnT_sb = xg if xg is not None else sb("xnT", [128, E_BLKS * 2048], BF16)
    wql = [sb(f"wql{b}", [128, 2048], BF16) for b in range(2)]
    nkv = 2 if layer == 0 else 1
    wkl = [sb(f"wkl{b}", [128, 2048], BF16) for b in range(nkv)]
    wvl = [sb(f"wvl{b}", [128, 2048], BF16) for b in range(nkv)]
    fT = sb("fT", [128, 2048], F32)
    fsq = sb("fsq", [128, 2048], BF16)
    rstd2 = sb("rstd2", [128, 2048], F32)
    xbf = sb("xbf", [128, 2048], F32)
    lnb = sb("lnb", [128, 512], F32)
    eT = sb("eT", [128, 16 * 512], BF16)
    zb = sb("zb", [128, 512], F32)
    ao_p = sb("ao_p", [128, 512], BF16)
    aop_sb = [wql[0][:, 1024:1536], wql[1][:, 1024:1536], wql[0][:, 1536:2048]]
    if layer == 0:
        qhatT = [sb(f"qhatT{b}", [128, 2048], BF16) for b in range(2)]
        khatT = [sb(f"khatT{b}", [128, 2048], BF16) for b in range(2)]
        v_nat = [sb(f"v{b}", [128, 2048], BF16) for b in range(2)]
        e_nat = sb("e_nat", [128, 2048], BF16)
        cols = sb("cols", [128, 34], F32)
        colsZ = sb("colsZ", [128, 32], F32)
        mrow = sb("mrow", [1, 2048], BF16)
        zrow = sb("zrow", [1, 2048], F32)
        biasb = sb("biasb", [128, 512], F32)
        smtmp = [sb(f"smtmp{b}", [128, 1024], F32) for b in range(2)]
        qs_c = sb("qs_c", [128, 1], F32)
        ks_c = sb("ks_c", [128, 1], F32)
        s2c = sb("s2c", [128, 1], F32)
    else:
        qhatT = [sb("qhatT0", [128, 2048], BF16), wkl[0]]
        khatT = [sb(f"khatT{g}", [128, 2048], BF16) for g in range(4)]
        vT_bf = [sb(f"vT{g}", [128, 2048], BF16) for g in range(4)]
        v_nat = sb("v", [128, 16 * 512], BF16)
        uv_bf = sb("uv_bf", [128, 512], BF16)
        uv_f = sb("uv_f", [128, 512], F32)
        df_f = sb("df_f", [128, 512], F32)
        zln = sb("zln", [128, 512], F32)

    ld = p.sem(tag + "ld")
    gld = [p.sem(tag + f"gld{b}") for b in range(2)]
    gvn = p.sem(tag + "gvn")
    gkv = p.sem(tag + "gkv")
    glw = p.sem(tag + "glw")
    aold2 = [p.sem(tag + f"aold{b}") for b in range(3)]
    xpl = p.sem(tag + "xpl")
    mm = p.sem(tag + "mm")
    se = p.sem(tag + "se")
    ve = p.sem(tag + "ve")
    sto = p.sem(tag + "sto")

    big = [ctx.enter_context(nc.psum_tensor(tag + f"big{i}", [128, 2048], F32))
           for i in range(2)]

    if xg is None:
        for eb in range(E_BLKS):
            p.dma("sync", xnT_sb[:, eb * 2048:(eb + 1) * 2048], xnT_d[eb],
                  incs=[(ld, 16)])
    if layer == 0:
        p.dma("sync", qs_c[:, :], T["l0_qscale"].ap()[:, :], incs=[(ld, 16)])
        p.dma("sync", ks_c[:, :], T["l0_kscale"].ap()[:, :], incs=[(ld, 16)])
        p.emit("vector", lambda e: e.tensor_tensor(s2c[:, :], qs_c[:, :], ks_c[:, :],
                                                   ALU.mult),
               waits=[(ld, None)], incs=[(ve, 1)])

    misc = {"rms_mm": 0}

    def proj_chunk(wsb, pq, sc, wsem, mm_w, fT_free):
        def mmf(e, sc=sc, pq=pq, wsb=wsb):
            for c in range(E_BLKS):
                inst = e.matmul(pq[:, :], wsb[:, c * 128:(c + 1) * 128],
                                xnT_sb[:, c * 2048 + sc * 512: c * 2048 + (sc + 1) * 512],
                                start=(c == 0), stop=(c == E_BLKS - 1))
            return inst
        p.emit("tensor", mmf, waits=[(ld, None), (wsem, None)] + list(mm_w),
               incs=[(mm, 1)])
        mmv_ = mm.n
        fw = [(mm, mmv_)]
        if fT_free:
            fw.append((ve, fT_free))
        p.emit("vector",
               lambda e, sc=sc, pq=pq: e.tensor_copy(fT[:, sc * 512:(sc + 1) * 512],
                                                     pq[:, :]),
               waits=fw, incs=[(ve, 1)])
        return ve.n

    def rms_chunks(pb, eps_col, fT_ve, pb_free, rstd_free_ve, l2=False):
        p.emit("vector", lambda e: e.tensor_tensor(fsq[:, :], fT[:, :], fT[:, :],
                                                   ALU.mult),
               waits=[(ve, fT_ve), (mm, misc["rms_mm"])], incs=[(ve, 1)])
        fsqv = ve.n
        pbr = {}
        ex = {}
        for sc in range(4):
            w = [(ve, fsqv), (C.ready, None)]
            if sc == 0:
                if pb_free:
                    w.append(pb_free)
            else:
                w.append(pbr[sc - 1])
            p.emit("tensor",
                   lambda e, sc=sc, pb=pb: e.matmul(pb, C.ones128_bf[:, :],
                                                    fsq[:, sc * 512:(sc + 1) * 512],
                                                    start=True, stop=True),
                   waits=w, incs=[(mm, 1)])
            misc["rms_mm"] = mm.n
            mmv_ = mm.n
            if l2:
                mw = [(mm, mmv_)]
                if sc:
                    mw.append((se, ex[sc - 1]))
                p.emit("vector",
                       lambda e, pb=pb: e.tensor_scalar(lnb[:, :], pb, C.v(7), None,
                                                        ALU.max),
                       waits=mw, incs=[(ve, 1)])
                pbr[sc] = (ve, ve.n)
                p.emit("scalar", lambda e: e.activation(lnb[:, :], lnb[:, :], AF.Ln),
                       waits=[(ve, ve.n)] + ([(se, ex[sc - 1])] if sc else []),
                       incs=[(se, 1)])
            else:
                mw = [(mm, mmv_)]
                if sc:
                    mw.append((se, ex[sc - 1]))
                p.emit("scalar",
                       lambda e, pb=pb: e.activation(lnb[:, :], pb, AF.Ln,
                                                     bias=eps_col, scale=1.0 / 128.0),
                       waits=mw, incs=[(se, 1)])
                pbr[sc] = (se, se.n)
            lnm = se.n
            rw = [(se, lnm)]
            if sc == 0 and rstd_free_ve:
                rw.append((ve, rstd_free_ve))
            p.emit("scalar",
                   lambda e, sc=sc: e.activation(rstd2[:, sc * 512:(sc + 1) * 512],
                                                 lnb[:, :], AF.Exp, scale=-0.5),
                   waits=rw, incs=[(se, 1)])
            ex[sc] = se.n
        return ex[3]

    kv_marks = {}
    if layer == 1:
        wvn = eT
        for c in range(E_BLKS):
            p.dma("gpsimd", wvn[:, c * 512:(c + 1) * 512], T["l1_wv_n"].ap()[c],
                  incs=[(gvn, 16)])
        pvs = [big[1][:, 0:512], big[1][:, 512:1024]]
        vcp = {}
        for t in range(S_TILES):
            pv = pvs[t & 1]

            def mmv(e, t=t, pv=pv):
                for c in range(E_BLKS):
                    inst = e.matmul(pv[:, :],
                                    xnT_sb[:, c * 2048 + t * 128: c * 2048 + (t + 1) * 128],
                                    wvn[:, c * 512:(c + 1) * 512],
                                    start=(c == 0), stop=(c == E_BLKS - 1))
                return inst
            w = [(ld, None), (gvn, None)]
            if t >= 2:
                w.append((ve, vcp[t - 2]))
            p.emit("tensor", mmv, waits=w, incs=[(mm, 1)])
            mmv_ = mm.n
            p.emit("vector",
                   lambda e, t=t, pv=pv: e.tensor_copy(v_nat[:, t * 512:(t + 1) * 512],
                                                       pv[:, :]),
                   waits=[(mm, mmv_)], incs=[(ve, 1)])
            vcp[t] = ve.n
        kvw_mm = 0
        prev_ve = vcp[S_TILES - 1]
        prev_pk_ve = 0
        pk0 = [big[0][:, 0:512], big[0][:, 512:1024]]
        pb0 = big[0][:, 1536:2048]
        for g in range(4):
            p.dma("gpsimd", wkl[0][:, :],
                  T["l1_wk"].ap()[g].rearrange("p c j -> p (c j)"),
                  waits=[(mm, kvw_mm)], incs=[(gkv, 16)])
            kse = {}
            for sc in range(4):
                mw = ([(ve, kse[sc - 2])] if sc >= 2
                      else ([(ve, prev_pk_ve)] if prev_pk_ve else []))
                kse[sc] = proj_chunk(wkl[0], pk0[sc & 1], sc, gkv, mw, prev_ve)
            krms = rms_chunks(pb0, C.v(4), kse[3], None, prev_ve)
            p.emit("vector",
                   lambda e, g=g: e.scalar_tensor_tensor(khatT[g][:, :], fT[:, :],
                                                         C.v(8), rstd2[:, :],
                                                         ALU.mult, ALU.mult),
                   waits=[(se, krms), (ve, kse[3])], incs=[(ve, 1)])
            khm = ve.n
            kvw_mm = mm.n
            p.dma("gpsimd", wvl[0][:, :],
                  T["l1_wv"].ap()[g].rearrange("p c j -> p (c j)"),
                  waits=[(mm, kvw_mm)], incs=[(gkv, 16)])
            vse = {}
            for sc in range(4):
                mw = [(ve, vse[sc - 2])] if sc >= 2 else [(ve, kse[3])]
                vse[sc] = proj_chunk(wvl[0], pk0[sc & 1], sc, gkv, mw, khm)
            vrms = rms_chunks(pb0, None, vse[3], None, khm, l2=True)
            p.emit("vector",
                   lambda e, g=g: e.tensor_tensor(vT_bf[g][:, :], fT[:, :],
                                                  rstd2[:, :], ALU.mult),
                   waits=[(se, vrms), (ve, vse[3])], incs=[(ve, 1)])
            prev_ve = ve.n
            prev_pk_ve = vse[3]
            kvw_mm = mm.n
        kv_marks["ve"] = prev_ve
        kv_marks["mm"] = kvw_mm

    qhat_ready = {}
    p_se_end = {}
    head_mm_done = {}
    c_end = {}
    rows_marks = {}
    a_marks = {}

    def P_steps(h):
        hb = h & 1
        arena = big[hb]
        pk = [arena[:, 0:512], arena[:, 512:1024]]
        pb = arena[:, 1536:2048]
        ce2 = c_end.get(h - 2, {"ve": 0, "se": 0, "mm": 0})
        fT_free = qhat_ready.get(h - 1, kv_marks.get("ve", 0))
        steps = []
        loc = {}
        if layer == 0:
            def s_dma():
                p.dma("gpsimd", wql[hb][:, :],
                      T["l0_wq"].ap()[h].rearrange("p c j -> p (c j)"),
                      waits=[(mm, head_mm_done.get(h - 2, 0))], incs=[(gld[hb], 16)])
                p.dma("gpsimd", wkl[hb][:, :],
                      T["l0_wk"].ap()[h].rearrange("p c j -> p (c j)"),
                      incs=[(gld[hb], 16)])
                p.dma("gpsimd", wvl[hb][:, :],
                      T["l0_wv"].ap()[h].rearrange("p c j -> p (c j)"),
                      incs=[(gld[hb], 16)])
            steps.append(s_dma)
            kse = {}

            def s_kproj(sc):
                mw = [(ve, kse[sc - 2])] if sc >= 2 else [(ve, ce2["ve"]), (se, ce2["se"])]
                kse[sc] = proj_chunk(wkl[hb], pk[sc & 1], sc, gld[hb], mw, fT_free)
            steps += [lambda sc=sc: s_kproj(sc) for sc in range(4)]

            def s_krms():
                loc["krms"] = rms_chunks(pb, C.v(3), kse[3], (ve, ce2["ve"]), fT_free)
            steps.append(s_krms)

            def s_khat():
                p.emit("vector",
                       lambda e: e.scalar_tensor_tensor(khatT[hb][:, :], fT[:, :],
                                                        s2c[:, :], rstd2[:, :],
                                                        ALU.mult, ALU.mult),
                       waits=[(se, loc["krms"]), (ve, kse[3]), (mm, ce2["mm"])],
                       incs=[(ve, 1)])
                loc["khat"] = ve.n
            steps.append(s_khat)

            def s_vt(t0):
                for t in (t0, t0 + 1, t0 + 2, t0 + 3):
                    pv = pk[t & 1][:, 0:128]

                    def mmv(e, t=t, pv=pv, hb=hb):
                        for c in range(E_BLKS):
                            inst = e.matmul(pv,
                                            xnT_sb[:, c * 2048 + t * 128: c * 2048 + (t + 1) * 128],
                                            wvl[hb][:, c * 128:(c + 1) * 128],
                                            start=(c == 0), stop=(c == E_BLKS - 1))
                        return inst
                    vw = [(gld[hb], None)]
                    if t < 2:
                        vw.append((ve, kse[3]))
                    else:
                        vw.append((ve, loc[("vcp", t - 2)]))
                    p.emit("tensor", mmv, waits=vw, incs=[(mm, 1)])
                    mmv_ = mm.n
                    p.emit("vector",
                           lambda e, t=t, pv=pv, hb=hb: e.tensor_copy(
                               v_nat[hb][:, t * 128:(t + 1) * 128], pv),
                           waits=[(mm, mmv_)], incs=[(ve, 1)])
                    loc[("vcp", t)] = ve.n
            steps += [lambda t0=t0: s_vt(t0) for t0 in range(0, 16, 4)]
            qse = {}

            def s_qproj(sc):
                mw = [(ve, qse[sc - 2])] if sc >= 2 else [(ve, loc[("vcp", 14 + sc)])]
                qse[sc] = proj_chunk(wql[hb], pk[sc & 1], sc, gld[hb], mw, loc["khat"])
            steps += [lambda sc=sc: s_qproj(sc) for sc in range(4)]

            def s_qrms():
                loc["qrms"] = rms_chunks(pb, C.v(3), qse[3], None, loc["khat"])
            steps.append(s_qrms)

            def s_qhat():
                p.emit("vector",
                       lambda e: e.tensor_tensor(qhatT[hb][:, :], fT[:, :], rstd2[:, :],
                                                 ALU.mult),
                       waits=[(se, loc["qrms"]), (ve, qse[3]), (mm, ce2["mm"])],
                       incs=[(ve, 1)])
                qhat_ready[h] = ve.n
                head_mm_done[h] = mm.n
                p_se_end[h] = se.n
            steps.append(s_qhat)
        else:
            def s_dma():
                p.dma("gpsimd", wql[hb][:, :],
                      T["l1_wq"].ap()[h].rearrange("p c j -> p (c j)"),
                      waits=[(mm, head_mm_done.get(h - 2, 0))],
                      incs=[(gld[hb], 16)])
            steps.append(s_dma)
            qse = {}

            def s_qproj(sc):
                mw = [(ve, qse[sc - 2])] if sc >= 2 else [(ve, ce2["ve"]), (se, ce2["se"])]
                qse[sc] = proj_chunk(wql[hb], pk[sc & 1], sc, gld[hb], mw, fT_free)
            steps += [lambda sc=sc: s_qproj(sc) for sc in range(4)]

            def s_qrms():
                loc["qrms"] = rms_chunks(pb, C.v(4), qse[3], (ve, ce2["ve"]), fT_free)
            steps.append(s_qrms)

            def s_qhat():
                p.emit("vector",
                       lambda e: e.tensor_tensor(qhatT[hb][:, :], fT[:, :], rstd2[:, :],
                                                 ALU.mult),
                       waits=[(se, loc["qrms"]), (ve, qse[3]), (mm, ce2["mm"])],
                       incs=[(ve, 1)])
                qhat_ready[h] = ve.n
                head_mm_done[h] = mm.n
                p_se_end[h] = se.n
            steps.append(s_qhat)
        return steps

    def A_steps(h):
        hb = h & 1
        psn = big[hb][:, :]
        stt = {}

        def s_tile(t):
            def mmsn(e, t=t):
                for kc in range(4):
                    inst = e.matmul(psn[:, kc * 512:(kc + 1) * 512],
                                    qhatT[hb][:, t * 128:(t + 1) * 128],
                                    khatT[hb][:, kc * 512:(kc + 1) * 512],
                                    start=True, stop=True)
                return inst
            if t == 0:
                aw = [(ve, qhat_ready[h]), (se, p_se_end[h])]
            else:
                aw = [(ve, stt[("mx", t - 1)]), (se, stt[("ex", t - 1)])]
            p.emit("tensor", mmsn, waits=aw, incs=[(mm, 1)])
            mmv_ = mm.n
            p.emit("vector",
                   lambda e, t=t: e.reduce_max(cols[:, t:t + 1], psn[:, :],
                                               axis=AX.X, negate=True),
                   waits=[(mm, mmv_)], incs=[(ve, 1)])
            stt[("mx", t)] = ve.n
            p.emit("scalar",
                   lambda e, t=t: e.activation(e_nat[:, :], psn[:, :], AF.Exp,
                                               bias=cols[:, t:t + 1],
                                               accum_out=cols[:, 16 + t:17 + t]),
                   waits=[(ve, stt[("mx", t)])], incs=[(se, 1)])
            stt[("ex", t)] = se.n

        def done():
            a_marks[h] = dict(mx15=stt[("mx", 15)], ex15=stt[("ex", 15)],
                              zs15=stt[("ex", 15)])
        return [lambda t=t: s_tile(t) for t in range(S_TILES)] + [done]

    def rows_step(h):
        hb = h & 1
        pmr = big[hb][0:1, 0:128]
        am = a_marks[h]

        def s_rows():
            trcp = 0
            for t in range(S_TILES):
                w = ([(ve, am["mx15"]), (se, am["ex15"]), (C.ld, None)]
                     if t == 0 else [(ve, trcp)])
                p.emit("tensor",
                       lambda e, t=t: e.transpose(pmr[:, :], cols[:, t:t + 1],
                                                  C.idn_f32[:, :]),
                       waits=w, incs=[(mm, 1)])
                mmv_ = mm.n
                p.emit("vector",
                       lambda e, t=t: e.tensor_copy(mrow[0:1, t * 128:(t + 1) * 128],
                                                    pmr[0:1, :]),
                       waits=[(mm, mmv_)], incs=[(ve, 1)])
                trcp = ve.n
            p.emit("vector", lambda e: e.reciprocal(cols[:, 16:32], cols[:, 16:32]),
                   waits=[(se, am["ex15"]), (ve, trcp)], incs=[(ve, 1)])
            trcp = ve.n
            for t in range(S_TILES):
                p.emit("tensor",
                       lambda e, t=t: e.transpose(pmr[:, :], cols[:, 16 + t:17 + t],
                                                  C.idn_f32[:, :]),
                       waits=[(ve, trcp)], incs=[(mm, 1)])
                mmv_ = mm.n
                p.emit("vector",
                       lambda e, t=t: e.tensor_copy(zrow[0:1, t * 128:(t + 1) * 128],
                                                    pmr[0:1, :]),
                       waits=[(mm, mmv_)], incs=[(ve, 1)])
                trcp = ve.n
            rows_marks[h] = trcp
        return [s_rows]

    def C_steps(h):
        hb = h & 1
        arena = big[hb]
        pT = [arena[:, 0:512], arena[:, 512:1024]]
        pav = arena[:, 1024:1536]
        paux = arena[:, 1536:2048]
        st = {}
        if layer == 0:
            am = a_marks[h]
            ce1 = c_end.get(h - 1, {})

            def s_qc(qc):
                bw = ([(ve, rows_marks[h]), (se, am["ex15"])] if qc == 0
                      else [(ve, st[("zbev", qc - 1)])])
                p.emit("tensor",
                       lambda e, qc=qc: e.matmul(paux, C.ones_row_bf[:, :],
                                                 mrow[0:1, qc * 512:(qc + 1) * 512],
                                                 start=True, stop=True),
                       waits=bw, incs=[(mm, 1)])
                mmb = mm.n
                bev = [(mm, mmb)]
                if qc:
                    bev.append((ve, st[("add", qc - 1, 15)]))
                elif ce1:
                    bev.append((ve, ce1["add15"]))
                p.emit("vector", lambda e: e.tensor_copy(biasb[:, :], paux),
                       waits=bev, incs=[(ve, 1)])
                st[("bias", qc)] = ve.n
                for kt in range(S_TILES):
                    b = kt & 1
                    pr = kt // 2
                    if kt >= 2:
                        sw = [(ve, st[("add", qc, kt - 2)])]
                    elif qc == 0:
                        sw = [(ve, am["mx15"]), (se, am["ex15"])]
                    else:
                        sw = [(ve, st[("add", qc - 1, 14 + kt)])]
                    p.emit("tensor",
                           lambda e, kt=kt, qc=qc, b=b: e.matmul(
                               pT[b], khatT[hb][:, kt * 128:(kt + 1) * 128],
                               qhatT[hb][:, qc * 512:(qc + 1) * 512],
                               start=True, stop=True),
                           waits=sw, incs=[(mm, 1)])
                    mms = mm.n
                    aw2 = [(mm, mms), (ve, st[("bias", qc)])]
                    expp = st.get(("exp", qc, pr - 2), (ce1.get("exp15", 0)
                                                       if qc == 0 else
                                                       st.get(("exp", qc - 1, 7 + pr), 0)))
                    if expp:
                        aw2.append((se, expp))
                    p.emit("vector",
                           lambda e, kt=kt, b=b, pr=pr: e.tensor_tensor(
                               smtmp[pr & 1][:, (kt & 1) * 512:(kt & 1) * 512 + 512],
                               pT[b], biasb[:, :], ALU.add),
                           waits=aw2, incs=[(ve, 1)])
                    st[("add", qc, kt)] = ve.n
                    if kt & 1:
                        p.emit("scalar",
                               lambda e, kt=kt, pr=pr: e.activation(
                                   eT[:, (kt - 1) * 512:(kt + 1) * 512],
                                   smtmp[pr & 1][:, :], AF.Exp),
                               waits=[(ve, st[("add", qc, kt)])], incs=[(se, 1)])
                        st[("exp", qc, pr)] = se.n

                def mmav(e, hb=hb):
                    for kt in range(S_TILES):
                        inst = e.matmul(pav, v_nat[hb][:, kt * 128:(kt + 1) * 128],
                                        eT[:, kt * 512:(kt + 1) * 512],
                                        start=(kt == 0), stop=(kt == S_TILES - 1))
                    return inst
                avw = [(se, st[("exp", qc, 7)])]
                aop = st.get(("ao", qc - 1), ce1.get("ao", 0))
                if aop:
                    avw.append((ve, aop))
                p.emit("tensor", mmav, waits=avw, incs=[(mm, 1)])
                st[("avmm", qc)] = mm.n
                p.emit("tensor",
                       lambda e, qc=qc: e.matmul(paux, C.bcast1_f32[:, :],
                                                 zrow[0:1, qc * 512:(qc + 1) * 512],
                                                 start=True, stop=True),
                       waits=[(ve, st[("bias", qc)])], incs=[(mm, 1)])
                mmz = mm.n
                zw = [(mm, mmz)]
                if qc:
                    zw.append((ve, st[("ao", qc - 1)]))
                elif ce1:
                    zw.append((ve, ce1["ao"]))
                p.emit("vector", lambda e: e.tensor_copy(zb[:, :], paux),
                       waits=zw, incs=[(ve, 1)])
                st[("zbev", qc)] = ve.n
                aow = [(mm, st[("avmm", qc)]), (ve, st[("zbev", qc)])]
                dprev = st.get(("dma", qc - 1), ce1.get("dma", 0))
                if dprev:
                    aow.append((sto, dprev))
                p.emit("vector",
                       lambda e: e.tensor_tensor(ao_p[:, :], pav, zb[:, :], ALU.mult),
                       waits=aow, incs=[(ve, 1)])
                st[("ao", qc)] = ve.n
                p.dma("sync", aoT_d[h, :, qc * 512:(qc + 1) * 512], ao_p[:, :],
                      waits=[(ve, st[("ao", qc)])], incs=[(sto, 16)])
                st[("dma", qc)] = sto.n

            def fin():
                c_end[h] = dict(ve=ve.n, se=se.n, mm=mm.n, av_mm=st[("avmm", 3)],
                                ao=st[("ao", 3)], dma=st[("dma", 3)],
                                add15=st[("add", 3, 15)], exp15=st[("exp", 3, 7)])
            return [lambda qc=qc: s_qc(qc) for qc in range(4)] + [fin]
        else:
            g = h // 4
            pzb = paux
            ce1 = c_end.get(h - 1, {})

            def s_qc(qc):
                for kt in range(S_TILES):
                    b = kt & 1
                    if kt >= 2:
                        sw = [(se, st[("exp", qc, kt - 2)])]
                    elif qc == 0:
                        sw = [(ve, qhat_ready[h]), (se, p_se_end[h])]
                        if ce1:
                            sw.append((se, ce1["exp15"]))
                    else:
                        sw = [(se, st[("exp", qc - 1, 14 + kt)])]
                    p.emit("tensor",
                           lambda e, kt=kt, qc=qc, b=b: e.matmul(
                               pT[b], khatT[g][:, kt * 128:(kt + 1) * 128],
                               qhatT[hb][:, qc * 512:(qc + 1) * 512],
                               start=True, stop=True),
                           waits=sw, incs=[(mm, 1)])
                    mms = mm.n
                    p.emit("scalar",
                           lambda e, kt=kt, b=b: e.activation(
                               eT[:, kt * 512:(kt + 1) * 512], pT[b], AF.Exp),
                           waits=[(mm, mms)], incs=[(se, 1)])
                    st[("exp", qc, kt)] = se.n

                def mmav(e, g=g):
                    for kt in range(S_TILES):
                        inst = e.matmul(pav,
                                        v_nat[:, kt * 512 + g * 128: kt * 512 + g * 128 + 128],
                                        eT[:, kt * 512:(kt + 1) * 512],
                                        start=(kt == 0), stop=(kt == S_TILES - 1))
                    return inst
                avw = [(se, st[("exp", qc, 15)])]
                dfp = st.get(("df", qc - 1), ce1.get("df", 0))
                if dfp:
                    avw.append((ve, dfp))
                p.emit("tensor", mmav, waits=avw, incs=[(mm, 1)])
                st[("avmm", qc)] = mm.n

                def mmzs(e):
                    for kt in range(S_TILES):
                        inst = e.matmul(pzb, C.ones128_bf[:, :],
                                        eT[:, kt * 512:(kt + 1) * 512],
                                        start=(kt == 0), stop=(kt == S_TILES - 1))
                    return inst
                zw = [(se, st[("exp", qc, 15)])]
                t2p = st.get(("tmp2", qc - 1), ce1.get("tmp2", 0))
                if t2p:
                    zw.append((ve, t2p))
                p.emit("tensor", mmzs, waits=zw, incs=[(mm, 1)])
                st[("zmm", qc)] = mm.n
                zlw = [(mm, st[("zmm", qc)])]
                p.emit("scalar", lambda e: e.activation(zln[:, :], pzb, AF.Ln),
                       waits=zlw, incs=[(se, 1)])
                st[("zln", qc)] = se.n
                zbw = [(se, st[("zln", qc)])]
                aop = st.get(("ao", qc - 1), ce1.get("ao", 0))
                if aop:
                    zbw.append((ve, aop))
                p.emit("scalar", lambda e: e.activation(zb[:, :], zln[:, :], AF.Exp,
                                                        scale=-1.0),
                       waits=zbw, incs=[(se, 1)])
                st[("zb", qc)] = se.n
                uvw = [(mm, st[("zmm", qc)])]
                p.emit("vector",
                       lambda e, qc=qc, g=g: e.tensor_tensor(
                           uv_bf[:, :], pav, vT_bf[g][:, qc * 512:(qc + 1) * 512],
                           ALU.mult),
                       waits=uvw, incs=[(ve, 1)])
                st[("uv", qc)] = ve.n
                p.emit("tensor",
                       lambda e: e.matmul(pzb, C.ones128_bf[:, :], uv_bf[:, :],
                                          start=True, stop=True),
                       waits=[(ve, st[("uv", qc)]), (se, st[("zln", qc)])],
                       incs=[(mm, 1)])
                mpd = mm.n
                p.emit("vector",
                       lambda e, qc=qc, g=g: e.tensor_tensor(
                           uv_f[:, :], pzb, vT_bf[g][:, qc * 512:(qc + 1) * 512],
                           ALU.mult),
                       waits=[(mm, mpd)], incs=[(ve, 1)])
                st[("tmp2", qc)] = ve.n
                p.emit("vector",
                       lambda e: e.tensor_tensor(df_f[:, :], pav, uv_f[:, :],
                                                 ALU.subtract),
                       waits=[(ve, st[("tmp2", qc)])], incs=[(ve, 1)])
                st[("df", qc)] = ve.n
                aow = [(ve, st[("df", qc)]), (se, st[("zb", qc)])]
                dprev = st.get(("dma", qc - 1), ce1.get("dma", 0))
                if dprev:
                    aow.append((sto, dprev))
                p.emit("vector",
                       lambda e: e.tensor_tensor(ao_p[:, :], df_f[:, :], zb[:, :],
                                                 ALU.mult),
                       waits=aow, incs=[(ve, 1)])
                st[("ao", qc)] = ve.n
                p.dma("sync", aoT_d[h, :, qc * 512:(qc + 1) * 512], ao_p[:, :],
                      waits=[(ve, st[("ao", qc)])], incs=[(sto, 16)])
                st[("dma", qc)] = sto.n

            def fin():
                c_end[h] = dict(ve=ve.n, se=se.n, mm=mm.n, av_mm=st[("zmm", 3)],
                                df=st[("df", 3)], tmp2=st[("tmp2", 3)],
                                ao=st[("ao", 3)], dma=st[("dma", 3)],
                                exp15=st[("exp", 3, 15)])
            return [lambda qc=qc: s_qc(qc) for qc in range(4)] + [fin]

    for f in P_steps(0):
        f()

    for h in range(NH):
        pp = P_steps(h + 1) if h + 1 < NH else []
        ac = A_steps(h) if layer == 0 else []
        na, np_ = len(ac), len(pp)
        ppi = 0
        for i, f in enumerate(ac):
            f()
            want = (i + 1) * np_ // max(na, 1) if na else 0
            while ppi < min(want, np_ // 2):
                pp[ppi]()
                ppi += 1
        if layer == 0:
            for f in rows_step(h):
                f()
        cc = C_steps(h)
        nc_, np2 = len(cc), len(pp) - ppi
        for i, f in enumerate(cc):
            f()
            want = ppi + (i + 1) * np2 // max(nc_, 1)
            while ppi < min(want, len(pp)):
                pp[ppi]()
                ppi += 1
        while ppi < len(pp):
            pp[ppi]()
            ppi += 1

    woall = xnT_sb[:, 0:16 * 1024]
    xb4 = [fT[:, 0:1024], fT[:, 1024:2048], xbf[:, 0:1024], xbf[:, 1024:2048]]
    xo2 = [rstd2[:, 0:1024], rstd2[:, 1024:2048]]
    wo_d = T["l0_wo"].ap() if layer == 0 else T["l1_wo"].ap()
    ce = c_end[NH - 1]
    pw = [big[i // 4][:, (i % 4) * 512:(i % 4 + 1) * 512] for i in range(8)]
    xnT_free = head_mm_done[NH - 1]

    glw_marks = {}
    woldw = {}
    gstop = {}
    evd = {}
    xplm = {}
    outd = {}
    gc = 0
    for i in range(4):
        p.dma("scalar", xb4[i], xbase_d[i, :, 0:1024],
              waits=([(ve, qhat_ready[NH - 1])] if i == 0 else []), incs=[(xpl, 16)])
    xplm[(0, 0)] = xpl.n
    groups = [(eh, tg) for eh in range(2) for tg in range(4)]
    for eh in range(2):
        for c in range(16):
            w = []
            if c == 0:
                w = [(mm, xnT_free)] if eh == 0 else [(mm, gstop[(0, 3)])]
            p.dma("gpsimd", woall[:, c * 1024:(c + 1) * 1024],
                  wo_d[c, :, eh * 1024:(eh + 1) * 1024], waits=w, incs=[(glw, 16)])
            glw_marks[(eh, c)] = glw.n
        for tg in range(4):
            for c in range(16):
                wb = gc % 3
                ww = [(mm, woldw[gc - 3])] if gc >= 3 else [(mm, xnT_free)]
                p.dma("sync", aop_sb[wb], aoT_d[c, :, tg * 512:(tg + 1) * 512],
                      waits=ww, incs=[(aold2[wb], 16)])

                def mmw(e, c=c, wb=wb):
                    for i in range(4):
                        for ec in range(2):
                            inst = e.matmul(pw[i * 2 + ec],
                                            aop_sb[wb][:, i * 128:(i + 1) * 128],
                                            woall[:, c * 1024 + ec * 512: c * 1024 + (ec + 1) * 512],
                                            start=(c == 0), stop=(c == 15))
                    return inst
                mw = [(aold2[wb], None), (glw, glw_marks[(eh, c)])]
                if c == 0:
                    gi = groups.index((eh, tg))
                    if gi == 0:
                        mw += [(ve, ce["ve"]), (se, ce["se"])]
                    else:
                        mw.append((ve, evd[groups[gi - 1]]))
                p.emit("tensor", mmw, waits=mw, incs=[(mm, 1)])
                woldw[gc] = mm.n
                gc += 1
            gstop[(eh, tg)] = mm.n
            gi = groups.index((eh, tg))
            vem = 0
            for i in range(4):
                t = tg * 4 + i
                for ec in range(2):
                    aw = [(mm, gstop[(eh, tg)]), (xpl, xplm[(eh, tg)])]
                    od = outd.get((eh, tg, i - 2), 0)
                    if od:
                        aw.append((sto, od))
                    p.emit("vector",
                           lambda e, i=i, ec=ec: e.tensor_tensor(
                               xo2[i & 1][:, ec * 512:(ec + 1) * 512],
                               xb4[i][:, ec * 512:(ec + 1) * 512],
                               pw[i * 2 + ec], ALU.add),
                           waits=aw, incs=[(ve, 1)])
                    vem = ve.n
                p.dma("sync", xout_d[t, :, eh * 1024:(eh + 1) * 1024], xo2[i & 1],
                      waits=[(ve, vem)], incs=[(sto, 16)])
                outd[(eh, tg, i)] = sto.n
            evd[(eh, tg)] = vem
            if gi + 1 < len(groups):
                neh, ntg = groups[gi + 1]
                for i in range(4):
                    p.dma("scalar", xb4[i], xbase_d[ntg * 4 + i, :, neh * 1024:(neh + 1) * 1024],
                          waits=([(ve, vem)] if i == 0 else []), incs=[(xpl, 16)])
                xplm[(neh, ntg)] = xpl.n
    return sto






def _mk_dram(nc, specs):
    T = {}
    for name, shape, dt, kind in specs:
        T[name] = nc.dram_tensor(name, list(shape), dt, kind=kind)
    return T


_FFN_L0_W = [
    ("l0_wg", (64, 128, 16, 128), I8, "ExternalInput"),
    ("l0_wu", (64, 128, 16, 128), I8, "ExternalInput"),
    ("l0_wd", (64, 128, 2048), I8, "ExternalInput"),
    ("l0_sg", (128, 64), F32, "ExternalInput"),
    ("l0_su", (128, 64), F32, "ExternalInput"),
    ("l0_sdown", (1, 2048), F32, "ExternalInput"),
]
_FFN_L1_W = [
    ("l1_w1", (64, 128, 16, 128), F32, "ExternalInput"),
    ("l1_w3", (64, 128, 16, 128), F32, "ExternalInput"),
    ("l1_w2t", (64, 128, 2048), F32, "ExternalInput"),
    ("h128", (128, 128), BF16, "ExternalInput"),
    ("h2x64", (128, 128), BF16, "ExternalInput"),
    ("l1_w2a", (8192, 2048), BF16, "Internal"),
    ("l1_w2pp", (64, 128, 2048), BF16, "Internal"),
]


_ATTN_L0_W = [
    ("l0_wq", (16, 128, 16, 128), F32, "ExternalInput"),
    ("l0_wk", (16, 128, 16, 128), F32, "ExternalInput"),
    ("l0_wv", (16, 128, 16, 128), F32, "ExternalInput"),
    ("l0_wo", (16, 128, 2048), F32, "ExternalInput"),
    ("l0_qscale", (128, 1), F32, "ExternalInput"),
    ("l0_kscale", (128, 1), F32, "ExternalInput"),
]
_ATTN_L1_W = [
    ("l1_wq", (16, 128, 16, 128), F32, "ExternalInput"),
    ("l1_wk", (4, 128, 16, 128), F32, "ExternalInput"),
    ("l1_wv", (4, 128, 16, 128), F32, "ExternalInput"),
    ("l1_wv_n", (16, 128, 512), F32, "ExternalInput"),
    ("l1_wo", (16, 128, 2048), F32, "ExternalInput"),
]


def build_attn_nc(layer, dbg=False):
    nc = bass.Bass("TRN2", target_bir_lowering=False)
    specs = [
        ("xnT", (16, 128, 2048), BF16, "ExternalInput"),
        ("x_base", (16, 128, 2048), F32, "ExternalInput"),
        ("idn_bf", (128, 128), BF16, "ExternalInput"),
        ("idn_f32", (128, 128), F32, "ExternalInput"),
        ("x_out", (16, 128, 2048), F32, "ExternalOutput"),
        ("aoT", (16, 128, 2048), BF16, "ExternalOutput" if dbg else "Internal"),
    ]
    if dbg:
        specs += [
            ("d_qhatT", (128, 2048), BF16, "ExternalOutput"),
            ("d_fT", (128, 2048), F32, "ExternalOutput"),
            ("d_wql", (128, 2048), BF16, "ExternalOutput"),
            ("d_xnT0", (128, 2048), BF16, "ExternalOutput"),
            ("d_rstd2", (128, 2048), F32, "ExternalOutput"),
            ("d_khatT", (128, 2048), BF16, "ExternalOutput"),
            ("d_v", (128, 2048), BF16, "ExternalOutput"),
            ("d_cols", (128, 34), F32, "ExternalOutput"),
            ("d_mrow", (1, 2048), BF16, "ExternalOutput"),
            ("d_zrow", (1, 2048), F32, "ExternalOutput"),
            ("d_eT", (128, 8192), BF16, "ExternalOutput"),
        ]
    specs += _ATTN_L0_W if layer == 0 else _ATTN_L1_W
    with ExitStack() as ctx:
        T = _mk_dram(nc, specs)
        p = Prog(nc, ctx)
        C = Consts(nc, ctx, p, T)
        sto = emit_attn(p, C, layer, T, tag=f"a{layer}_", dbg=dbg)
        p.emit("sync", lambda e: None, waits=[(sto, None)])
        p.finalize()
    return nc


def build_ffn_nc(layer):
    nc = bass.Bass("TRN2", target_bir_lowering=False)
    specs = [
        ("x_in", (16, 128, 2048), F32, "ExternalInput"),
        ("idn_bf", (128, 128), BF16, "ExternalInput"),
        ("idn_f32", (128, 128), F32, "ExternalInput"),
        ("xnT", (16, 128, 2048), BF16, "ExternalOutput"),
        ("x_acc", (16, 128, 2048), F32, "ExternalOutput"),
    ]
    specs += _FFN_L0_W if layer == 0 else _FFN_L1_W
    with ExitStack() as ctx:
        T = _mk_dram(nc, specs)
        p = Prog(nc, ctx)
        C = Consts(nc, ctx, p, T)
        if layer == 1:
            w2done = emit_w2pp(p, C, T)
            for eng in ENGINES:
                p.emit(eng, lambda e: None, waits=[(w2done, None)])
        sto = emit_ffn(p, C, layer, T, tag=f"f{layer}_")
        p.emit("sync", lambda e: None, waits=[(sto, None)])
        p.finalize()
    return nc


def build_fused_nc():
    nc = bass.Bass("TRN2", target_bir_lowering=False)
    specs = [
        ("x_in", (16, 128, 2048), F32, "ExternalInput"),
        ("idn_bf", (128, 128), BF16, "ExternalInput"),
        ("idn_f32", (128, 128), F32, "ExternalInput"),
        ("h128", (128, 128), BF16, "ExternalInput"),
        ("h2x64", (128, 128), BF16, "ExternalInput"),
        ("out", (16, 128, 2048), F32, "ExternalOutput"),
        ("xnT_s", (16, 128, 2048), BF16, "Internal"),
        ("xacc_s", (16, 128, 2048), F32, "Internal"),
        ("x1_s", (16, 128, 2048), F32, "Internal"),
        ("aoT_s", (16, 128, 2048), BF16, "Internal"),
        ("l1_w2a", (8192, 2048), BF16, "Internal"),
        ("l1_w2pp", (64, 128, 2048), BF16, "Internal"),
    ]
    specs += _FFN_L0_W + _ATTN_L0_W
    specs += [t for t in _FFN_L1_W if t[0] in ("l1_w1", "l1_w3", "l1_w2t")]
    specs += _ATTN_L1_W
    with ExitStack() as ctx:
        T = _mk_dram(nc, specs)
        p = Prog(nc, ctx)
        C = Consts(nc, ctx, p, T)
        xg = ctx.enter_context(nc.sbuf_tensor("xnT_glob", [128, E_BLKS * 2048], BF16))

        def phase(emitter):
            with ExitStack() as pc:
                p.alloc_ctx = pc
                emitter()
            p.alloc_ctx = ctx
            p.barrier()

        Tf0 = dict(T, x_in=T["x_in"], xnT=T["xnT_s"], x_acc=T["xacc_s"])
        Ta0 = dict(T, xnT=T["xnT_s"], x_base=T["xacc_s"], x_out=T["x1_s"], aoT=T["aoT_s"])
        Tf1 = dict(T, x_in=T["x1_s"], xnT=T["xnT_s"], x_acc=T["xacc_s"])
        Ta1 = dict(T, xnT=T["xnT_s"], x_base=T["xacc_s"], x_out=T["out"], aoT=T["aoT_s"])

        phase(lambda: emit_w2pp(p, C, T))
        phase(lambda: emit_ffn(p, C, 0, Tf0, tag="f0_", xg=xg))
        phase(lambda: emit_attn(p, C, 0, Ta0, tag="a0_", xg=xg))
        phase(lambda: emit_ffn(p, C, 1, Tf1, tag="f1_", xg=xg))
        phase(lambda: emit_attn(p, C, 1, Ta1, tag="a1_", xg=xg))
        p.finalize()
    return nc


_BUILD_CACHE = {}


def _get_nc(key, builder):
    if key not in _BUILD_CACHE:
        _BUILD_CACHE[key] = builder()
    return _BUILD_CACHE[key]


def kernel_fused(**inputs):
    w = prep_weights(inputs)
    c = host_consts()
    x = inputs["L0_attn_x"]
    nc = _get_nc("fused", build_fused_nc)
    base = {"idn_bf": c["idn_bf"], "idn_f32": c["idn_f32"],
            "h128": c["h128"], "h2x64": c["h2x64"]}
    for k in ("l0_wg", "l0_wu", "l0_wd", "l0_sg", "l0_su", "l0_sdown",
              "l0_wq", "l0_wk", "l0_wv", "l0_wo", "l0_qscale", "l0_kscale",
              "l1_w1", "l1_w3", "l1_w2t",
              "l1_wq", "l1_wk", "l1_wv", "l1_wv_n", "l1_wo"):
        base[k] = w[k]
    maps = [dict(base, x_in=np.ascontiguousarray(x[i].reshape(16, 128, 2048)))
            for i in range(NCORES)]
    res = run_bass_kernel_spmd(nc, maps, core_ids=list(range(NCORES)))
    out = np.empty((NCORES, 2048, 2048), dtype=np.float32)
    for i in range(NCORES):
        out[i] = res.results[i]["out"].reshape(2048, 2048)
    return out


def kernel(**inputs):
    return kernel_fused(**inputs)


def kernel_unfused(**inputs):
    w = prep_weights(inputs)
    c = host_consts()
    x = inputs["L0_attn_x"]
    consts = {"idn_bf": c["idn_bf"], "idn_f32": c["idn_f32"]}

    def shard(arr):
        return [np.ascontiguousarray(arr[cid].reshape(16, 128, 2048))
                for cid in range(NCORES)]

    core_ids = list(range(NCORES))

    nc = _get_nc("ffn0", lambda: build_ffn_nc(0))
    base = dict(consts)
    for k in ("l0_wg", "l0_wu", "l0_wd", "l0_sg", "l0_su", "l0_sdown"):
        base[k] = w[k]
    xs = shard(x)
    maps = [dict(base, x_in=xs[i]) for i in range(NCORES)]
    res = run_bass_kernel_spmd(nc, maps, core_ids=core_ids)
    xnT = [r["xnT"] for r in res.results]
    xacc = [r["x_acc"] for r in res.results]

    nc = _get_nc("attn0", lambda: build_attn_nc(0))
    base = dict(consts)
    for k in ("l0_wq", "l0_wk", "l0_wv", "l0_wo", "l0_qscale", "l0_kscale"):
        base[k] = w[k]
    maps = [dict(base, xnT=xnT[i], x_base=xacc[i]) for i in range(NCORES)]
    res = run_bass_kernel_spmd(nc, maps, core_ids=core_ids)
    x1 = [r["x_out"] for r in res.results]

    nc = _get_nc("ffn1", lambda: build_ffn_nc(1))
    base = dict(consts)
    for k in ("l1_w1", "l1_w3", "l1_w2t"):
        base[k] = w[k]
    base["h128"] = c["h128"]
    base["h2x64"] = c["h2x64"]
    maps = [dict(base, x_in=x1[i]) for i in range(NCORES)]
    res = run_bass_kernel_spmd(nc, maps, core_ids=core_ids)
    xnT = [r["xnT"] for r in res.results]
    xacc = [r["x_acc"] for r in res.results]

    nc = _get_nc("attn1", lambda: build_attn_nc(1))
    base = dict(consts)
    for k in ("l1_wq", "l1_wk", "l1_wv", "l1_wv_n", "l1_wo"):
        base[k] = w[k]
    maps = [dict(base, xnT=xnT[i], x_base=xacc[i]) for i in range(NCORES)]
    res = run_bass_kernel_spmd(nc, maps, core_ids=core_ids)

    out = np.empty((NCORES, 2048, 2048), dtype=np.float32)
    for i in range(NCORES):
        out[i] = res.results[i]["x_out"].reshape(2048, 2048)
    return out
